# version 19
# baseline (speedup 1.0000x reference)
import numpy as np
import concourse.bass as bass
import concourse.mybir as mybir
from concourse.bass_utils import run_bass_kernel_spmd

F32 = mybir.dt.float32
BF16 = mybir.dt.bfloat16
AF = mybir.ActivationFunctionType
ALU = mybir.AluOpType
AX = mybir.AxisListType

D = 1024
T = 2048
NG = 8
NH = 16
HD = 64
NCORES = 8

PROF_MARKS = []


def _mark(p, label):
    PROF_MARKS.append((label, p.nc.n_instructions()))


GOLDEN_CENTER = 0.36787944117144233
GOLDEN_LOWER = 0.21231792754821535
GOLDEN_UPPER = 0.5


def host_consts():
    import ml_dtypes
    bf = ml_dtypes.bfloat16
    c = {}
    c["IDENT32"] = np.eye(128, dtype=np.float32)
    c["IDENT16"] = np.eye(128, dtype=bf)
    c["ONESCOL16"] = np.ones((128, 1), dtype=bf)
    c["ONESROW16"] = np.ones((1, 128), dtype=bf)
    c["ONESCOL32"] = np.ones((128, 1), dtype=np.float32)
    c["ONESROW32"] = np.ones((1, 128), dtype=np.float32)
    c["ONESMAT16"] = np.ones((128, 128), dtype=bf)
    j = np.arange(128)[:, None]
    i = np.arange(512)[None, :]
    for d_ in (0, 128, 256, 384):
        c[f"CMASK{d_}"] = ((j + d_) <= i).astype(bf)
    invfreq = (np.float32(1.0) / (np.float32(10000.0) **
               (np.arange(0, HD, 2, dtype=np.float32) / np.float32(HD))))
    pos = np.arange(T, dtype=np.float32)
    fr = (pos[None, :] * invfreq[:, None]).astype(np.float32)
    cos = np.cos(fr.astype(np.float64)).astype(np.float32)
    sin = np.sin(fr.astype(np.float64)).astype(np.float32)
    C = np.concatenate([cos, cos, cos, cos], axis=0)
    S = np.concatenate([-sin, sin, -sin, sin], axis=0)
    c["ROPE_C"] = C
    c["ROPE_S"] = S
    perm = np.arange(128)
    perm = perm + np.where((perm % 64) < 32, 32, -32)
    SW = np.zeros((128, 128), dtype=bf)
    SW[perm, np.arange(128)] = 1.0
    c["SWAPMAT"] = SW
    return c


def shard_inputs(inputs, nb=2):
    import ml_dtypes
    consts = host_consts()
    x = np.ascontiguousarray(inputs["L0_attn_x"])
    ncores = x.shape[0] // nb
    weights = {}
    for k, v in inputs.items():
        if k == "L0_attn_x":
            continue
        weights[k] = np.ascontiguousarray(v)
    weights["L1_ffn_w1T"] = np.ascontiguousarray(inputs["L1_ffn_w1"].T)
    weights["L1_ffn_w2T"] = np.ascontiguousarray(inputs["L1_ffn_w2"].T)
    weights["bo_fm"] = np.ascontiguousarray(inputs["L0_attn_bo"].reshape(NG, 128).T)
    cw = inputs["L1_ffn_conv_w"]
    for k_ in range(3):
        weights[f"convw{k_}_fm"] = np.ascontiguousarray(cw[:, 0, k_].reshape(32, 128).T)
    weights["convb_fm"] = np.ascontiguousarray(inputs["L1_ffn_conv_b"].reshape(32, 128).T)
    cd = np.zeros((96, 128, 128), dtype=np.float32)
    idx = np.arange(128)
    for hg_ in range(32):
        for k_ in range(3):
            cd[hg_ * 3 + k_, idx, idx] = cw[hg_ * 128:(hg_ + 1) * 128, 0, k_]
    weights["convdiag"] = cd
    weights.update(consts)
    maps = []
    for ci in range(ncores):
        m = dict(weights)
        m["x"] = np.ascontiguousarray(x[ci * nb:(ci + 1) * nb])
        maps.append(m)
    return maps


_UID = [0]

def uniq(nm):
    _UID[0] += 1
    return f"{nm}_{_UID[0]}"


_SEM_ALIAS = {}
for _t in ("a", "f", "L", "M"):
    for _r in ("gxb", "lnmm", "sq", "stv", "sts", "stat", "bcm", "bce", "xn",
               "vch", "sch", "cld"):
        _SEM_ALIAS[_t + _r] = "c" + _r
for _t in ("ti", "to"):
    for _r in ("ld", "tr", "ev", "st"):
        _SEM_ALIAS[_t + _r] = "t" + _r
for _n, _c in [
    ("apmm", "pmm"), ("Lpmm", "pmm"), ("Mw1", "pmm"), ("fgm", "pmm"),
    ("apev", "pev"), ("Lpev", "pev"), ("Mw1e", "pev"),
    ("fum", "pm2"), ("Mw2", "pm2"),
    ("aaex", "ex"), ("Laex", "ex"),
    ("aapv", "pv"), ("Lapv", "pv"),
    ("aamm", "smm"), ("Lamm", "smm"),
    ("aabc", "abc"), ("Labc", "abc"),
    ("aarc", "rc"), ("Lrc", "rc"),
    ("aacn", "nrm"), ("Lnrm", "nrm"),
    ("aoev", "oev"), ("Loev", "oev"), ("fov", "oev"), ("Mov", "oev"),
    ("aomm", "omm"), ("Lomm", "omm"),
    ("aost", "ost"), ("Lost", "ost"), ("fst", "ost"), ("Mst", "ost"),
    ("aoxc", "oxc"), ("Loxc", "oxc"), ("fxc", "oxc"), ("Mxc2", "oxc"),
    ("awld2", "gw"), ("agwo", "gw"), ("Lgw", "gw"), ("Lgw2", "gw"),
    ("gwg", "gw1"), ("gwu", "gw2"), ("gwd", "gw3"), ("gwr", "gw"),
    ("awld", "scl"), ("amld", "scl"), ("Lsld", "scl"), ("Mcld", "scl"),
    ("Mwl", "swl"), ("Mwl2", "swl"),
    ("fsil", "sil"), ("Msil", "sil"),
    ("fD", "Dg"), ("MD", "Dg"), ("MD2", "Dg"),
    ("fbcw", "bcw"), ("Mbcm2", "bcw"), ("Mbcm3", "bcw"),
    ("fw16", "bcev"), ("Mbce2", "bcev"), ("Mbce3", "bcev"),
    ("Mabs2", "sabs"), ("Mabs3", "sabs"), ("fabs", "sabs"),
    ("Msg2", "ssg"), ("Msg3", "ssg"), ("Msg", "ssg"),
    ("Mgm", "gm"), ("Mgm2", "gm"),
    ("Mtr", "qtr"), ("Mtr2", "qtr"),
    ("Mmt", "qmt"), ("Mmt2", "qmt"),
    ("Msq2", "qsq"), ("Msq3", "qsq"),
    ("My", "qy"), ("My2", "qy"),
    ("Mq2", "qq"), ("Mxq", "qq"),
]:
    _SEM_ALIAS[_n] = _c


class P:
    def __init__(self, nc, stack):
        self.nc = nc
        self.stack = stack
        self.sems = {}

    def s(self, name):
        name = _SEM_ALIAS.get(name, name)
        if name not in self.sems:
            h = self.stack.enter_context(self.nc.semaphore(name))
            self.sems[name] = [h, 0]
        return self.sems[name]

    def inc(self, name, instr, n=1):
        rec = self.s(name)
        rec[1] += n
        instr.then_inc(rec[0], n)
        return rec[1]

    def val(self, name):
        return self.s(name)[1]

    def wait(self, eng, name, val=None):
        rec = self.s(name)
        eng.wait_ge(rec[0], rec[1] if val is None else val)

    def sb(self, name, shape, dtype):
        return self.stack.enter_context(self.nc.sbuf_tensor(name, shape, dtype))

    def ps(self, name, shape, dtype=F32):
        return self.stack.enter_context(self.nc.psum_tensor(name, shape, dtype))


def declare_io(nc, nb, fm_external=False):
    io = {}
    io["consts"] = {}
    for nm, shp, dt in [
        ("IDENT32", [128, 128], F32), ("IDENT16", [128, 128], BF16),
        ("ONESCOL16", [128, 1], BF16), ("ONESROW16", [1, 128], BF16),
        ("ONESCOL32", [128, 1], F32), ("ONESROW32", [1, 128], F32),
        ("ONESMAT16", [128, 128], BF16),
        ("CMASK0", [128, 512], BF16), ("CMASK128", [128, 512], BF16),
        ("CMASK256", [128, 512], BF16), ("CMASK384", [128, 512], BF16),
        ("ROPE_C", [128, T], F32), ("ROPE_S", [128, T], F32),
        ("SWAPMAT", [128, 128], BF16),
    ]:
        io["consts"][nm] = nc.dram_tensor(nm, shp, F32 if dt == F32 else dt, kind="ExternalInput").ap()
    for nm, shp in [
        ("L0_attn_Wq", [D, D]), ("L0_attn_Wk", [D, D]), ("L0_attn_Wv", [D, D]),
        ("L0_attn_Wo", [D, D]), ("L0_attn_bo", [D]), ("bo_fm", [128, 8]),
        ("L0_ffn_Wg", [8, D, D]), ("L0_ffn_Wu", [8, D, D]), ("L0_ffn_Wd", [8, D, D]),
        ("L0_ffn_Wr", [D, 8]), ("L0_ffn_temperature", [1]),
        ("L1_attn_Wq", [D, D]), ("L1_attn_Wk", [D, D]), ("L1_attn_Wv", [D, D]),
        ("L1_attn_Wout", [D, D]),
        ("L1_ffn_w1", [4096, D]), ("L1_ffn_w1T", [D, 4096]),
        ("L1_ffn_w2", [D, 4096]), ("L1_ffn_w2T", [4096, D]),
        ("L1_ffn_conv_w", [4096, 1, 3]), ("L1_ffn_conv_b", [4096]),
        ("convw0_fm", [128, 32]), ("convw1_fm", [128, 32]), ("convw2_fm", [128, 32]),
        ("convb_fm", [128, 32]), ("convdiag", [96, 128, 128]),
    ]:
        io[nm] = nc.dram_tensor(nm, shp, F32, kind="ExternalInput").ap()
    return io


def emit_tr_td2fm(p, src_td, dst_fm, nb, tag):
    nc = p.nc
    with (
        nc.sbuf_tensor(uniq(f"{tag}_in"), [128, 16, D], F32) as tin,
        nc.sbuf_tensor(uniq(f"{tag}_out"), [128, NG, T], F32) as tout,
        nc.sbuf_tensor(uniq(f"{tag}_id"), [128, 128], F32) as ident,
        nc.psum_tensor(uniq(f"{tag}_p0"), [128, 128], F32) as p0,
        nc.psum_tensor(uniq(f"{tag}_p1"), [128, 128], F32) as p1,
        nc.Block() as block,
    ):
        banks = [p0, p1]
        ld, tr, ev, st = f"{tag}ld", f"{tag}tr", f"{tag}ev", f"{tag}st"
        ld0, tr0, ev0, st0 = p.val(ld), p.val(tr), p.val(ev), p.val(st)

        @block.sync
        def _(s):
            p.inc(ld, s.dma_start(out=ident[:, :], in_=p.io["consts"]["IDENT32"]), 16)
            for b in range(nb):
                if b > 0:
                    s.wait_ge(p.s(tr)[0], tr0 + 128 * b)
                p.inc(ld, s.dma_start(
                    out=tin[:, :],
                    in_=src_td[b].rearrange("(tt p) d -> p tt d", p=128)), 16)

        @block.tensor
        def _(t):
            k = 0
            for b in range(nb):
                t.wait_ge(p.s(ld)[0], ld0 + 16 * (b + 2))
                for tt in range(16):
                    for dg in range(NG):
                        bank = banks[k % 2]
                        if k >= 2:
                            t.wait_ge(p.s(ev)[0], ev0 + k - 1)
                        ins = t.transpose(bank[:, :], tin[:, tt, dg * 128:(dg + 1) * 128],
                                          ident[:, :])
                        p.inc(tr, ins, 1)
                        k += 1

        @block.vector
        def _(v):
            k = 0
            for b in range(nb):
                if b > 0:
                    v.wait_ge(p.s(st)[0], st0 + 16 * b)
                for tt in range(16):
                    for dg in range(NG):
                        bank = banks[k % 2]
                        v.wait_ge(p.s(tr)[0], tr0 + k + 1)
                        ins = v.tensor_copy(tout[:, dg, tt * 128:(tt + 1) * 128], bank[:, :])
                        p.inc(ev, ins, 1)
                        k += 1

        @block.scalar
        def _(s):
            for b in range(nb):
                s.wait_ge(p.s(ev)[0], ev0 + 128 * (b + 1))
                p.inc(st, s.dma_start(
                    out=dst_fm[b].rearrange("(dg p) t -> p dg t", p=128),
                    in_=tout[:, :]), 16)
            s.wait_ge(p.s(st)[0], p.val(st))


def emit_tr_fm2td(p, src_fm, dst_td, nb, tag):
    nc = p.nc
    with (
        nc.sbuf_tensor(uniq(f"{tag}_in"), [128, NG, T], F32) as tin,
        nc.sbuf_tensor(uniq(f"{tag}_out"), [128, 16, D], F32) as tout,
        nc.sbuf_tensor(uniq(f"{tag}_id"), [128, 128], F32) as ident,
        nc.psum_tensor(uniq(f"{tag}_p0"), [128, 128], F32) as p0,
        nc.psum_tensor(uniq(f"{tag}_p1"), [128, 128], F32) as p1,
        nc.Block() as block,
    ):
        banks = [p0, p1]
        ld, tr, ev, st = f"{tag}ld", f"{tag}tr", f"{tag}ev", f"{tag}st"
        ld0, tr0, ev0, st0 = p.val(ld), p.val(tr), p.val(ev), p.val(st)

        @block.sync
        def _(s):
            p.inc(ld, s.dma_start(out=ident[:, :], in_=p.io["consts"]["IDENT32"]), 16)
            for b in range(nb):
                if b > 0:
                    s.wait_ge(p.s(tr)[0], tr0 + 128 * b)
                p.inc(ld, s.dma_start(
                    out=tin[:, :],
                    in_=src_fm[b].rearrange("(dg p) t -> p dg t", p=128)), 16)

        @block.tensor
        def _(t):
            k = 0
            for b in range(nb):
                t.wait_ge(p.s(ld)[0], ld0 + 16 * (b + 2))
                for dg in range(NG):
                    for tt in range(16):
                        bank = banks[k % 2]
                        if k >= 2:
                            t.wait_ge(p.s(ev)[0], ev0 + k - 1)
                        ins = t.transpose(bank[:, :], tin[:, dg, tt * 128:(tt + 1) * 128],
                                          ident[:, :])
                        p.inc(tr, ins, 1)
                        k += 1

        @block.vector
        def _(v):
            k = 0
            for b in range(nb):
                if b > 0:
                    v.wait_ge(p.s(st)[0], st0 + 16 * b)
                for dg in range(NG):
                    for tt in range(16):
                        bank = banks[k % 2]
                        v.wait_ge(p.s(tr)[0], tr0 + k + 1)
                        ins = v.tensor_copy(tout[:, tt, dg * 128:(dg + 1) * 128], bank[:, :])
                        p.inc(ev, ins, 1)
                        k += 1

        @block.scalar
        def _(s):
            for b in range(nb):
                s.wait_ge(p.s(ev)[0], ev0 + 128 * (b + 1))
                p.inc(st, s.dma_start(
                    out=dst_td[b].rearrange("(tt p) d -> p tt d", p=128),
                    in_=tout[:, :]), 16)
            s.wait_ge(p.s(st)[0], p.val(st))


def emit_ln_chunk(p, block, tag, xb, xnc, sc, pb, xn_guard=None):
    nc = p.nc

    @block.tensor
    def _(t):
        p.wait(t, tag + "gxb")
        p.wait(t, tag + "cld")
        p.wait(t, tag + "stat")
        for dg in range(NG):
            ins = t.matmul(pb["lnsum"][:, :], sc["ones16"][:, :], xb[:, dg, :],
                           start=(dg == 0), stop=(dg == NG - 1))
        p.inc(tag + "lnmm", ins, 1)

    @block.vector
    def _(v):
        p.wait(v, tag + "gxb")
        p.wait(v, tag + "lnmm")
        p.inc(tag + "sq", v.tensor_mul(sc["xsq"][:, :, :], xb[:, :, :], xb[:, :, :]), 1)

    @block.tensor
    def _(t):
        p.wait(t, tag + "sq")
        for dg in range(NG):
            ins = t.matmul(pb["lnsq"][:, :], sc["ones16"][:, :], sc["xsq"][:, dg, :],
                           start=(dg == 0), stop=(dg == NG - 1))
        p.inc(tag + "lnmm", ins, 1)

    @block.vector
    def _(v):
        p.wait(v, tag + "lnmm")
        st = sc["st"]
        A, B, C = st[0:1, 0:512], st[0:1, 512:1024], st[0:1, 1024:1536]
        p.inc(tag + "vch", v.tensor_scalar(A, pb["lnsum"][0:1, :], 1.0 / D, None, ALU.mult))
        p.wait(v, tag + "vch")
        p.inc(tag + "vch", v.tensor_scalar(B, pb["lnsq"][0:1, :], 1.0 / D, None, ALU.mult))
        p.wait(v, tag + "vch")
        p.inc(tag + "vch", v.tensor_mul(C, A, A))
        p.wait(v, tag + "vch")
        p.inc(tag + "vch", v.tensor_sub(B, B, C))
        p.wait(v, tag + "vch")
        p.inc(tag + "stv", v.tensor_scalar(B, B, 1e-5, None, ALU.add))

    @block.scalar
    def _(s):
        p.wait(s, tag + "stv")
        B = sc["st"][0:1, 512:1024]
        p.inc(tag + "sch", s.activation(B, B, AF.Ln))
        p.wait(s, tag + "sch")
        p.inc(tag + "sts", s.activation(B, B, AF.Exp, scale=-0.5))

    @block.vector
    def _(v):
        p.wait(v, tag + "sts")
        p.wait(v, tag + "bcm")
        A, B, C = sc["st"][0:1, 0:512], sc["st"][0:1, 512:1024], sc["st"][0:1, 1024:1536]
        p.inc(tag + "vch", v.tensor_copy(sc["a16"][0:1, :], B))
        p.inc(tag + "vch", v.tensor_mul(C, A, B))
        p.wait(v, tag + "vch")
        p.inc(tag + "stat", v.tensor_scalar(sc["b16"][0:1, :], C, -1.0, None, ALU.mult))

    @block.tensor
    def _(t):
        p.wait(t, tag + "stat")
        p.wait(t, tag + "bce")
        t.matmul(pb["psA"][:, :], sc["onesrow16"][:, :], sc["a16"][0:1, :], start=True, stop=True)
        ins = t.matmul(pb["psB"][:, :], sc["onesrow16"][:, :], sc["b16"][0:1, :], start=True, stop=True)
        p.inc(tag + "bcm", ins, 1)

    @block.vector
    def _(v):
        p.wait(v, tag + "bcm")
        v.tensor_copy(sc["A16"][:, :], pb["psA"][:, :])
        p.inc(tag + "bce", v.tensor_copy(sc["B16"][:, :], pb["psB"][:, :]), 1)
        if xn_guard is not None:
            nm, lag = xn_guard
            p.wait(v, nm, max(0, p.val(nm) - lag))
        p.wait(v, tag + "bce")
        a_bc = sc["A16"][:, :].unsqueeze(1).broadcast_to([128, NG, 512])
        b_bc = sc["B16"][:, :].unsqueeze(1).broadcast_to([128, NG, 512])
        p.inc(tag + "vch", v.tensor_tensor(xnc[:, :, :], xb[:, :, :], a_bc, ALU.mult))
        p.wait(v, tag + "vch")
        p.inc(tag + "xn", v.tensor_tensor(xnc[:, :, :], xnc[:, :, :], b_bc, ALU.add), 1)


def alloc_ln_scratch(p, stack, tag):
    nc = p.nc
    sc = {}
    sc["xsq"] = stack.enter_context(nc.sbuf_tensor(uniq(tag + "xsq"), [128, NG, 512], BF16))
    sc["st"] = stack.enter_context(nc.sbuf_tensor(uniq(tag + "st"), [1, 1536], F32))
    sc["a16"] = stack.enter_context(nc.sbuf_tensor(uniq(tag + "a16"), [1, 512], BF16))
    sc["b16"] = stack.enter_context(nc.sbuf_tensor(uniq(tag + "b16"), [1, 512], BF16))
    sc["A16"] = stack.enter_context(nc.sbuf_tensor(uniq(tag + "A16"), [128, 512], BF16))
    sc["B16"] = stack.enter_context(nc.sbuf_tensor(uniq(tag + "B16"), [128, 512], BF16))
    return sc


def emit_ln_consts(p, block, sc, tag):
    @block.sync
    def _(s):
        p.inc(tag + "cld", s.dma_start(out=sc["ones16"][:, :], in_=p.io["consts"]["ONESCOL16"]), 16)
        p.inc(tag + "cld", s.dma_start(out=sc["onesrow16"][:, :], in_=p.io["consts"]["ONESROW16"]), 16)


def emit_l0attn(p, x_in, x_out, nb):
    nc = p.nc
    cs = p.io["consts"]
    from contextlib import ExitStack
    with ExitStack() as es:
        def sb(nm, shp, dt):
            return es.enter_context(nc.sbuf_tensor(uniq(nm), shp, dt))
        Wo = sb("a_wo", [128, NG, D], BF16)
        bo = sb("a_bo", [128, NG], F32)
        Qb = sb("a_q", [128, NG, T], BF16)
        Kb = sb("a_k", [128, NG, T], BF16)
        Vb = sb("a_v", [128, 16, D], BF16)
        ones16 = sb("a_ones", [128, 1], BF16)
        onesrow16 = sb("a_onesr", [1, 128], BF16)
        onesmat = sb("a_onesm", [128, 128], BF16)
        block = es.enter_context(nc.Block())

        @block.sync
        def _(s):
            p.inc("acld", s.dma_start(out=ones16[:, :], in_=cs["ONESCOL16"]), 16)
            p.inc("acld", s.dma_start(out=onesrow16[:, :], in_=cs["ONESROW16"]), 16)
            p.inc("acld", s.dma_start(out=onesmat[:, :], in_=cs["ONESMAT16"]), 16)
            p.inc("awld", s.dma_start(out=bo[:, :], in_=p.io["bo_fm"]), 16)

        @block.gpsimd
        def _(g):
            p.inc("agwo", g.dma_start(
                out=Wo[:, :, :],
                in_=p.io["L0_attn_Wo"].rearrange("(kg p) m -> p kg m", p=128)), 16)

        for b in range(nb):
            _l0attn_batch(p, block, b, x_in, x_out, Wo, bo,
                          Qb, Kb, Vb, ones16, onesrow16, onesmat)

        @block.sync
        def _(s):
            p.wait(s, "aost")


def _l0attn_batch(p, block, b, x_in, x_out, Wo, bo,
                  Qb, Kb, Vb, ones16, onesrow16, onesmat):
    nc = p.nc
    from contextlib import ExitStack
    with ExitStack() as es:
        def sb(nm, shp, dt):
            return es.enter_context(nc.sbuf_tensor(uniq(nm), shp, dt))
        Wq = sb("a_wq", [128, NG, D], BF16)
        Wk = sb("a_wk", [128, NG, D], BF16)
        Wv = sb("a_wv", [128, NG, D], BF16)
        xb1 = sb("a_xb", [128, NG, 512], BF16)
        xn2 = sb("a_xn", [128, 2, NG, 512], BF16)
        sc = alloc_ln_scratch(p, es, "a_ln")
        sc["ones16"] = ones16
        sc["onesrow16"] = onesrow16
        lnsum = es.enter_context(nc.psum_tensor(uniq("a_lnsum"), [1, 512], F32))
        lnsq = es.enter_context(nc.psum_tensor(uniq("a_lnsq"), [1, 512], F32))
        psA = es.enter_context(nc.psum_tensor(uniq("a_psA"), [128, 512], F32))
        psB = es.enter_context(nc.psum_tensor(uniq("a_psB"), [128, 512], F32))
        pb = {"lnsum": lnsum, "lnsq": lnsq, "psA": psA, "psB": psB}
        pj = [es.enter_context(nc.psum_tensor(uniq(f"a_pj{i}"), [128, 512], F32))
              for i in range(3)]

        @block.gpsimd
        def _(g):
            if b > 0:
                p.wait(g, "aost")
            p.inc("awld2", g.dma_start(
                out=Wq[:, :, :], in_=p.io["L0_attn_Wq"].rearrange("(kg p) m -> p kg m", p=128)), 16)
            p.inc("awld2", g.dma_start(
                out=Wk[:, :, :], in_=p.io["L0_attn_Wk"].rearrange("(kg p) m -> p kg m", p=128)), 16)
            p.inc("awld2", g.dma_start(
                out=Wv[:, :, :], in_=p.io["L0_attn_Wv"].rearrange("(kg p) m -> p kg m", p=128)), 16)

        awld2_base = p.val("awld2")
        for tc in range(4):
            xb = xb1
            xnc = xn2[:, tc % 2]

            @block.gpsimd
            def _(g, tc=tc, xb=xb):
                if b > 0 and tc == 0:
                    p.wait(g, "aost")
                if b > 0 or tc >= 1:
                    p.wait(g, "axn")
                p.inc("agxb", g.dma_start(
                    out=xb[:, :, :],
                    in_=x_in[b, :, tc * 512:(tc + 1) * 512].rearrange("(dg p) t -> p dg t", p=128)), 16)

            emit_ln_chunk(p, block, "a", xb, xnc, sc, pb, xn_guard=("apmm", 24))

            @block.tensor
            def _(t, tc=tc, xnc=xnc):
                base_ev = p.val("apev")
                p.wait(t, "axn")
                t.wait_ge(p.s("awld2")[0], awld2_base)
                p.wait(t, "acld")
                k = 0
                for (W, dst, kind) in [(Wq, Qb, "qk"), (Wk, Kb, "qk"), (Wv, Vb, "v")]:
                    for gidx in range(NG):
                        bank = pj[k % 3]
                        t.wait_ge(p.s("apev")[0], base_ev if k < 3 else base_ev + k - 2)
                        if kind == "qk":
                            for kg in range(NG):
                                ins = t.matmul(bank[:, :], W[:, kg, gidx * 128:(gidx + 1) * 128],
                                               xnc[:, kg, :], start=(kg == 0), stop=(kg == NG - 1))
                        else:
                            half, tt4 = gidx % 2, gidx // 2
                            for kg in range(NG):
                                ins = t.matmul(bank[:, :], xnc[:, kg, tt4 * 128:(tt4 + 1) * 128],
                                               W[:, kg, half * 512:(half + 1) * 512],
                                               start=(kg == 0), stop=(kg == NG - 1))
                        p.inc("apmm", ins, 1)
                        k += 1

            @block.scalar
            def _(s, tc=tc):
                base_mm = p.val("apmm") - 24
                if b > 0 and tc == 0:
                    p.wait(s, "aapv")
                k = 0
                for (W, dst, kind) in [(Wq, Qb, "qk"), (Wk, Kb, "qk"), (Wv, Vb, "v")]:
                    for gidx in range(NG):
                        bank = pj[k % 3]
                        s.wait_ge(p.s("apmm")[0], base_mm + k + 1)
                        if kind == "qk":
                            ins = s.copy(dst[:, gidx, tc * 512:(tc + 1) * 512], bank[:, :])
                        else:
                            half, tt4 = gidx % 2, gidx // 2
                            vt = tc * 4 + tt4
                            ins = s.copy(dst[:, vt, half * 512:(half + 1) * 512], bank[:, :])
                        p.inc("apev", ins, 1)
                        k += 1

    with ExitStack() as es:
        def sb(nm, shp, dt):
            return es.enter_context(nc.sbuf_tensor(uniq(nm), shp, dt))
        Cb = sb("a_ctx", [128, NG, T], BF16)
        PT2 = sb("a_pt", [128, 16, 512], BF16)
        MSK = sb("a_msk", [128, 4, 512], BF16)
        rr32 = sb("a_rr32", [128, 512], F32)

        @block.sync
        def _(s):
            p.wait(s, "apev")
            for d_ in range(4):
                p.inc("amld", s.dma_start(out=MSK[:, d_, :], in_=p.io["consts"][f"CMASK{d_*128}"]), 16)
        from contextlib import ExitStack as _ES
        es2 = _ES()
        sbank = [es2.enter_context(nc.psum_tensor(uniq(f"a_s{i}"), [128, 512], F32)) for i in range(4)]
        ctxP = es2.enter_context(nc.psum_tensor(uniq("a_ctxP"), [128, 512], F32))
        rsP = es2.enter_context(nc.psum_tensor(uniq("a_rsP"), [128, 512], F32))
        njts = [4, 8, 12, 16]
        cum = [0, 4, 12, 24]
        for h in range(NH):
            hp, hr = h // 2, (h % 2) * 64
            qs = Qb[hr:hr + 64, hp, :]
            ks = Kb[hr:hr + 64, hp, :]
            base_aex = p.val("aaex")
            base_apv = p.val("aapv")
            base_amk = p.val("aamk")
            base_amm = p.val("aamm")
            base_arc = p.val("aarc")
            base_acn = p.val("aacn")

            def nmask(lh):
                n = 0
                for c2 in range(4):
                    for j2 in range(njts[c2]):
                        if cum[c2] + j2 >= lh:
                            return n
                        if j2 >= 4 * c2:
                            n += 1
                return n

            @block.tensor
            def _(t, h=h, hr=hr, qs=qs, ks=ks, base_aex=base_aex, base_amk=base_amk,
                  base_arc=base_arc, base_acn=base_acn, nmask=nmask):
                if h == 0:
                    p.wait(t, "apev")
                for ci in range(4):
                    i0 = ci * 512
                    for jt in range(njts[ci]):
                        lh = cum[ci] + jt
                        t.wait_ge(p.s("aaex")[0], base_aex + max(0, lh - 3))
                        ins = t.matmul(sbank[lh % 4][:, :], ks[:, jt * 128:(jt + 1) * 128],
                                       qs[:, i0:i0 + 512], start=True, stop=True,
                                       tile_position=(hr, 0))
                        p.inc("aamm", ins, 1)
                    t.wait_ge(p.s("aarc")[0], base_arc + ci)
                    for jt in range(njts[ci]):
                        lh = cum[ci] + jt
                        if jt >= 4 * ci:
                            t.wait_ge(p.s("aamk")[0], base_amk + nmask(lh) + 1)
                        else:
                            t.wait_ge(p.s("aaex")[0], base_aex + lh + 1)
                        t.matmul(rsP[:, :], onesmat[:, :], PT2[:, lh % 16, :],
                                 start=(jt == 0), stop=(jt == njts[ci] - 1))
                    t.wait_ge(p.s("aacn")[0], base_acn + ci)
                    for jt in range(njts[ci]):
                        lh = cum[ci] + jt
                        ins = t.matmul(ctxP[hr:hr + 64, :], Vb[:, jt, h * 64:(h + 1) * 64],
                                       PT2[:, lh % 16, :],
                                       start=(jt == 0), stop=(jt == njts[ci] - 1),
                                       tile_position=(0, hr))
                        p.inc("aapv", ins, 1)

            @block.scalar
            def _(s, h=h, base_amm=base_amm, base_apv=base_apv):
                for lh in range(40):
                    s.wait_ge(p.s("aamm")[0], base_amm + lh + 1)
                    if lh >= 2:
                        s.wait_ge(p.s("aapv")[0], base_apv + max(0, lh - 15))
                    ins = s.activation(PT2[:, lh % 16, :], sbank[lh % 4][:, :], AF.Exp, scale=0.125)
                    p.inc("aaex", ins, 1)

            @block.vector
            def _(v, h=h, hp=hp, hr=hr, base_aex=base_aex, base_apv=base_apv):
                if h == 0:
                    p.wait(v, "amld")
                for ci in range(4):
                    for jt in range(njts[ci]):
                        lh = cum[ci] + jt
                        if jt >= 4 * ci:
                            v.wait_ge(p.s("aaex")[0], base_aex + lh + 1)
                            d_ = jt - 4 * ci
                            ins = v.tensor_mul(PT2[:, lh % 16, :], PT2[:, lh % 16, :], MSK[:, d_, :])
                            p.inc("aamk", ins, 1)
                    v.wait_ge(p.s("aapv")[0], base_apv + cum[ci] + njts[ci])
                    p.inc("aarc", v.reciprocal(rr32[:, :], rsP[:, :]), 1)
                    if b > 0 and h == 0 and ci == 0:
                        p.wait(v, "aomm")
                    p.wait(v, "aarc")
                    p.inc("aacn", v.tensor_mul(Cb[hr:hr + 64, hp, ci * 512:(ci + 1) * 512],
                                               ctxP[hr:hr + 64, :], rr32[hr:hr + 64, :]), 1)

        es2.close()
        xc = sb("a_xc", [128, NG, 512], F32)
        ot = sb("a_ot", [128, NG, 512], F32)
        ov2 = sb("a_ov", [128, 2, 512], F32)
        po0 = es.enter_context(nc.psum_tensor(uniq("a_po0"), [128, 512], F32))
        po1 = es.enter_context(nc.psum_tensor(uniq("a_po1"), [128, 512], F32))
        po = [po0, po1]
        for tc in range(4):
            @block.sync
            def _(s, tc=tc):
                if tc > 0:
                    p.wait(s, "aoev")
                else:
                    p.wait(s, "aacn")
                p.inc("aoxc", s.dma_start(
                    out=xc[:, :, :],
                    in_=x_in[b, :, tc * 512:(tc + 1) * 512].rearrange("(dg p) t -> p dg t", p=128)), 16)

            @block.tensor
            def _(t, tc=tc):
                base_oev = p.val("aoev")
                p.wait(t, "aacn")
                p.wait(t, "awld")
                p.wait(t, "agwo")
                for mg in range(NG):
                    t.wait_ge(p.s("aoev")[0], base_oev if mg < 2 else base_oev + mg - 1)
                    for kg in range(NG):
                        ins = t.matmul(po[mg % 2][:, :], Wo[:, kg, mg * 128:(mg + 1) * 128],
                                       Cb[:, kg, tc * 512:(tc + 1) * 512],
                                       start=(kg == 0), stop=(kg == NG - 1))
                    p.inc("aomm", ins, 1)

            @block.vector
            def _(v, tc=tc):
                base_omm = p.val("aomm") - NG
                p.wait(v, "aoxc")
                if tc > 0:
                    p.wait(v, "aost")
                for mg in range(NG):
                    v.wait_ge(p.s("aomm")[0], base_omm + mg + 1)
                    p.inc("avch", v.tensor_scalar(ov2[:, mg % 2, :], po[mg % 2][:, :],
                                                  bo[:, mg:mg + 1], None, ALU.add))
                    p.wait(v, "avch")
                    p.inc("aoev", v.tensor_add(ot[:, mg, :], ov2[:, mg % 2, :], xc[:, mg, :]), 1)

            @block.scalar
            def _(s, tc=tc):
                p.wait(s, "aoev")
                p.inc("aost", s.dma_start(
                    out=x_out[b, :, tc * 512:(tc + 1) * 512].rearrange("(dg p) t -> p dg t", p=128),
                    in_=ot[:, :, :]), 16)


def emit_l0ffn(p, x_in, x_out, nb):
    nc = p.nc
    cs = p.io["consts"]
    from contextlib import ExitStack
    with ExitStack() as es:
        def sb(nm, shp, dt):
            return es.enter_context(nc.sbuf_tensor(uniq(nm), shp, dt))
        ones16 = sb("f_ones", [128, 1], BF16)
        onesrow16 = sb("f_onesr", [1, 128], BF16)
        onesrow32 = sb("f_onesr32", [1, 128], F32)
        onesmat = sb("f_onesm", [128, 128], BF16)
        ident16 = sb("f_id16", [128, 128], BF16)
        Wr = sb("f_wr", [128, NG, 8], BF16)
        tmp1 = sb("f_tmp1", [1, 1], F32)
        invt = sb("f_invt", [128, 1], F32)
        block = es.enter_context(nc.Block())

        @block.sync
        def _(s):
            p.inc("fcld", s.dma_start(out=ones16[:, :], in_=cs["ONESCOL16"]), 16)
            p.inc("fcld", s.dma_start(out=onesrow16[:, :], in_=cs["ONESROW16"]), 16)
            p.inc("fcld", s.dma_start(out=onesrow32[:, :], in_=cs["ONESROW32"]), 16)
            p.inc("fcld", s.dma_start(out=onesmat[:, :], in_=cs["ONESMAT16"]), 16)
            p.inc("fcld", s.dma_start(out=ident16[:, :], in_=cs["IDENT16"]), 16)
            p.inc("fcld", s.dma_start(out=tmp1[:, :], in_=p.io["L0_ffn_temperature"].rearrange("(one e) -> one e", one=1)), 16)

        @block.gpsimd
        def _(g):
            p.inc("gwr", g.dma_start(
                out=Wr[:, :, :], in_=p.io["L0_ffn_Wr"].rearrange("(kg p) e -> p kg e", p=128)), 16)

        with nc.psum_tensor(uniq("f_pt"), [128, 1], F32) as ptb:
            @block.tensor
            def _(t):
                p.wait(t, "fcld")
                p.inc("ftm", t.matmul(ptb[:, :], onesrow32[:, :], tmp1[:, :], start=True, stop=True), 1)

            @block.vector
            def _(v):
                p.wait(v, "ftm")
                p.inc("fvch", v.reciprocal(invt[:, :], ptb[:, :]))
                p.wait(v, "fvch")
                v.nop()

        for b in range(nb):
            _l0ffn_batch(p, block, b, x_in, x_out, ones16, onesrow16, onesmat,
                         ident16, Wr, invt)

        @block.sync
        def _(s):
            p.wait(s, "fst")


def _l0ffn_batch(p, block, b, x_in, x_out, ones16, onesrow16, onesmat, ident16, Wr, invt):
    nc = p.nc
    from contextlib import ExitStack
    with ExitStack() as es0:
        def sb0(nm, shp, dt):
            return es0.enter_context(nc.sbuf_tensor(uniq(nm), shp, dt))
        xn = sb0("f_xn", [128, NG, T], BF16)
        acc = sb0("f_acc", [128, NG, T], F32)
        w_all = sb0("f_wall", [128, 16, 8], F32)

        with ExitStack() as es:
            def sb(nm, shp, dt):
                return es.enter_context(nc.sbuf_tensor(uniq(nm), shp, dt))
            xb = sb("f_xb", [128, NG, 512], BF16)
            sc = alloc_ln_scratch(p, es, "f_ln")
            sc["ones16"] = ones16
            sc["onesrow16"] = onesrow16
            lnsum = es.enter_context(nc.psum_tensor(uniq("f_lnsum"), [1, 512], F32))
            lnsq = es.enter_context(nc.psum_tensor(uniq("f_lnsq"), [1, 512], F32))
            psA = es.enter_context(nc.psum_tensor(uniq("f_psA"), [128, 512], F32))
            psB = es.enter_context(nc.psum_tensor(uniq("f_psB"), [128, 512], F32))
            pb = {"lnsum": lnsum, "lnsq": lnsq, "psA": psA, "psB": psB}
            if b == 0:
                @block.tensor
                def _(t):
                    p.wait(t, "fvch")

            for tc in range(4):
                xnc = xn[:, :, tc * 512:(tc + 1) * 512]

                @block.gpsimd
                def _(g, tc=tc):
                    if b > 0 and tc == 0:
                        p.wait(g, "fst")
                    if tc >= 1:
                        p.wait(g, "fxn")
                    p.inc("fgxb", g.dma_start(
                        out=xb[:, :, :],
                        in_=x_in[b, :, tc * 512:(tc + 1) * 512].rearrange("(dg p) t -> p dg t", p=128)), 16)

                emit_ln_chunk(p, block, "f", xb, xnc, sc, pb, xn_guard=None)

            @block.gpsimd
            def _(g):
                if b > 0:
                    p.wait(g, "fst")
                p.inc("fmz", g.memset(acc[:, :, :], 0.0), 1)

        with (
            nc.psum_tensor(uniq("f_rz"), [128, 128], F32) as rz,
            nc.psum_tensor(uniq("f_rw"), [8, 128], F32) as rw_unused,
        ):
            with ExitStack() as es:
                rt = es.enter_context(nc.sbuf_tensor(uniq("f_rt"), [128, 16, 8], F32))
                r1 = es.enter_context(nc.sbuf_tensor(uniq("f_r1"), [128, 16, 8], F32))
                r2 = es.enter_context(nc.sbuf_tensor(uniq("f_r2"), [128, 16, 8], F32))
                r3 = es.enter_context(nc.sbuf_tensor(uniq("f_r3"), [128, 16, 8], F32))
                q1 = es.enter_context(nc.sbuf_tensor(uniq("f_q1"), [128, 16], F32))
                q2 = es.enter_context(nc.sbuf_tensor(uniq("f_q2"), [128, 16], F32))

                @block.tensor
                def _(t):
                    p.wait(t, "fxn")
                    p.wait(t, "gwr")
                    for tt in range(16):
                        for kg in range(NG):
                            ins = t.matmul(rz[:, tt * 8:(tt + 1) * 8],
                                           xn[:, kg, tt * 128:(tt + 1) * 128],
                                           Wr[:, kg, :], start=(kg == 0), stop=(kg == NG - 1))
                    p.inc("frz", ins, 1)

                @block.scalar
                def _(s):
                    p.wait(s, "frz")
                    p.inc("frsig", s.activation(rt[:, :, :], rz[:, :], AF.Sigmoid, scale=invt[:, :]), 1)

                @block.vector
                def _(v):
                    p.wait(v, "frsig")
                    p.inc("fvch", v.tensor_scalar(r1[:, :, :], rt[:, :, :], -0.3678794503211975, None, ALU.add))
                    p.wait(v, "fvch")
                    p.inc("fgc", v.tensor_scalar(r1[:, :, :], r1[:, :, :], --9.149754731652138e-09, None, ALU.add), 1)

                @block.scalar
                def _(s):
                    p.wait(s, "fgc")
                    p.inc("fabs", s.activation(r1[:, :, :], r1[:, :, :], AF.Abs), 1)

                @block.vector
                def _(v):
                    p.wait(v, "frsig")
                    p.inc("fvch", v.tensor_scalar(r2[:, :, :], rt[:, :, :], GOLDEN_LOWER, None, ALU.is_ge))
                    p.wait(v, "fvch")
                    p.inc("fvch", v.tensor_scalar(r3[:, :, :], rt[:, :, :], GOLDEN_UPPER, None, ALU.is_le))
                    p.wait(v, "fvch")
                    p.inc("frd", v.tensor_mul(r2[:, :, :], r2[:, :, :], r3[:, :, :]), 1)

                @block.scalar
                def _(s):
                    p.wait(s, "frd")
                    p.wait(s, "fabs")
                    p.inc("fsch", s.activation(rt[:, :, :], r1[:, :, :], AF.Exp, scale=-10.0))
                    p.wait(s, "fsch")
                    p.inc("fre", s.activation(r3[:, :, :], r1[:, :, :], AF.Exp, scale=-1.0 / 0.3), 1)

                @block.vector
                def _(v):
                    p.wait(v, "fre")
                    p.inc("fvch", v.tensor_mul(r1[:, :, :], rt[:, :, :], r2[:, :, :]))
                    p.wait(v, "fvch")
                    p.inc("fvch", v.tensor_reduce(q1[:, :], r1[:, :, :], AX.X, ALU.add))
                    p.wait(v, "fvch")
                    p.inc("fvch", v.tensor_scalar(q1[:, :], q1[:, :], 1e-8, None, ALU.is_lt))
                    p.wait(v, "fvch")
                    p.inc("fvch", v.tensor_reduce(q2[:, :], r3[:, :, :], AX.X, ALU.max))
                    p.wait(v, "fvch")
                    p.inc("fvch", v.tensor_tensor(r2[:, :, :], r3[:, :, :],
                                                  q2[:, :].unsqueeze(2).broadcast_to([128, 16, 8]), ALU.is_ge))
                    p.wait(v, "fvch")
                    p.inc("fvch", v.scalar_tensor_tensor(r2[:, :, :], r2[:, :, :], -1e30, r3[:, :, :],
                                                         ALU.mult, ALU.add))
                    p.wait(v, "fvch")
                    p.inc("fvch", v.tensor_reduce(q2[:, :], r2[:, :, :], AX.X, ALU.max))
                    p.wait(v, "fvch")
                    p.inc("fvch", v.tensor_tensor(r2[:, :, :], r3[:, :, :],
                                                  q2[:, :].unsqueeze(2).broadcast_to([128, 16, 8]), ALU.is_ge))
                    p.wait(v, "fvch")
                    p.inc("fvch", v.tensor_mul(r3[:, :, :], r3[:, :, :], r2[:, :, :]))
                    p.wait(v, "fvch")
                    p.inc("fvch", v.tensor_reduce(q2[:, :], r3[:, :, :], AX.X, ALU.add))
                    p.wait(v, "fvch")
                    p.inc("fvch", v.tensor_scalar(q2[:, :], q2[:, :], 1e-8, None, ALU.max))
                    p.wait(v, "fvch")
                    p.inc("fvch", v.reciprocal(q2[:, :], q2[:, :]))
                    p.wait(v, "fvch")
                    p.inc("fvch", v.tensor_tensor(r3[:, :, :], r3[:, :, :],
                                                  q2[:, :].unsqueeze(2).broadcast_to([128, 16, 8]), ALU.mult))
                    p.wait(v, "fvch")
                    p.inc("fvch", v.tensor_sub(r3[:, :, :], r3[:, :, :], r1[:, :, :]))
                    p.wait(v, "fvch")
                    p.inc("fvch", v.tensor_tensor(r3[:, :, :], r3[:, :, :],
                                                  q1[:, :].unsqueeze(2).broadcast_to([128, 16, 8]), ALU.mult))
                    p.wait(v, "fvch")
                    p.inc("fvch", v.tensor_add(r1[:, :, :], r1[:, :, :], r3[:, :, :]))
                    p.wait(v, "fvch")
                    p.inc("fvch", v.tensor_reduce(q2[:, :], r1[:, :, :], AX.X, ALU.add))
                    p.wait(v, "fvch")
                    p.inc("fvch", v.tensor_scalar(q2[:, :], q2[:, :], 1e-8, None, ALU.max))
                    p.wait(v, "fvch")
                    p.inc("fvch", v.reciprocal(q2[:, :], q2[:, :]))
                    p.wait(v, "fvch")
                    p.inc("frw", v.tensor_tensor(w_all[:, :, :], r1[:, :, :],
                                                 q2[:, :].unsqueeze(2).broadcast_to([128, 16, 8]), ALU.mult), 1)

        with ExitStack() as es:
            def sb(nm, shp, dt):
                return es.enter_context(nc.sbuf_tensor(uniq(nm), shp, dt))
            Wg = sb("f_wg", [128, NG, D], BF16)
            Wu = sb("f_wu", [128, NG, D], BF16)
            Wd = sb("f_wd", [128, NG, D], BF16)
            h_all = sb("f_h", [128, NG, T], BF16)
            tsil = sb("f_tsil", [128, 2, 512], F32)
            h1t = sb("f_h1", [128, 2, 512], F32)
            W16 = sb("f_w16", [128, 2, 512], BF16)
            Dt = sb("f_dt", [128, 4, 128], BF16)
            pg = [es.enter_context(nc.psum_tensor(uniq(f"f_pg{i}"), [128, 512], F32)) for i in range(2)]
            pu = [es.enter_context(nc.psum_tensor(uniq(f"f_pu{i}"), [128, 512], F32)) for i in range(2)]
            po = [es.enter_context(nc.psum_tensor(uniq(f"f_po{i}"), [128, 512], F32)) for i in range(3)]
            psW = es.enter_context(nc.psum_tensor(uniq("f_psW"), [128, 512], F32))

            for e in range(8):
                @block.gpsimd
                def _(g, e=e):
                    if e == 0:
                        p.wait(g, "fxn")
                        p.wait(g, "frw")
                    if e > 0 or b > 0:
                        p.wait(g, "fgm", p.val("fgm"))
                        p.wait(g, "fum", p.val("fum"))
                    p.inc("gwg", g.dma_start(
                        out=Wg[:, :, :],
                        in_=p.io["L0_ffn_Wg"][e].rearrange("(kg p) i -> p kg i", p=128)), 16)
                    p.inc("gwu", g.dma_start(
                        out=Wu[:, :, :],
                        in_=p.io["L0_ffn_Wu"][e].rearrange("(kg p) i -> p kg i", p=128)), 16)
                    if e > 0 or b > 0:
                        p.wait(g, "fdm", p.val("fdm"))
                    p.inc("gwd", g.dma_start(
                        out=Wd[:, :, :],
                        in_=p.io["L0_ffn_Wd"][e].rearrange("(ig p) d -> p ig d", p=128)), 16)

                base_gwg = p.val("gwg")
                base_gwu = p.val("gwu")
                base_gwd = p.val("gwd")

                for tp in range(2):
                    for sub in range(2):
                        @block.vector
                        def _(v, e=e, tp=tp, sub=sub):
                            if e == 0 and tp == 0 and sub == 0:
                                p.wait(v, "frw")
                            p.wait(v, "fbcw", p.val("fbcw"))
                            for tt4 in range(4):
                                tt = (2 * tp + sub) * 4 + tt4
                                p.inc("fD", v.tensor_scalar(Dt[:, tt4, :], ident16[:, :],
                                                            w_all[:, tt, e:e + 1], None, ALU.mult), 1)

                        @block.tensor
                        def _(t, e=e, tp=tp, sub=sub):
                            p.wait(t, "fD")
                            p.wait(t, "fw16", p.val("fw16"))
                            for tt4 in range(4):
                                ins = t.matmul(psW[:, tt4 * 128:(tt4 + 1) * 128], onesmat[:, :],
                                               Dt[:, tt4, :], start=True, stop=True)
                            p.inc("fbcw", ins, 1)

                        @block.vector
                        def _(v, e=e, tp=tp, sub=sub):
                            p.wait(v, "fbcw")
                            p.inc("fw16", v.tensor_copy(W16[:, sub, :], psW[:, :]), 1)

                    base_sil = p.val("fsil")
                    base_h1 = p.val("fh1")
                    base_gm = p.val("fgm")
                    base_um = p.val("fum")

                    @block.tensor
                    def _(t, e=e, tp=tp, base_sil=base_sil, base_h1=base_h1):
                        if e == 0 and tp == 0:
                            p.wait(t, "fxn")
                        if tp == 0:
                            t.wait_ge(p.s("gwg")[0], base_gwg)
                            t.wait_ge(p.s("gwu")[0], base_gwu)
                        for ig in range(NG):
                            t.wait_ge(p.s("fsil")[0], base_sil + 2 * ig)
                            for sub in range(2):
                                tcx = 2 * tp + sub
                                for kg in range(NG):
                                    ins = t.matmul(pg[sub][:, :], Wg[:, kg, ig * 128:(ig + 1) * 128],
                                                   xn[:, kg, tcx * 512:(tcx + 1) * 512],
                                                   start=(kg == 0), stop=(kg == NG - 1))
                                p.inc("fgm", ins, 1)
                            t.wait_ge(p.s("fh1")[0], base_h1 + 2 * ig)
                            for sub in range(2):
                                tcx = 2 * tp + sub
                                for kg in range(NG):
                                    ins = t.matmul(pu[sub][:, :], Wu[:, kg, ig * 128:(ig + 1) * 128],
                                                   xn[:, kg, tcx * 512:(tcx + 1) * 512],
                                                   start=(kg == 0), stop=(kg == NG - 1))
                                p.inc("fum", ins, 1)

                    @block.scalar
                    def _(s, e=e, tp=tp, base_sil=base_sil, base_h1=base_h1,
                          base_gm=base_gm):
                        for ig in range(NG):
                            for sub in range(2):
                                s.wait_ge(p.s("fgm")[0], base_gm + 2 * ig + sub + 1)
                                if ig >= 1:
                                    s.wait_ge(p.s("fh1")[0], base_h1 + 2 * (ig - 1) + sub + 1)
                                p.inc("fsil", s.activation(tsil[:, sub, :], pg[sub][:, :],
                                                           AF.Silu), 1)

                    @block.vector
                    def _(v, e=e, tp=tp, base_sil=base_sil, base_um=base_um):
                        p.wait(v, "fw16")
                        for ig in range(NG):
                            for sub in range(2):
                                v.wait_ge(p.s("fsil")[0], base_sil + 2 * ig + sub + 1)
                                v.wait_ge(p.s("fum")[0], base_um + 2 * ig + sub + 1)
                                p.inc("fh1", v.tensor_mul(h1t[:, sub, :], tsil[:, sub, :],
                                                          pu[sub][:, :]), 1)
                                p.wait(v, "fh1")
                                p.inc("fh", v.tensor_mul(
                                    h_all[:, ig, (2 * tp + sub) * 512:(2 * tp + sub + 1) * 512],
                                    h1t[:, sub, :], W16[:, sub, :]), 1)

                base_fh = p.val("fh") - 32

                @block.tensor
                def _(t, e=e):
                    base_dm = p.val("fdm")
                    t.wait_ge(p.s("gwd")[0], base_gwd)
                    g = 0
                    for tp in range(2):
                        t.wait_ge(p.s("fh")[0], base_fh + 16 * (tp + 1))
                        for dg in range(NG):
                            for sub in range(2):
                                tcx = 2 * tp + sub
                                G = base_dm + g
                                bank = po[G % 3]
                                t.wait_ge(p.s("fac")[0], max(0, G - 2))
                                for ig in range(NG):
                                    ins = t.matmul(bank[:, :], Wd[:, ig, dg * 128:(dg + 1) * 128],
                                                   h_all[:, ig, tcx * 512:(tcx + 1) * 512],
                                                   start=(ig == 0), stop=(ig == NG - 1))
                                p.inc("fdm", ins, 1)
                                g += 1

                @block.vector
                def _(v, e=e):
                    base_dm = p.val("fdm") - 32
                    if e == 0:
                        p.wait(v, "fmz")
                    g = 0
                    for tp in range(2):
                        for dg in range(NG):
                            for sub in range(2):
                                tcx = 2 * tp + sub
                                G = base_dm + g
                                v.wait_ge(p.s("fdm")[0], G + 1)
                                p.inc("fac", v.tensor_add(acc[:, dg, tcx * 512:(tcx + 1) * 512],
                                                          acc[:, dg, tcx * 512:(tcx + 1) * 512],
                                                          po[G % 3][:, :]), 1)
                                g += 1

        with ExitStack() as es:
            xc = es.enter_context(nc.sbuf_tensor(uniq("f_xc"), [128, NG, 512], F32))
            for tc in range(4):
                @block.sync
                def _(s, tc=tc):
                    if tc > 0:
                        p.wait(s, "fov")
                    else:
                        p.wait(s, "fac")
                    p.inc("fxc", s.dma_start(
                        out=xc[:, :, :],
                        in_=x_in[b, :, tc * 512:(tc + 1) * 512].rearrange("(dg p) t -> p dg t", p=128)), 16)

                @block.vector
                def _(v, tc=tc):
                    p.wait(v, "fxc")
                    p.wait(v, "fac")
                    if tc > 0:
                        p.wait(v, "fst")
                    p.inc("fov", v.tensor_add(acc[:, :, tc * 512:(tc + 1) * 512],
                                              acc[:, :, tc * 512:(tc + 1) * 512], xc[:, :, :]), 1)

                @block.scalar
                def _(s, tc=tc):
                    p.wait(s, "fov")
                    p.inc("fst", s.dma_start(
                        out=x_out[b, :, tc * 512:(tc + 1) * 512].rearrange("(dg p) t -> p dg t", p=128),
                        in_=acc[:, :, tc * 512:(tc + 1) * 512]), 16)


GTOK = 204

def emit_l1attn(p, x_in, x_out, nb):
    nc = p.nc
    cs = p.io["consts"]
    from contextlib import ExitStack
    with ExitStack() as es:
        def sb(nm, shp, dt):
            return es.enter_context(nc.sbuf_tensor(uniq(nm), shp, dt))
        Wo = sb("L_wo", [128, NG, D], BF16)
        Qb = sb("L_q", [128, NG, T], BF16)
        Kb = sb("L_k", [128, NG, T], BF16)
        Vb = sb("L_v", [128, 16, D], BF16)
        ID16 = sb("L_id16", [128, 128], BF16)
        ones16 = sb("L_ones", [128, 1], BF16)
        onesrow16 = sb("L_onesr", [1, 128], BF16)
        onesmat = sb("L_onesm", [128, 128], BF16)
        block = es.enter_context(nc.Block())

        @block.sync
        def _(s):
            p.inc("Lcld", s.dma_start(out=ones16[:, :], in_=cs["ONESCOL16"]), 16)
            p.inc("Lcld", s.dma_start(out=onesrow16[:, :], in_=cs["ONESROW16"]), 16)
            p.inc("Lcld", s.dma_start(out=onesmat[:, :], in_=cs["ONESMAT16"]), 16)
            p.inc("Lcld", s.dma_start(out=ID16[:, :], in_=cs["IDENT16"]), 16)

        @block.gpsimd
        def _(g):
            p.inc("Lgw", g.dma_start(
                out=Wo[:, :, :],
                in_=p.io["L1_attn_Wout"].rearrange("(kg p) m -> p kg m", p=128)), 16)

        for b in range(nb):
            _l1attn_batch(p, block, b, x_in, x_out, Wo, Qb, Kb, Vb,
                          ID16, ones16, onesrow16, onesmat)

        @block.sync
        def _(s):
            p.wait(s, "Lost")


def _l1attn_batch(p, block, b, x_in, x_out, Wo, Qb, Kb, Vb, ID16,
                  ones16, onesrow16, onesmat):
    nc = p.nc
    from contextlib import ExitStack
    with ExitStack() as es:
        def sb(nm, shp, dt):
            return es.enter_context(nc.sbuf_tensor(uniq(nm), shp, dt))
        Wq = sb("L_wq", [128, NG, D], BF16)
        Wk = sb("L_wk", [128, NG, D], BF16)
        Wv = sb("L_wv", [128, NG, D], BF16)
        xb1 = sb("L_xb", [128, NG, 512], BF16)
        xn1 = sb("L_xn", [128, NG, 512], BF16)
        qraw = sb("L_qraw", [128, 2, 512], BF16)
        rtmp = sb("L_rtmp", [128, 512], F32)
        C16 = sb("L_c16", [128, T], BF16)
        S16 = sb("L_s16", [128, T], BF16)
        SWP = sb("L_swp", [128, 128], BF16)
        sc = alloc_ln_scratch(p, es, "L_ln")
        sc["ones16"] = ones16
        sc["onesrow16"] = onesrow16
        lnsum = es.enter_context(nc.psum_tensor(uniq("L_lnsum"), [1, 512], F32))
        lnsq = es.enter_context(nc.psum_tensor(uniq("L_lnsq"), [1, 512], F32))
        psA = es.enter_context(nc.psum_tensor(uniq("L_psA"), [128, 512], F32))
        psB = es.enter_context(nc.psum_tensor(uniq("L_psB"), [128, 512], F32))
        psw = es.enter_context(nc.psum_tensor(uniq("L_psw"), [128, 512], F32))
        pb = {"lnsum": lnsum, "lnsq": lnsq, "psA": psA, "psB": psB}
        pj = [es.enter_context(nc.psum_tensor(uniq(f"L_pj{i}"), [128, 512], F32))
              for i in range(3)]

        @block.gpsimd
        def _(g):
            if b > 0:
                p.wait(g, "Lost")
            p.inc("Lgw2", g.dma_start(
                out=Wq[:, :, :], in_=p.io["L1_attn_Wq"].rearrange("(kg p) m -> p kg m", p=128)), 16)
            p.inc("Lgw2", g.dma_start(
                out=Wk[:, :, :], in_=p.io["L1_attn_Wk"].rearrange("(kg p) m -> p kg m", p=128)), 16)
            p.inc("Lgw2", g.dma_start(
                out=Wv[:, :, :], in_=p.io["L1_attn_Wv"].rearrange("(kg p) m -> p kg m", p=128)), 16)
            p.inc("Lgw2", g.dma_start(out=C16[:, :], in_=p.io["consts"]["ROPE_C"]), 16)
            p.inc("Lgw2", g.dma_start(out=S16[:, :], in_=p.io["consts"]["ROPE_S"]), 16)

        @block.sync
        def _(s):
            if b > 0:
                p.wait(s, "Lost")
            p.inc("Lsld", s.dma_start(out=SWP[:, :], in_=p.io["consts"]["SWAPMAT"]), 16)

        base_gw2 = p.val("Lgw2")
        for tc in range(4):
            xnc = xn1

            @block.gpsimd
            def _(g, tc=tc):
                if b > 0 and tc == 0:
                    p.wait(g, "Lost")
                if b > 0 or tc >= 1:
                    p.wait(g, "Lxn")
                p.inc("Lgxb", g.dma_start(
                    out=xb1[:, :, :],
                    in_=x_in[b, :, tc * 512:(tc + 1) * 512].rearrange("(dg p) t -> p dg t", p=128)), 16)

            emit_ln_chunk(p, block, "L", xb1, xnc, sc, pb, xn_guard=("Lpmm", 0))

            @block.tensor
            def _(t, tc=tc, xnc=xnc):
                base_ev = p.val("Lpev")
                base_sm = p.val("Lsm")
                p.wait(t, "Lxn")
                t.wait_ge(p.s("Lgw2")[0], base_gw2)
                p.wait(t, "Lcld")
                p.wait(t, "Lsld")
                k = 0
                for (W, kind) in [(Wq, "q"), (Wk, "k"), (Wv, "v")]:
                    for gidx in range(NG):
                        bank = pj[k % 3]
                        t.wait_ge(p.s("Lpev")[0], base_ev if k < 3 else base_ev + k - 2)
                        if kind in ("q", "k"):
                            for kg in range(NG):
                                ins = t.matmul(bank[:, :], W[:, kg, gidx * 128:(gidx + 1) * 128],
                                               xnc[:, kg, :], start=(kg == 0), stop=(kg == NG - 1))
                        else:
                            half, tt4 = gidx % 2, gidx // 2
                            for kg in range(NG):
                                ins = t.matmul(bank[:, :], xnc[:, kg, tt4 * 128:(tt4 + 1) * 128],
                                               W[:, kg, half * 512:(half + 1) * 512],
                                               start=(kg == 0), stop=(kg == NG - 1))
                        p.inc("Lpmm", ins, 1)
                        k += 1
                        if kind in ("q", "k"):
                            t.wait_ge(p.s("Lpev")[0], base_ev + k)
                            t.wait_ge(p.s("Lsm")[0], base_sm + k - 1)
                            ins = t.matmul(psw[:, :], SWP[:, :], qraw[:, (k - 1) % 2, :],
                                           start=True, stop=True)
                            p.inc("Lsw", ins, 1)

            @block.scalar
            def _(s, tc=tc):
                base_mm = p.val("Lpmm") - 24
                base_rp = p.val("Lrp")
                if b > 0 and tc == 0:
                    p.wait(s, "Lapv")
                k = 0
                for (W, kind) in [(Wq, "q"), (Wk, "k"), (Wv, "v")]:
                    for gidx in range(NG):
                        bank = pj[k % 3]
                        s.wait_ge(p.s("Lpmm")[0], base_mm + k + 1)
                        if kind in ("q", "k"):
                            kqk = k if k < 16 else None
                            if k >= 2 and kqk is not None:
                                s.wait_ge(p.s("Lrp")[0], base_rp + (k - 2) + 1)
                            ins = s.copy(qraw[:, k % 2, :], bank[:, :])
                        else:
                            half, tt4 = gidx % 2, gidx // 2
                            vt = tc * 4 + tt4
                            ins = s.copy(Vb[:, vt, half * 512:(half + 1) * 512], bank[:, :])
                        p.inc("Lpev", ins, 1)
                        k += 1

            @block.vector
            def _(v, tc=tc):
                base_sw = p.val("Lsw") - 16
                base_ev = p.val("Lpev") - 24
                sl = slice(tc * 512, (tc + 1) * 512)
                k = 0
                for (dst, kind) in [(Qb, "q"), (Kb, "k")]:
                    for gidx in range(NG):
                        v.wait_ge(p.s("Lpev")[0], base_ev + k + 1)
                        v.wait_ge(p.s("Lsw")[0], base_sw + k + 1)
                        p.wait(v, "Lrp")
                        p.inc("Lsm", v.tensor_mul(rtmp[:, :], psw[:, :], S16[:, sl]), 1)
                        p.wait(v, "Lsm")
                        p.inc("Lvch", v.tensor_tensor(qraw[:, k % 2, :], qraw[:, k % 2, :],
                                                      C16[:, sl], ALU.mult))
                        p.wait(v, "Lvch")
                        p.inc("Lrp", v.tensor_tensor(dst[:, gidx, sl], qraw[:, k % 2, :],
                                                     rtmp[:, :], ALU.add), 1)
                        k += 1

    with ExitStack() as es:
        def sb(nm, shp, dt):
            return es.enter_context(nc.sbuf_tensor(uniq(nm), shp, dt))
        Cb = sb("L_ctx", [128, NG, T], BF16)
        PTg = sb("L_ptg", [128, 2, 256], BF16)
        KLS = sb("L_kls", [128, T], BF16)
        KLT = sb("L_klt", [128, 2, 15, 64], BF16)
        kv16 = sb("L_kv16", [128, 64], BF16)
        mx = sb("L_mx", [128, 1], F32)
        asum = sb("L_asum", [128, 1], F32)
        rr32 = sb("L_rr32", [128, 256], F32)
        from contextlib import ExitStack as _ES
        es2 = _ES()
        sA2 = [es2.enter_context(nc.psum_tensor(uniq(f"L_sA{i}"), [128, 256], F32)) for i in range(2)]
        rsP = es2.enter_context(nc.psum_tensor(uniq("L_rsP"), [128, 256], F32))
        ctxP = es2.enter_context(nc.psum_tensor(uniq("L_ctxP"), [128, 256], F32))
        ptr = [es2.enter_context(nc.psum_tensor(uniq(f"L_ptr{i}"), [128, 64], F32))
               for i in range(2)]
        kvP = es2.enter_context(nc.psum_tensor(uniq("L_kvP"), [128, 64], F32))
        loP = es2.enter_context(nc.psum_tensor(uniq("L_loP"), [128, 512], F32))

        jcnt = [128, GTOK - 128]
        lpending = [None]
        for h in range(NH):
            hp, hr = h // 2, (h % 2) * 64
            qs = Qb[hr:hr + 64, hp, :]
            ks = Kb[hr:hr + 64, hp, :]
            base_ex = p.val("Laex")
            base_pv = p.val("Lapv")
            base_mm2 = p.val("Lamm")

            @block.tensor
            def _(t, h=h, hr=hr, qs=qs, ks=ks):
                if h == 0:
                    p.wait(t, "Lpev")
                    p.wait(t, "Lrp")
                p.wait(t, "Lnrm")
                for jt in range(2):
                    t.wait_ge(p.s("Laex")[0], base_ex + max(0, jt - 1))
                    ins = t.matmul(sA2[jt][0:jcnt[jt], 0:GTOK], ks[:, jt * 128:jt * 128 + jcnt[jt]],
                                   qs[:, 0:GTOK], start=True, stop=True, tile_position=(hr, 0))
                    p.inc("Lamm", ins, 1)
                for jt in range(2):
                    t.wait_ge(p.s("Laex")[0], base_ex + jt + 1)
                    t.matmul(rsP[:, 0:GTOK], onesmat[0:jcnt[jt], :], PTg[0:jcnt[jt], jt, 0:GTOK],
                             start=(jt == 0), stop=(jt == 1))
                for jt in range(2):
                    ins = t.matmul(ctxP[hr:hr + 64, 0:GTOK],
                                   Vb[0:jcnt[jt], jt, h * 64:(h + 1) * 64],
                                   PTg[0:jcnt[jt], jt, 0:GTOK],
                                   start=(jt == 0), stop=(jt == 1), tile_position=(0, hr))
                    p.inc("Lapv", ins, 1)

            @block.scalar
            def _(s, h=h, base_mm2=base_mm2, base_pv=base_pv):
                for jt in range(2):
                    s.wait_ge(p.s("Lamm")[0], base_mm2 + jt + 1)
                    if h > 0:
                        s.wait_ge(p.s("Lapv")[0], base_pv)
                    ins = s.activation(PTg[0:jcnt[jt], jt, 0:GTOK], sA2[jt][0:jcnt[jt], 0:GTOK],
                                       AF.Exp, scale=0.125)
                    p.inc("Laex", ins, 1)

            @block.vector
            def _(v, h=h, hp=hp, hr=hr, base_pv=base_pv):
                v.wait_ge(p.s("Lapv")[0], base_pv + 2)
                p.inc("Lrc", v.reciprocal(rr32[:, 0:GTOK], rsP[:, 0:GTOK]), 1)
                if b > 0 and h == 0:
                    p.wait(v, "Lomm")
                p.wait(v, "Lrc")
                p.inc("Lnrm", v.tensor_mul(Cb[hr:hr + 64, hp, 0:GTOK],
                                           ctxP[hr:hr + 64, 0:GTOK], rr32[hr:hr + 64, 0:GTOK]), 1)

            lfn = lpending[0]
            if lfn is not None:
                lfn()
            lpending[0] = None

            if h % 2 == 0:
                @block.vector
                def _(v, h=h, hp=hp):
                    p.wait(v, "Lkt", p.val("Lkt"))
                    p.inc("Lvch", v.memset(KLS[:, 128:GTOK], 0.0))
                    p.wait(v, "Lvch")
                    p.inc("Lmx", v.tensor_reduce(mx[:, :], Kb[:, hp, GTOK:T],
                                                 AX.X, ALU.max, negate=True), 1)

                @block.scalar
                def _(s, h=h, hp=hp):
                    p.wait(s, "Lmx")
                    ins = s.activation(KLS[:, GTOK:T], Kb[:, hp, GTOK:T],
                                       AF.Exp, bias=mx[:, :], accum_out=asum[:, :])
                    p.inc("Lke", ins, 1)

                @block.vector
                def _(v, h=h):
                    p.wait(v, "Lke")
                    p.inc("Lvch", v.reciprocal(asum[:, :], asum[:, :]))
                    p.wait(v, "Lvch")
                    p.inc("Lkn", v.tensor_scalar(KLS[:, GTOK:T], KLS[:, GTOK:T],
                                                 asum[:, :], None, ALU.mult), 1)

            kn_target = p.val("Lkn")

            def local_mm(h=h, hr=hr, hp=hp, qs=qs, kn_target=kn_target):
                @block.tensor
                def _(t):
                    t.wait_ge(p.s("Lkn")[0], kn_target)
                    base_kt = p.val("Lkte")
                    for tt in range(1, 16):
                        t.wait_ge(p.s("Lkte")[0], base_kt + tt - 2 if tt >= 3 else base_kt)
                        ins = t.matmul(ptr[tt % 2][:, :],
                                       KLS[hr:hr + 64, tt * 128:(tt + 1) * 128],
                                       ID16[hr:hr + 64, 64:128] if hr == 64 else ID16[0:64, 0:64],
                                       start=True, stop=True, tile_position=(hr, 0))
                        p.inc("Lkt", ins, 1)

                @block.vector
                def _(v):
                    base_kt = p.val("Lkt") - 15
                    for tt in range(1, 16):
                        v.wait_ge(p.s("Lkt")[0], base_kt + tt)
                        p.inc("Lkte", v.tensor_copy(KLT[:, h % 2, tt - 1, :], ptr[tt % 2][:, 0:64]), 1)

                @block.tensor
                def _(t):
                    p.wait(t, "Lkte")
                    for tt in range(1, 16):
                        ins = t.matmul(kvP[hr:hr + 64, :], Vb[:, tt, h * 64:(h + 1) * 64],
                                       KLT[:, h % 2, tt - 1, :], start=(tt == 1), stop=(tt == 15),
                                       tile_position=(0, hr))
                    p.inc("Lkv", ins, 1)

                @block.vector
                def _(v):
                    p.wait(v, "Lkv")
                    p.inc("Lkve", v.tensor_copy(kv16[hr:hr + 64, :], kvP[hr:hr + 64, :]), 1)

                @block.tensor
                def _(t):
                    base_loe = p.val("Lloe")
                    p.wait(t, "Lkve")
                    lo_chunks = [(GTOK, 512), (512, 1024), (1024, 1536), (1536, 2048)]
                    for i_, (c0, c1) in enumerate(lo_chunks):
                        t.wait_ge(p.s("Lloe")[0], base_loe + i_)
                        ins = t.matmul(loP[hr:hr + 64, 0:c1 - c0], kv16[hr:hr + 64, :],
                                       qs[:, c0:c1], start=True, stop=True,
                                       tile_position=(hr, hr))
                        p.inc("Llo", ins, 1)

                @block.vector
                def _(v):
                    base_lo = p.val("Llo") - 4
                    lo_chunks = [(GTOK, 512), (512, 1024), (1024, 1536), (1536, 2048)]
                    for i_, (c0, c1) in enumerate(lo_chunks):
                        v.wait_ge(p.s("Llo")[0], base_lo + i_ + 1)
                        p.inc("Lloe", v.tensor_copy(Cb[hr:hr + 64, hp, c0:c1],
                                                    loP[hr:hr + 64, 0:c1 - c0]), 1)

            lpending[0] = local_mm

        lfn = lpending[0]
        if lfn is not None:
            lfn()
        lpending[0] = None

        es2.close()
        xc = sb("L_xc", [128, NG, 512], F32)
        ot = sb("L_ot", [128, NG, 512], F32)
        po0 = es.enter_context(nc.psum_tensor(uniq("L_po0"), [128, 512], F32))
        po1 = es.enter_context(nc.psum_tensor(uniq("L_po1"), [128, 512], F32))
        po = [po0, po1]
        for tc in range(4):
            @block.sync
            def _(s, tc=tc):
                if tc > 0:
                    p.wait(s, "Loev")
                else:
                    p.wait(s, "Lnrm")
                p.inc("Loxc", s.dma_start(
                    out=xc[:, :, :],
                    in_=x_in[b, :, tc * 512:(tc + 1) * 512].rearrange("(dg p) t -> p dg t", p=128)), 16)

            @block.tensor
            def _(t, tc=tc):
                base_oev = p.val("Loev")
                p.wait(t, "Lnrm")
                p.wait(t, "Lloe")
                p.wait(t, "Lgw")
                for mg in range(NG):
                    t.wait_ge(p.s("Loev")[0], base_oev if mg < 2 else base_oev + mg - 1)
                    for kg in range(NG):
                        ins = t.matmul(po[mg % 2][:, :], Wo[:, kg, mg * 128:(mg + 1) * 128],
                                       Cb[:, kg, tc * 512:(tc + 1) * 512],
                                       start=(kg == 0), stop=(kg == NG - 1))
                    p.inc("Lomm", ins, 1)

            @block.vector
            def _(v, tc=tc):
                base_omm = p.val("Lomm") - NG
                p.wait(v, "Loxc")
                if tc > 0:
                    p.wait(v, "Lost")
                for mg in range(NG):
                    v.wait_ge(p.s("Lomm")[0], base_omm + mg + 1)
                    p.inc("Loev", v.tensor_add(ot[:, mg, :], po[mg % 2][:, :], xc[:, mg, :]), 1)

            @block.scalar
            def _(s, tc=tc):
                p.wait(s, "Loev")
                p.inc("Lost", s.dma_start(
                    out=x_out[b, :, tc * 512:(tc + 1) * 512].rearrange("(dg p) t -> p dg t", p=128),
                    in_=ot[:, :, :]), 16)


def emit_l1ffn(p, x_in, x_out, nb, h1s, wqs1, wqs2, out_td=None):
    nc = p.nc
    cs = p.io["consts"]
    from contextlib import ExitStack
    with ExitStack() as es:
        def sb(nm, shp, dt):
            return es.enter_context(nc.sbuf_tensor(uniq(nm), shp, dt))
        ones16 = sb("M_ones", [128, 1], BF16)
        onesrow16 = sb("M_onesr", [1, 128], BF16)
        onescol32 = sb("M_onesc32", [128, 1], F32)
        onesrow32 = sb("M_onesr32", [1, 128], F32)
        onesmat = sb("M_onesm", [128, 128], BF16)
        ident16 = sb("M_id16", [128, 128], BF16)
        ident32 = sb("M_id32", [128, 128], F32)
        cw = [sb(f"M_cw{k}", [128, 32], F32) for k in range(3)]
        cb = sb("M_cb", [128, 32], F32)
        srep1 = sb("M_sr1", [128, 2], F32)
        srep2 = sb("M_sr2", [128, 2], F32)
        block = es.enter_context(nc.Block())

        @block.sync
        def _(s):
            for nm, dst in [("ONESCOL16", ones16), ("ONESROW16", onesrow16),
                            ("ONESCOL32", onescol32), ("ONESROW32", onesrow32),
                            ("ONESMAT16", onesmat), ("IDENT16", ident16), ("IDENT32", ident32),
                            ("ROPE_C", None)][:7]:
                p.inc("Mcld", s.dma_start(out=dst[:, :], in_=cs[nm]), 16)
            for k in range(3):
                p.inc("Mcld", s.dma_start(out=cw[k][:, :], in_=p.io[f"convw{k}_fm"]), 16)
            p.inc("Mcld", s.dma_start(out=cb[:, :], in_=p.io["convb_fm"]), 16)

        with ExitStack() as esw:
            stage = esw.enter_context(nc.sbuf_tensor(uniq("M_stage"), [128, 2, 4096], F32))
            qst = esw.enter_context(nc.sbuf_tensor(uniq("M_qst"), [128, 2, 4096], BF16))
            ystage = esw.enter_context(nc.sbuf_tensor(uniq("M_ystage"), [128, 4096], F32))
            abuf = esw.enter_context(nc.sbuf_tensor(uniq("M_abuf"), [128, 4096], F32))
            rsum = esw.enter_context(nc.sbuf_tensor(uniq("M_rsum"), [128, 3], F32))
            sc1 = esw.enter_context(nc.sbuf_tensor(uniq("M_sc1"), [1, 4], F32))
            pm = esw.enter_context(nc.psum_tensor(uniq("M_pm"), [128, 2], F32))

            def w_src(wi, kg):
                if wi == 0:
                    return p.io["L1_ffn_w1T"][kg * 128:(kg + 1) * 128, :]
                return p.io["L1_ffn_w2T"][kg * 512:(kg + 1) * 512, :].rearrange(
                    "(c p) w -> p c w", p=128)

            def w_dst(wi, kg, wq):
                if wi == 0:
                    return wq[kg * 128:(kg + 1) * 128, :]
                return wq[kg * 512:(kg + 1) * 512, :].rearrange("(c p) w -> p c w", p=128)

            for wi, (wq, srep, nelem) in enumerate([
                    (wqs1, srep1, 4096 * 1024), (wqs2, srep2, 4096 * 1024)]):
                for kg in range(NG):
                    @block.sync
                    def _(s, wi=wi, kg=kg):
                        if kg > 1 or wi > 0:
                            p.wait(s, "Mab", max(0, p.val("Mab") - 1))
                            p.wait(s, "Myw", p.val("Myw"))
                        p.inc("Mwl" + ("b" if kg % 2 else ""),
                              s.dma_start(out=stage[:, kg % 2, :], in_=w_src(wi, kg)), 16)

                    @block.vector
                    def _(v, wi=wi, kg=kg):
                        p.wait(v, "Mwl" + ("b" if kg % 2 else ""))
                        if kg == 0:
                            p.inc("Mvch", v.memset(rsum[:, wi:wi + 1], 0.0))
                            p.wait(v, "Mvch")
                        p.wait(v, "Mab", p.val("Mab"))
                        p.inc("Mvch", v.tensor_reduce(rsum[:, 2:3], stage[:, kg % 2, :],
                                                      AX.X, ALU.add, apply_absolute_value=True))
                        p.wait(v, "Mvch")
                        p.inc("Mab", v.tensor_add(rsum[:, wi:wi + 1], rsum[:, wi:wi + 1],
                                                  rsum[:, 2:3]), 1)

                @block.tensor
                def _(t, wi=wi):
                    p.wait(t, "Mab")
                    p.wait(t, "Mcld")
                    p.inc("Mtm", t.matmul(pm[0:1, wi:wi + 1], onescol32[:, :], rsum[:, wi:wi + 1],
                                          start=True, stop=True), 1)

                @block.vector
                def _(v, wi=wi, nelem=nelem):
                    p.wait(v, "Mtm")
                    p.inc("Mvch", v.tensor_scalar(sc1[0:1, 0:1], pm[0:1, wi:wi + 1],
                                                  1.0 / nelem, None, ALU.mult))
                    p.wait(v, "Mvch")
                    p.inc("Mvch", v.tensor_scalar(sc1[0:1, 0:1], sc1[0:1, 0:1], 1e-5, None, ALU.max))
                    p.wait(v, "Mvch")
                    p.inc("Mvch", v.reciprocal(sc1[0:1, 1:2], sc1[0:1, 0:1]))
                    p.wait(v, "Mvch")
                    p.inc("Msc", v.tensor_copy(sc1[0:1, 2:4], sc1[0:1, 0:2]), 1)

                @block.tensor
                def _(t, wi=wi):
                    p.wait(t, "Msc")
                    p.inc("Mbc1", t.matmul(pm[:, 0:2], onesrow32[:, :], sc1[0:1, 0:2],
                                           start=True, stop=True), 1)

                @block.vector
                def _(v, wi=wi, srep=srep):
                    p.wait(v, "Mbc1")
                    p.inc("Mvch", v.tensor_copy(srep[:, 0:1], pm[:, 1:2]))
                    p.wait(v, "Mvch")
                    p.inc("Msr", v.tensor_copy(srep[:, 1:2], pm[:, 0:1]), 1)

                for kg in range(NG):
                    @block.sync
                    def _(s, wi=wi, kg=kg):
                        p.wait(s, "Myw", max(0, p.val("Myw") - 1))
                        if kg < 2:
                            p.wait(s, "Mab")
                        p.inc("Mwl2" + ("b" if kg % 2 else ""),
                              s.dma_start(out=stage[:, kg % 2, :], in_=w_src(wi, kg)), 16)

                    @block.vector
                    def _(v, wi=wi, kg=kg, srep=srep):
                        p.wait(v, "Mwl2" + ("b" if kg % 2 else ""))
                        if kg == 0:
                            p.wait(v, "Msr")
                        p.wait(v, "Mqz", p.val("Mqz"))
                        p.inc("Myw", v.tensor_scalar(ystage[:, :], stage[:, kg % 2, :],
                                                     srep[:, 0:1], None, ALU.mult), 1)
                        p.wait(v, "Myw")
                        p.inc("Mvch", v.tensor_scalar(abuf[:, :], ystage[:, :], 0.5, None, ALU.is_ge))
                        p.wait(v, "Mvch")
                        p.inc("Mvch", v.tensor_scalar(ystage[:, :], ystage[:, :], -0.5, None,
                                                      ALU.is_le))
                        p.wait(v, "Mvch")
                        p.wait(v, "Mqs" + ("b" if kg % 2 else ""))
                        p.inc("Mqz", v.tensor_sub(qst[:, kg % 2, :], abuf[:, :], ystage[:, :]), 1)

                    @block.scalar
                    def _(s, wi=wi, kg=kg, wq=wq):
                        p.wait(s, "Mqz")
                        p.inc("Mqs" + ("b" if kg % 2 else ""),
                              s.dma_start(out=w_dst(wi, kg, wq), in_=qst[:, kg % 2, :]), 16)

        for b in range(nb):
            _mark(p, f"l1f_b{b}")
            _l1ffn_batch(p, block, b, x_in, x_out, h1s, wqs1, wqs2, ones16, onesrow16,
                         onesmat, ident16, ident32, cw, cb, srep1, srep2, out_td)

        @block.sync
        def _(s):
            p.wait(s, "Mst")


def _l1ffn_batch(p, block, b, x_in, x_out, h1s, wqs1, wqs2, ones16, onesrow16,
                 onesmat, ident16, ident32, cw, cb, srep1, srep2, out_td=None):
    nc = p.nc
    from contextlib import ExitStack
    _mark(p, f"l1f_b{b}_p1")
    with ExitStack() as es:
        def sb(nm, shp, dt):
            return es.enter_context(nc.sbuf_tensor(uniq(nm), shp, dt))
        w1q = sb("M_w1q", [128, NG, 4096], BF16)
        xb1 = sb("M_xb", [128, NG, 512], BF16)
        xn1 = sb("M_xn", [128, NG, 512], BF16)
        yb = sb("M_y", [128, 2, NG, 512], BF16)
        ab = sb("M_a", [128, NG, 512], BF16)
        accb = sb("M_acc", [128, NG, 512], BF16)
        sgb = sb("M_sg", [128, NG, 512], BF16)
        mb = sb("M_mb", [128, NG, 512], BF16)
        Sbc = sb("M_sbc", [128, 512], BF16)
        Rbc = sb("M_rbc", [128, 512], BF16)
        mt = sb("M_mt", [128, 4], F32)
        st2 = sb("M_st2", [128, 4], F32)
        Dt = sb("M_dt", [128, 128], BF16)
        h1sb = sb("M_h1sb", [128, NG, 512], BF16)
        sc = alloc_ln_scratch(p, es, "M_ln")
        sc["ones16"] = ones16
        sc["onesrow16"] = onesrow16
        lnsum = es.enter_context(nc.psum_tensor(uniq("M_lnsum"), [1, 512], F32))
        lnsq = es.enter_context(nc.psum_tensor(uniq("M_lnsq"), [1, 512], F32))
        psA = es.enter_context(nc.psum_tensor(uniq("M_psA"), [128, 512], F32))
        psB = es.enter_context(nc.psum_tensor(uniq("M_psB"), [128, 512], F32))
        pq1 = es.enter_context(nc.psum_tensor(uniq("M_pq1"), [128, 512], F32))
        pw = [es.enter_context(nc.psum_tensor(uniq(f"M_pw{i}"), [128, 512], F32)) for i in range(3)]
        pb = {"lnsum": lnsum, "lnsq": lnsq, "psA": psA, "psB": psB}

        @block.gpsimd
        def _(g):
            if b == 0:
                p.wait(g, "Mqs")
                p.wait(g, "Mqsb")
            else:
                p.wait(g, "Mst")
            p.inc("Mwq1", g.dma_start(
                out=w1q[:, :, :], in_=wqs1.rearrange("(kg p) h -> p kg h", p=128)), 16)

        for tc in range(4):
            sl = slice(tc * 512, (tc + 1) * 512)

            @block.gpsimd
            def _(g, tc=tc):
                if b > 0 and tc == 0:
                    p.wait(g, "Mst")
                if b > 0 or tc >= 1:
                    p.wait(g, "cxn")
                p.inc("Mgxb", g.dma_start(
                    out=xb1[:, :, :],
                    in_=x_in[b, :, tc * 512:(tc + 1) * 512].rearrange("(dg p) t -> p dg t", p=128)), 16)

            emit_ln_chunk(p, block, "M", xb1, xn1, sc, pb, xn_guard=("Mw1", 0))

            @block.vector
            def _(v, tc=tc):
                p.wait(v, "Mxn")
                p.wait(v, "Mtr", p.val("Mtr"))
                p.inc("Mgm", v.tensor_reduce(ab[:, 0, :], xn1[:, :, :].transpose([0, 2, 1]),
                                             AX.X, ALU.max, apply_absolute_value=True), 1)

            @block.tensor
            def _(t, tc=tc):
                base_mt = p.val("Mmt")
                p.wait(t, "Mgm")
                p.wait(t, "Mbce2", p.val("Mbce2"))
                for i in range(4):
                    if i > 0:
                        t.wait_ge(p.s("Mmt")[0], base_mt + i)
                    ins = t.matmul(pq1[:, i * 128:(i + 1) * 128], ab[:, 0, i * 128:(i + 1) * 128],
                                   ident16[:, :], start=True, stop=True)
                    p.inc("Mtr", ins, 1)

            @block.vector
            def _(v, tc=tc):
                base_tr = p.val("Mtr") - 4
                for i in range(4):
                    v.wait_ge(p.s("Mtr")[0], base_tr + i + 1)
                    p.inc("Mmt", v.tensor_reduce(mt[:, i:i + 1], pq1[:, i * 128:(i + 1) * 128],
                                                  AX.X, ALU.max), 1)
                p.wait(v, "Mmt")
                p.inc("Mvch", v.tensor_scalar(mt[:, :], mt[:, :], 1e-5, None, ALU.max))
                p.wait(v, "Mvch")
                p.inc("Mvch", v.reciprocal(st2[:, :], mt[:, :]))
                p.wait(v, "Mvch")
                p.inc("Mvch", v.tensor_scalar(st2[:, :], st2[:, :], 127.0, None, ALU.mult))
                p.wait(v, "Mvch")
                p.inc("Msq2", v.tensor_scalar(mt[:, :], mt[:, :], 1.0 / 127.0, None, ALU.mult), 1)

            @block.vector
            def _(v, tc=tc):
                p.wait(v, "Msq2")
                p.wait(v, "Mbcm2", p.val("Mbcm2"))
                for i in range(4):
                    p.inc("MD", v.tensor_scalar(mb[:, 0, i * 128:(i + 1) * 128], ident16[:, :],
                                                st2[:, i:i + 1], None, ALU.mult), 1)

            @block.tensor
            def _(t, tc=tc):
                p.wait(t, "MD")
                p.wait(t, "Mmt")
                for i in range(4):
                    ins = t.matmul(pq1[:, i * 128:(i + 1) * 128], onesmat[:, :],
                                   mb[:, 0, i * 128:(i + 1) * 128], start=True, stop=True)
                p.inc("Mbcm2", ins, 1)

            @block.vector
            def _(v, tc=tc):
                p.wait(v, "Mbcm2")
                p.inc("Mbce2", v.tensor_copy(Sbc[:, :], pq1[:, :]), 1)
                for i in range(4):
                    p.inc("MD", v.tensor_scalar(mb[:, 1, i * 128:(i + 1) * 128], ident16[:, :],
                                                mt[:, i:i + 1], None, ALU.mult), 1)

            @block.tensor
            def _(t, tc=tc):
                p.wait(t, "MD")
                p.wait(t, "Mbce2", p.val("Mbce2"))
                for i in range(4):
                    ins = t.matmul(pq1[:, i * 128:(i + 1) * 128], onesmat[:, :],
                                   mb[:, 1, i * 128:(i + 1) * 128], start=True, stop=True)
                p.inc("Mbcm2", ins, 1)

            @block.vector
            def _(v, tc=tc):
                p.wait(v, "Mbcm2")
                p.inc("Mbce2", v.tensor_copy(Rbc[:, :], pq1[:, :]), 1)

            @block.vector
            def _(v, tc=tc):
                p.wait(v, "Mbce2")
                p.wait(v, "Mw1", max(0, p.val("Mw1") - 32))
                p.inc("My", v.tensor_tensor(yb[:, tc % 2, :, :], xn1[:, :, :],
                                            Sbc[:, :].unsqueeze(1).broadcast_to([128, NG, 512]),
                                            ALU.mult), 1)

            @block.scalar
            def _(s, tc=tc):
                p.wait(s, "My")
                p.inc("Msg2", s.sign(sgb[:, :, :], yb[:, tc % 2, :, :]), 1)
                p.wait(s, "Msg2")
                p.inc("Mabs2", s.activation(ab[:, :, :], yb[:, tc % 2, :, :], AF.Abs), 1)

            @block.vector
            def _(v, tc=tc):
                p.wait(v, "Mabs2")
                p.wait(v, "Mbcm2", p.val("Mbcm2"))
                p.inc("Mvch", v.tensor_scalar(accb[:, :, :], ab[:, :, :], 63.5, 64.0,
                                              ALU.is_ge, ALU.mult))
                p.wait(v, "Mvch")
                for B in (32.0, 16.0, 8.0, 4.0, 2.0, 1.0):
                    p.inc("Mvch", v.tensor_scalar(mb[:, :, :], accb[:, :, :], B - 0.5,
                                                  None, ALU.add))
                    p.wait(v, "Mvch")
                    p.inc("Mvch", v.tensor_tensor(mb[:, :, :], mb[:, :, :], ab[:, :, :],
                                                  ALU.is_le))
                    p.wait(v, "Mvch")
                    p.inc("Mvch", v.tensor_scalar(mb[:, :, :], mb[:, :, :], B, None, ALU.mult))
                    p.wait(v, "Mvch")
                    p.inc("Mvch", v.tensor_add(accb[:, :, :], accb[:, :, :], mb[:, :, :]))
                    p.wait(v, "Mvch")
                p.inc("Mvch", v.tensor_mul(sgb[:, :, :], accb[:, :, :], sgb[:, :, :]))
                p.wait(v, "Mvch")
                p.inc("Mxq", v.tensor_tensor(yb[:, tc % 2, :, :], sgb[:, :, :],
                                             Rbc[:, :].unsqueeze(1).broadcast_to([128, NG, 512]),
                                             ALU.mult), 1)

            for hgb in range(4):
                @block.tensor
                def _(t, tc=tc, hgb=hgb):
                    base_e = p.val("Mw1e")
                    if tc == 0 and hgb == 0:
                        p.wait(t, "Mwq1")
                    p.wait(t, "Mxq")
                    for hgi in range(NG):
                        hg = hgb * NG + hgi
                        t.wait_ge(p.s("Mw1e")[0], base_e if hgi < 3 else base_e + hgi - 2)
                        for kg in range(NG):
                            ins = t.matmul(pw[hgi % 3][:, :], w1q[:, kg, hg * 128:(hg + 1) * 128],
                                           yb[:, tc % 2, kg, :], start=(kg == 0), stop=(kg == NG - 1))
                        p.inc("Mw1", ins, 1)

                @block.scalar
                def _(s, tc=tc, hgb=hgb):
                    base_mm = p.val("Mw1") - NG
                    p.wait(s, "Mh1s", p.val("Mh1s"))
                    for hgi in range(NG):
                        s.wait_ge(p.s("Mw1")[0], base_mm + hgi + 1)
                        p.inc("Mw1e", s.copy(h1sb[:, hgi, :], pw[hgi % 3][:, :]), 1)

                @block.sync
                def _(s, tc=tc, hgb=hgb):
                    p.wait(s, "Mw1e")
                    p.inc("Mh1s", s.dma_start(
                        out=h1s[b, hgb * 1024:(hgb + 1) * 1024, tc * 512:(tc + 1) * 512]
                            .rearrange("(hgi p) t -> p hgi t", p=128),
                        in_=h1sb[:, :, :]), 16)

    _mark(p, f"l1f_b{b}_p2")
    with ExitStack() as es:
        def sb(nm, shp, dt):
            return es.enter_context(nc.sbuf_tensor(uniq(nm), shp, dt))
        w2q = sb("M_w2q", [128, 32, D], BF16)
        hc = sb("M_hc", [128, 32, 258], BF16)
        h2 = sb("M_h2", [128, 32, 256], BF16)
        y2 = sb("M_y2", [128, 32, 256], BF16)
        sg2 = sb("M_sg2", [128, 2, 32, 256], BF16)
        macc = sb("M_macc", [128, 32, 256], BF16)
        mbf = sb("M_mbf", [128, 32, 256], BF16)
        dgr = sb("M_dgr", [128, 4, 6, 128], BF16)
        gmb = sb("M_gmb", [128, 256], BF16)
        S2 = sb("M_s2", [128, 256], BF16)
        R2 = sb("M_r2", [128, 2, 256], BF16)
        mt2 = sb("M_mt2", [128, 2], F32)
        st22 = sb("M_st22", [128, 2], F32)
        xc2 = sb("M_xc2", [128, 4, 256], F32)
        oo = sb("M_oo", [128, 4, 256], F32)
        pq2 = es.enter_context(nc.psum_tensor(uniq("M_pq2"), [128, 512], F32))
        cpb = es.enter_context(nc.psum_tensor(uniq("M_cpb"), [128, 512], F32))
        po = [es.enter_context(nc.psum_tensor(uniq(f"M_po{i}"), [128, 512], F32)) for i in range(4)]
        ptb = [es.enter_context(nc.psum_tensor(uniq(f"M_ptb{i}"), [128, 128], F32))
               for i in range(2)] if out_td is not None else None

        @block.gpsimd
        def _(g):
            p.wait(g, "Mh1s")
            p.wait(g, "pev")
            p.wait(g, "Mxq")
            p.inc("Mwq2", g.dma_start(
                out=w2q[:, :, :], in_=wqs2.rearrange("(hg p) d -> p hg d", p=128)), 16)



        pending = [None]

        def emit_deferred():
            fn = pending[0]
            if fn is not None:
                fn()
            pending[0] = None

        def conv_stage(tcn):
            if tcn >= 8:
                return
            c0n, c1n = tcn * 256, (tcn + 1) * 256

            @block.sync
            def _(s, tcn=tcn, c0n=c0n, c1n=c1n):
                if tcn == 0:
                    p.wait(s, "Mw1e")
                    p.wait(s, "Mh1s")
                    p.wait(s, "Mxq")
                if tcn > 0:
                    p.wait(s, "Mcg", p.val("Mcg"))
                lo = max(0, c0n - 1)
                hi = min(T, c1n + 1)
                p.inc("Mh1l", s.dma_start(
                    out=hc[:, :, (1 if tcn == 0 else 0):(257 if tcn == 7 else 258)],
                    in_=h1s[b, :, lo:hi].rearrange("(hg p) t -> p hg t", p=128)), 16)

            if tcn in (0, 7):
                @block.gpsimd
                def _(g, tcn=tcn):
                    p.wait(g, "Mh1l")
                    p.inc("Mgch", g.memset(hc[:, :, 0:1] if tcn == 0 else hc[:, :, 257:258],
                                           0.0), 1)

            @block.gpsimd
            def _(g, tcn=tcn):
                base_cg = p.val("Mcg")
                for r in range(16):
                    g.wait_ge(p.s("Mcg")[0], max(0, base_cg + r - 3))
                    p.inc(f"Mdg{r % 4}", g.dma_start(
                        out=dgr[:, r % 4, :, :],
                        in_=p.io["convdiag"][6 * r:6 * r + 6].rearrange("n p m -> p n m")), 16)

            @block.tensor
            def _(t, tcn=tcn):
                base_sl2 = p.val("Msl2")
                base_dg = {s_: p.val(f"Mdg{s_}") - 64 for s_ in range(4)}
                p.wait(t, "Mh1l")
                if tcn in (0, 7):
                    p.wait(t, "Mgch")
                p.wait(t, "Mbce3", p.val("Mbce3"))
                for r in range(16):
                    bank = cpb if r % 2 == 0 else pq2
                    t.wait_ge(p.s("Msl2")[0], max(0, base_sl2 + 2 * r - 2))
                    t.wait_ge(p.s(f"Mdg{r % 4}")[0], base_dg[r % 4] + 16 * (r // 4 + 1))
                    for h2i in range(2):
                        hg = 2 * r + h2i
                        for tap in range(3):
                            ins = t.matmul(bank[:, h2i * 256:(h2i + 1) * 256],
                                           dgr[:, r % 4, h2i * 3 + tap, :],
                                           hc[:, hg, tap:tap + 256],
                                           start=(tap == 0), stop=(tap == 2))
                    p.inc("Mcg", ins, 1)

        conv_stage(0)
        for tc in range(8):
            c0, c1 = tc * 256, (tc + 1) * 256

            @block.scalar
            def _(s, tc=tc):
                base_cg = p.val("Mcg") - 16
                p.wait(s, "qy", p.val("qy"))
                for r in range(16):
                    bank = cpb if r % 2 == 0 else pq2
                    s.wait_ge(p.s("Mcg")[0], base_cg + r + 1)
                    for h2i in range(2):
                        hg = 2 * r + h2i
                        p.inc("Msl2", s.activation(h2[:, hg, :],
                                                   bank[:, h2i * 256:(h2i + 1) * 256],
                                                   AF.Silu, bias=cb[:, hg:hg + 1],
                                                   scale=srep1[:, 1:2]), 1)
                p.wait(s, "Msl2")
                p.wait(s, "Mq2", p.val("Mq2"))
                p.inc("Mabh", s.activation(macc[:, :, :], h2[:, :, :], AF.Abs), 1)

            @block.vector
            def _(v, tc=tc):
                p.wait(v, "Mabh")
                p.wait(v, "Mtr2", p.val("Mtr2"))
                for w in (16, 8, 4, 2):
                    p.inc("Mvch", v.tensor_tensor(macc[:, 0:w, :], macc[:, 0:w, :],
                                                  macc[:, w:2 * w, :], ALU.max))
                    p.wait(v, "Mvch")
                p.inc("Mgm2", v.tensor_tensor(gmb[:, :], macc[:, 0, :], macc[:, 1, :],
                                              ALU.max), 1)

            @block.tensor
            def _(t, tc=tc):
                base_mt = p.val("Mmt2")
                p.wait(t, "Mgm2")
                p.wait(t, "Mbce3", p.val("Mbce3"))
                for i in range(2):
                    if i > 0:
                        t.wait_ge(p.s("Mmt2")[0], base_mt + i)
                    ins = t.matmul(pq2[:, i * 128:(i + 1) * 128], gmb[:, i * 128:(i + 1) * 128],
                                   ident16[:, :], start=True, stop=True)
                    p.inc("Mtr2", ins, 1)

            @block.vector
            def _(v, tc=tc):
                base_tr = p.val("Mtr2") - 2
                for i in range(2):
                    v.wait_ge(p.s("Mtr2")[0], base_tr + i + 1)
                    p.inc("Mmt2", v.tensor_reduce(mt2[:, i:i + 1], pq2[:, i * 128:(i + 1) * 128],
                                                  AX.X, ALU.max), 1)
                p.wait(v, "Mmt2")
                p.inc("Mvch", v.tensor_scalar(mt2[:, :], mt2[:, :], 1e-5, None, ALU.max))
                p.wait(v, "Mvch")
                p.inc("Mvch", v.reciprocal(st22[:, :], mt2[:, :]))
                p.wait(v, "Mvch")
                p.inc("Mvch", v.tensor_scalar(st22[:, :], st22[:, :], 127.0, None, ALU.mult))
                p.wait(v, "Mvch")
                p.inc("Mvch", v.tensor_scalar(mt2[:, :], mt2[:, :], srep2[:, 1:2], 1.0 / 127.0,
                                              ALU.mult, ALU.mult))
                p.wait(v, "Mvch")
                p.inc("Msq3", v.tensor_copy(st22[:, :], st22[:, :]), 1)

            @block.vector
            def _(v, tc=tc):
                p.wait(v, "Msq3")
                p.wait(v, "Mbcm3", p.val("Mbcm3"))
                for j, (vecsrc, i) in enumerate([(st22, 0), (st22, 1), (mt2, 0), (mt2, 1)]):
                    p.inc("MD2", v.tensor_scalar(
                        mbf[:, j // 2, (j % 2) * 128:(j % 2 + 1) * 128], ident16[:, :],
                        vecsrc[:, i:i + 1], None, ALU.mult), 1)

            @block.tensor
            def _(t, tc=tc):
                p.wait(t, "MD2")
                p.wait(t, "Mmt2")
                for j in range(4):
                    ins = t.matmul(pq2[:, j * 128:(j + 1) * 128], onesmat[:, :],
                                   mbf[:, j // 2, (j % 2) * 128:(j % 2 + 1) * 128],
                                   start=True, stop=True)
                p.inc("Mbcm3", ins, 1)

            @block.vector
            def _(v, tc=tc):
                p.wait(v, "Mbcm3")
                p.inc("Mbce3", v.tensor_copy(S2[:, :], pq2[:, 0:256]))
                p.wait(v, "Mbce3")
                p.inc("Mbce3", v.tensor_copy(R2[:, tc % 2, :], pq2[:, 256:512]), 1)

            emit_deferred()
            conv_stage(tc + 1)

            @block.vector
            def _(v, tc=tc):
                p.wait(v, "Mbce3")
                p.wait(v, "Mq2", p.val("Mq2"))
                p.inc("My2", v.tensor_tensor(y2[:, :, :], h2[:, :, :],
                                             S2[:, :].unsqueeze(1).broadcast_to([128, 32, 256]),
                                             ALU.mult), 1)

            @block.scalar
            def _(s, tc=tc):
                p.wait(s, "My2")
                p.wait(s, "Mw2", max(0, p.val("Mw2") - 8))
                p.inc("Msg3", s.sign(sg2[:, tc % 2, :, :], y2[:, :, :]), 1)
                p.wait(s, "Msg3")
                p.inc("Mabs3", s.activation(y2[:, :, :], y2[:, :, :], AF.Abs), 1)

            def ladder(eng, chain, lo, hi, tc=tc):
                p.wait(eng, "Mabs3")
                p.inc(chain, eng.tensor_scalar(macc[:, lo:hi, :], y2[:, lo:hi, :], 63.5, 64.0,
                                               ALU.is_ge, ALU.mult))
                p.wait(eng, chain)
                for B in (32.0, 16.0, 8.0, 4.0, 2.0, 1.0):
                    p.inc(chain, eng.tensor_scalar(mbf[:, lo:hi, :], macc[:, lo:hi, :], B - 0.5,
                                                   None, ALU.add))
                    p.wait(eng, chain)
                    p.inc(chain, eng.tensor_tensor(mbf[:, lo:hi, :], mbf[:, lo:hi, :],
                                                   y2[:, lo:hi, :], ALU.is_le))
                    p.wait(eng, chain)
                    p.inc(chain, eng.tensor_scalar(mbf[:, lo:hi, :], mbf[:, lo:hi, :], B,
                                                   None, ALU.mult))
                    p.wait(eng, chain)
                    p.inc(chain, eng.tensor_add(macc[:, lo:hi, :], macc[:, lo:hi, :],
                                                mbf[:, lo:hi, :]))
                    p.wait(eng, chain)
                p.inc("Mq2", eng.tensor_mul(sg2[:, tc % 2, lo:hi, :], macc[:, lo:hi, :],
                                            sg2[:, tc % 2, lo:hi, :]), 1)

            @block.vector
            def _(v, tc=tc):
                p.wait(v, "Mbcm3", p.val("Mbcm3"))
                ladder(v, "Mlv", 0, 11, tc)
                ladder(v, "Mlv", 11, 22, tc)
                ladder(v, "Mlv", 22, 32, tc)

            @block.tensor
            def _(t, tc=tc):
                base_ev = p.val("Mov")
                if tc == 0:
                    p.wait(t, "Mwq2")
                p.wait(t, "Mq2")
                t.wait_ge(p.s("Mov")[0], base_ev)
                for dg in range(NG):
                    dst = po[dg % 4][:, (dg // 4) * 256:(dg // 4) * 256 + 256]
                    for hg in range(32):
                        ins = t.matmul(dst, w2q[:, hg, dg * 128:(dg + 1) * 128],
                                       sg2[:, tc % 2, hg, :], start=(hg == 0), stop=(hg == 31))
                    p.inc("Mw2", ins, 1)

            def deferred(tc=tc, c0=c0, c1=c1):
                base_mm = p.val("Mw2") - NG
                for half in range(2):
                    @block.sync
                    def _(s, half=half):
                        p.wait(s, "Mov")
                        if out_td is not None:
                            p.wait(s, "Mst")
                        p.inc("Mxc2", s.dma_start(
                            out=xc2[:, :, :],
                            in_=x_in[b, half * 512:(half + 1) * 512, c0:c1]
                                .rearrange("(dg p) t -> p dg t", p=128)), 16)

                    @block.vector
                    def _(v, half=half):
                        p.wait(v, "Mst")
                        for dgi in range(4):
                            dg = half * 4 + dgi
                            v.wait_ge(p.s("Mw2")[0], base_mm + dg + 1)
                            src_ = po[dg % 4][:, (dg // 4) * 256:(dg // 4) * 256 + 256]
                            p.inc("Mev2", v.tensor_tensor(oo[:, dgi, :], src_,
                                                          R2[:, tc % 2, :], ALU.mult), 1)

                    @block.gpsimd
                    def _(g, half=half):
                        base_ev2 = p.val("Mev2") - 4
                        p.wait(g, "Mxc2")
                        for dgi in range(4):
                            g.wait_ge(p.s("Mev2")[0], base_ev2 + dgi + 1)
                            p.inc("Mov", g.tensor_add(oo[:, dgi, :], oo[:, dgi, :],
                                                      xc2[:, dgi, :]), 1)

                    if out_td is None:
                        @block.scalar
                        def _(s, half=half):
                            p.wait(s, "Mov")
                            p.inc("Mst", s.dma_start(
                                out=x_out[b, half * 512:(half + 1) * 512, c0:c1]
                                    .rearrange("(dg p) t -> p dg t", p=128),
                                in_=oo[:, :, :]), 16)
                    else:
                        @block.tensor
                        def _(t, half=half):
                            base_te = p.val("Mte")
                            p.wait(t, "Mov")
                            k = 0
                            for dgi in range(4):
                                for tt in range(2):
                                    if k >= 2:
                                        t.wait_ge(p.s("Mte")[0], base_te + k - 1)
                                    ins = t.transpose(ptb[k % 2][:, :],
                                                      oo[:, dgi, tt * 128:(tt + 1) * 128],
                                                      ident32[:, :])
                                    p.inc("Mtt", ins, 1)
                                    k += 1

                        @block.vector
                        def _(v, half=half):
                            base_tt = p.val("Mtt") - 8
                            k = 0
                            for dgi in range(4):
                                for tt in range(2):
                                    v.wait_ge(p.s("Mtt")[0], base_tt + k + 1)
                                    p.inc("Mte", v.tensor_copy(
                                        xc2[:, tt * 2 + dgi // 2,
                                            (dgi % 2) * 128:(dgi % 2) * 128 + 128],
                                        ptb[k % 2][:, :]), 1)
                                    k += 1

                        @block.scalar
                        def _(s, half=half):
                            p.wait(s, "Mte")
                            for tt in range(2):
                                p.inc("Mst", s.dma_start(
                                    out=out_td[b, c0 + tt * 128:c0 + (tt + 1) * 128,
                                               half * 512:(half + 1) * 512]
                                        .rearrange("t (a c) -> t a c", a=2),
                                    in_=xc2[:, tt * 2:tt * 2 + 2, :]), 16)

            pending[0] = deferred

        emit_deferred()


def build_program(nb, stages=("trin", "l0a", "l0f", "l1a", "l1f", "trout")):
    from contextlib import ExitStack
    nc = bass.Bass("TRN2", target_bir_lowering=False)
    stack = ExitStack()
    p = P(nc, stack)
    p.io = declare_io(nc, nb)
    x_in = nc.dram_tensor("x", [nb, T, D], F32, kind="ExternalInput").ap()
    out = nc.dram_tensor("out", [nb, T, D], F32, kind="ExternalOutput").ap()
    xr0 = nc.dram_tensor("xr0", [nb, D, T], F32, kind="Internal").ap()
    xr1 = nc.dram_tensor("xr1", [nb, D, T], F32, kind="Internal").ap()
    xr2 = nc.dram_tensor("xr2", [nb, D, T], F32, kind="Internal").ap()
    xr3 = nc.dram_tensor("xr3", [nb, D, T], F32, kind="Internal").ap()
    xr4 = nc.dram_tensor("xr4", [nb, D, T], F32, kind="Internal").ap()
    h1s = nc.dram_tensor("h1s", [nb, 4096, T], BF16, kind="Internal").ap()
    wqs1 = nc.dram_tensor("wqs1", [D, 4096], BF16, kind="Internal").ap()
    wqs2 = nc.dram_tensor("wqs2", [4096, D], BF16, kind="Internal").ap()
    scratch = [xr0, xr1, xr2, xr3, xr4]
    with stack:
        cur = None
        if "trin" in stages:
            emit_tr_td2fm(p, x_in, xr0, nb, "ti")
            cur = 0
        if "l0a" in stages:
            emit_l0attn(p, scratch[cur], scratch[cur + 1], nb)
            cur += 1
        if "l0f" in stages:
            emit_l0ffn(p, scratch[cur], scratch[cur + 1], nb)
            cur += 1
        if "l1a" in stages:
            emit_l1attn(p, scratch[cur], scratch[cur + 1], nb)
            cur += 1
        if "l1f" in stages:
            emit_l1ffn(p, scratch[cur], scratch[cur + 1], nb, h1s, wqs1, wqs2,
                       out_td=(out if "trout" in stages else None))
            cur += 1
        elif "trout" in stages:
            emit_tr_fm2td(p, scratch[cur], out, nb, "to")
            cur = None
        if "l1f" in stages and "trout" not in stages:
            pass
    return nc


def kernel(**inputs):
    nb = 2
    maps = shard_inputs(inputs, nb)
    nc = build_program(nb)
    res = run_bass_kernel_spmd(nc, maps, core_ids=list(range(NCORES)))
    return np.concatenate([r["out"] for r in res.results], axis=0).astype(np.float32)


# version 20 (final)
# speedup vs baseline: 1.0270x; 1.0270x over previous
import numpy as np
import concourse.bass as bass
import concourse.mybir as mybir
from concourse.bass_utils import run_bass_kernel_spmd

F32 = mybir.dt.float32
BF16 = mybir.dt.bfloat16
AF = mybir.ActivationFunctionType
ALU = mybir.AluOpType
AX = mybir.AxisListType

D = 1024
T = 2048
NG = 8
NH = 16
HD = 64
NCORES = 8

PROF_MARKS = []


def _mark(p, label):
    PROF_MARKS.append((label, p.nc.n_instructions()))


GOLDEN_CENTER = 0.36787944117144233
GOLDEN_LOWER = 0.21231792754821535
GOLDEN_UPPER = 0.5


def host_consts():
    import ml_dtypes
    bf = ml_dtypes.bfloat16
    c = {}
    c["IDENT32"] = np.eye(128, dtype=np.float32)
    c["IDENT16"] = np.eye(128, dtype=bf)
    c["ONESCOL16"] = np.ones((128, 1), dtype=bf)
    c["ONESROW16"] = np.ones((1, 128), dtype=bf)
    c["ONESCOL32"] = np.ones((128, 1), dtype=np.float32)
    c["ONESROW32"] = np.ones((1, 128), dtype=np.float32)
    c["ONESMAT16"] = np.ones((128, 128), dtype=bf)
    j = np.arange(128)[:, None]
    i = np.arange(512)[None, :]
    for d_ in (0, 128, 256, 384):
        c[f"CMASK{d_}"] = ((j + d_) <= i).astype(bf)
    invfreq = (np.float32(1.0) / (np.float32(10000.0) **
               (np.arange(0, HD, 2, dtype=np.float32) / np.float32(HD))))
    pos = np.arange(T, dtype=np.float32)
    fr = (pos[None, :] * invfreq[:, None]).astype(np.float32)
    cos = np.cos(fr.astype(np.float64)).astype(np.float32)
    sin = np.sin(fr.astype(np.float64)).astype(np.float32)
    C = np.concatenate([cos, cos, cos, cos], axis=0)
    S = np.concatenate([-sin, sin, -sin, sin], axis=0)
    c["ROPE_C"] = C
    c["ROPE_S"] = S
    perm = np.arange(128)
    perm = perm + np.where((perm % 64) < 32, 32, -32)
    SW = np.zeros((128, 128), dtype=bf)
    SW[perm, np.arange(128)] = 1.0
    c["SWAPMAT"] = SW
    return c


def shard_inputs(inputs, nb=2):
    import ml_dtypes
    consts = host_consts()
    x = np.ascontiguousarray(inputs["L0_attn_x"])
    ncores = x.shape[0] // nb
    weights = {}
    for k, v in inputs.items():
        if k == "L0_attn_x":
            continue
        weights[k] = np.ascontiguousarray(v)
    weights["L1_ffn_w1T"] = np.ascontiguousarray(inputs["L1_ffn_w1"].T)
    weights["L1_ffn_w2T"] = np.ascontiguousarray(inputs["L1_ffn_w2"].T)
    weights["bo_fm"] = np.ascontiguousarray(inputs["L0_attn_bo"].reshape(NG, 128).T)
    cw = inputs["L1_ffn_conv_w"]
    for k_ in range(3):
        weights[f"convw{k_}_fm"] = np.ascontiguousarray(cw[:, 0, k_].reshape(32, 128).T)
    weights["convb_fm"] = np.ascontiguousarray(inputs["L1_ffn_conv_b"].reshape(32, 128).T)
    cd = np.zeros((96, 128, 128), dtype=np.float32)
    idx = np.arange(128)
    for hg_ in range(32):
        for k_ in range(3):
            cd[hg_ * 3 + k_, idx, idx] = cw[hg_ * 128:(hg_ + 1) * 128, 0, k_]
    weights["convdiag"] = cd
    weights.update(consts)
    maps = []
    for ci in range(ncores):
        m = dict(weights)
        m["x"] = np.ascontiguousarray(x[ci * nb:(ci + 1) * nb])
        maps.append(m)
    return maps


_UID = [0]

def uniq(nm):
    _UID[0] += 1
    return f"{nm}_{_UID[0]}"


_SEM_ALIAS = {}
for _t in ("a", "f", "L", "M"):
    for _r in ("gxb", "lnmm", "sq", "stv", "sts", "stat", "bcm", "bce", "xn",
               "vch", "sch", "cld"):
        _SEM_ALIAS[_t + _r] = "c" + _r
for _t in ("ti", "to"):
    for _r in ("ld", "tr", "ev", "st"):
        _SEM_ALIAS[_t + _r] = "t" + _r
for _n, _c in [
    ("apmm", "pmm"), ("Lpmm", "pmm"), ("Mw1", "pmm"), ("fgm", "pmm"),
    ("apev", "pev"), ("Lpev", "pev"), ("Mw1e", "pev"),
    ("fum", "pm2"), ("Mw2", "pm2"),
    ("aaex", "ex"), ("Laex", "ex"),
    ("aapv", "pv"), ("Lapv", "pv"),
    ("aamm", "smm"), ("Lamm", "smm"),
    ("aabc", "abc"), ("Labc", "abc"),
    ("aarc", "rc"), ("Lrc", "rc"),
    ("aacn", "nrm"), ("Lnrm", "nrm"),
    ("aoev", "oev"), ("Loev", "oev"), ("fov", "oev"), ("Mov", "oev"),
    ("aomm", "omm"), ("Lomm", "omm"),
    ("aost", "ost"), ("Lost", "ost"), ("fst", "ost"), ("Mst", "ost"),
    ("aoxc", "oxc"), ("Loxc", "oxc"), ("fxc", "oxc"), ("Mxc2", "oxc"),
    ("awld2", "gw"), ("agwo", "gw"), ("Lgw", "gw"), ("Lgw2", "gw"),
    ("gwg", "gw1"), ("gwu", "gw2"), ("gwd", "gw3"), ("gwr", "gw"),
    ("awld", "scl"), ("amld", "scl"), ("Lsld", "scl"), ("Mcld", "scl"),
    ("Mwl", "swl"), ("Mwl2", "swl"),
    ("fsil", "sil"), ("Msil", "sil"),
    ("fD", "Dg"), ("MD", "Dg"), ("MD2", "Dg"),
    ("fbcw", "bcw"), ("Mbcm2", "bcw"), ("Mbcm3", "bcw"),
    ("fw16", "bcev"), ("Mbce2", "bcev"), ("Mbce3", "bcev"),
    ("Mabs2", "sabs"), ("Mabs3", "sabs"), ("fabs", "sabs"),
    ("Msg2", "ssg"), ("Msg3", "ssg"), ("Msg", "ssg"),
    ("Mgm", "gm"), ("Mgm2", "gm"),
    ("Mtr", "qtr"), ("Mtr2", "qtr"),
    ("Mmt", "qmt"), ("Mmt2", "qmt"),
    ("Msq2", "qsq"), ("Msq3", "qsq"),
    ("My", "qy"), ("My2", "qy"),
    ("Mq2", "qq"), ("Mxq", "qq"),
]:
    _SEM_ALIAS[_n] = _c


class P:
    def __init__(self, nc, stack):
        self.nc = nc
        self.stack = stack
        self.sems = {}

    def s(self, name):
        name = _SEM_ALIAS.get(name, name)
        if name not in self.sems:
            h = self.stack.enter_context(self.nc.semaphore(name))
            self.sems[name] = [h, 0]
        return self.sems[name]

    def inc(self, name, instr, n=1):
        rec = self.s(name)
        rec[1] += n
        instr.then_inc(rec[0], n)
        return rec[1]

    def val(self, name):
        return self.s(name)[1]

    def wait(self, eng, name, val=None):
        rec = self.s(name)
        eng.wait_ge(rec[0], rec[1] if val is None else val)

    def sb(self, name, shape, dtype):
        return self.stack.enter_context(self.nc.sbuf_tensor(name, shape, dtype))

    def ps(self, name, shape, dtype=F32):
        return self.stack.enter_context(self.nc.psum_tensor(name, shape, dtype))


def declare_io(nc, nb, fm_external=False):
    io = {}
    io["consts"] = {}
    for nm, shp, dt in [
        ("IDENT32", [128, 128], F32), ("IDENT16", [128, 128], BF16),
        ("ONESCOL16", [128, 1], BF16), ("ONESROW16", [1, 128], BF16),
        ("ONESCOL32", [128, 1], F32), ("ONESROW32", [1, 128], F32),
        ("ONESMAT16", [128, 128], BF16),
        ("CMASK0", [128, 512], BF16), ("CMASK128", [128, 512], BF16),
        ("CMASK256", [128, 512], BF16), ("CMASK384", [128, 512], BF16),
        ("ROPE_C", [128, T], F32), ("ROPE_S", [128, T], F32),
        ("SWAPMAT", [128, 128], BF16),
    ]:
        io["consts"][nm] = nc.dram_tensor(nm, shp, F32 if dt == F32 else dt, kind="ExternalInput").ap()
    for nm, shp in [
        ("L0_attn_Wq", [D, D]), ("L0_attn_Wk", [D, D]), ("L0_attn_Wv", [D, D]),
        ("L0_attn_Wo", [D, D]), ("L0_attn_bo", [D]), ("bo_fm", [128, 8]),
        ("L0_ffn_Wg", [8, D, D]), ("L0_ffn_Wu", [8, D, D]), ("L0_ffn_Wd", [8, D, D]),
        ("L0_ffn_Wr", [D, 8]), ("L0_ffn_temperature", [1]),
        ("L1_attn_Wq", [D, D]), ("L1_attn_Wk", [D, D]), ("L1_attn_Wv", [D, D]),
        ("L1_attn_Wout", [D, D]),
        ("L1_ffn_w1", [4096, D]), ("L1_ffn_w1T", [D, 4096]),
        ("L1_ffn_w2", [D, 4096]), ("L1_ffn_w2T", [4096, D]),
        ("L1_ffn_conv_w", [4096, 1, 3]), ("L1_ffn_conv_b", [4096]),
        ("convw0_fm", [128, 32]), ("convw1_fm", [128, 32]), ("convw2_fm", [128, 32]),
        ("convb_fm", [128, 32]), ("convdiag", [96, 128, 128]),
    ]:
        io[nm] = nc.dram_tensor(nm, shp, F32, kind="ExternalInput").ap()
    return io


def emit_tr_td2fm(p, src_td, dst_fm, nb, tag):
    nc = p.nc
    with (
        nc.sbuf_tensor(uniq(f"{tag}_in"), [128, 16, D], F32) as tin,
        nc.sbuf_tensor(uniq(f"{tag}_out"), [128, NG, T], F32) as tout,
        nc.sbuf_tensor(uniq(f"{tag}_id"), [128, 128], F32) as ident,
        nc.psum_tensor(uniq(f"{tag}_p0"), [128, 128], F32) as p0,
        nc.psum_tensor(uniq(f"{tag}_p1"), [128, 128], F32) as p1,
        nc.Block() as block,
    ):
        banks = [p0, p1]
        ld, tr, ev, st = f"{tag}ld", f"{tag}tr", f"{tag}ev", f"{tag}st"
        ld0, tr0, ev0, st0 = p.val(ld), p.val(tr), p.val(ev), p.val(st)

        @block.sync
        def _(s):
            p.inc(ld, s.dma_start(out=ident[:, :], in_=p.io["consts"]["IDENT32"]), 16)
            for b in range(nb):
                if b > 0:
                    s.wait_ge(p.s(tr)[0], tr0 + 128 * b)
                p.inc(ld, s.dma_start(
                    out=tin[:, :],
                    in_=src_td[b].rearrange("(tt p) d -> p tt d", p=128)), 16)

        @block.tensor
        def _(t):
            k = 0
            for b in range(nb):
                t.wait_ge(p.s(ld)[0], ld0 + 16 * (b + 2))
                for tt in range(16):
                    for dg in range(NG):
                        bank = banks[k % 2]
                        if k >= 2:
                            t.wait_ge(p.s(ev)[0], ev0 + k - 1)
                        ins = t.transpose(bank[:, :], tin[:, tt, dg * 128:(dg + 1) * 128],
                                          ident[:, :])
                        p.inc(tr, ins, 1)
                        k += 1

        @block.vector
        def _(v):
            k = 0
            for b in range(nb):
                if b > 0:
                    v.wait_ge(p.s(st)[0], st0 + 16 * b)
                for tt in range(16):
                    for dg in range(NG):
                        bank = banks[k % 2]
                        v.wait_ge(p.s(tr)[0], tr0 + k + 1)
                        ins = v.tensor_copy(tout[:, dg, tt * 128:(tt + 1) * 128], bank[:, :])
                        p.inc(ev, ins, 1)
                        k += 1

        @block.scalar
        def _(s):
            for b in range(nb):
                s.wait_ge(p.s(ev)[0], ev0 + 128 * (b + 1))
                p.inc(st, s.dma_start(
                    out=dst_fm[b].rearrange("(dg p) t -> p dg t", p=128),
                    in_=tout[:, :]), 16)
            s.wait_ge(p.s(st)[0], p.val(st))


def emit_tr_fm2td(p, src_fm, dst_td, nb, tag):
    nc = p.nc
    with (
        nc.sbuf_tensor(uniq(f"{tag}_in"), [128, NG, T], F32) as tin,
        nc.sbuf_tensor(uniq(f"{tag}_out"), [128, 16, D], F32) as tout,
        nc.sbuf_tensor(uniq(f"{tag}_id"), [128, 128], F32) as ident,
        nc.psum_tensor(uniq(f"{tag}_p0"), [128, 128], F32) as p0,
        nc.psum_tensor(uniq(f"{tag}_p1"), [128, 128], F32) as p1,
        nc.Block() as block,
    ):
        banks = [p0, p1]
        ld, tr, ev, st = f"{tag}ld", f"{tag}tr", f"{tag}ev", f"{tag}st"
        ld0, tr0, ev0, st0 = p.val(ld), p.val(tr), p.val(ev), p.val(st)

        @block.sync
        def _(s):
            p.inc(ld, s.dma_start(out=ident[:, :], in_=p.io["consts"]["IDENT32"]), 16)
            for b in range(nb):
                if b > 0:
                    s.wait_ge(p.s(tr)[0], tr0 + 128 * b)
                p.inc(ld, s.dma_start(
                    out=tin[:, :],
                    in_=src_fm[b].rearrange("(dg p) t -> p dg t", p=128)), 16)

        @block.tensor
        def _(t):
            k = 0
            for b in range(nb):
                t.wait_ge(p.s(ld)[0], ld0 + 16 * (b + 2))
                for dg in range(NG):
                    for tt in range(16):
                        bank = banks[k % 2]
                        if k >= 2:
                            t.wait_ge(p.s(ev)[0], ev0 + k - 1)
                        ins = t.transpose(bank[:, :], tin[:, dg, tt * 128:(tt + 1) * 128],
                                          ident[:, :])
                        p.inc(tr, ins, 1)
                        k += 1

        @block.vector
        def _(v):
            k = 0
            for b in range(nb):
                if b > 0:
                    v.wait_ge(p.s(st)[0], st0 + 16 * b)
                for dg in range(NG):
                    for tt in range(16):
                        bank = banks[k % 2]
                        v.wait_ge(p.s(tr)[0], tr0 + k + 1)
                        ins = v.tensor_copy(tout[:, tt, dg * 128:(dg + 1) * 128], bank[:, :])
                        p.inc(ev, ins, 1)
                        k += 1

        @block.scalar
        def _(s):
            for b in range(nb):
                s.wait_ge(p.s(ev)[0], ev0 + 128 * (b + 1))
                p.inc(st, s.dma_start(
                    out=dst_td[b].rearrange("(tt p) d -> p tt d", p=128),
                    in_=tout[:, :]), 16)
            s.wait_ge(p.s(st)[0], p.val(st))


def emit_ln_chunk(p, block, tag, xb, xnc, sc, pb, xn_guard=None):
    nc = p.nc

    @block.tensor
    def _(t):
        p.wait(t, tag + "gxb")
        p.wait(t, tag + "cld")
        p.wait(t, tag + "stat")
        for dg in range(NG):
            ins = t.matmul(pb["lnsum"][:, :], sc["ones16"][:, :], xb[:, dg, :],
                           start=(dg == 0), stop=(dg == NG - 1))
        p.inc(tag + "lnmm", ins, 1)

    @block.vector
    def _(v):
        p.wait(v, tag + "gxb")
        p.wait(v, tag + "lnmm")
        p.inc(tag + "sq", v.tensor_mul(sc["xsq"][:, :, :], xb[:, :, :], xb[:, :, :]), 1)

    @block.tensor
    def _(t):
        p.wait(t, tag + "sq")
        for dg in range(NG):
            ins = t.matmul(pb["lnsq"][:, :], sc["ones16"][:, :], sc["xsq"][:, dg, :],
                           start=(dg == 0), stop=(dg == NG - 1))
        p.inc(tag + "lnmm", ins, 1)

    @block.vector
    def _(v):
        p.wait(v, tag + "lnmm")
        st = sc["st"]
        A, B, C = st[0:1, 0:512], st[0:1, 512:1024], st[0:1, 1024:1536]
        p.inc(tag + "vch", v.tensor_scalar(A, pb["lnsum"][0:1, :], 1.0 / D, None, ALU.mult))
        p.wait(v, tag + "vch")
        p.inc(tag + "vch", v.tensor_scalar(B, pb["lnsq"][0:1, :], 1.0 / D, None, ALU.mult))
        p.wait(v, tag + "vch")
        p.inc(tag + "vch", v.tensor_mul(C, A, A))
        p.wait(v, tag + "vch")
        p.inc(tag + "vch", v.tensor_sub(B, B, C))
        p.wait(v, tag + "vch")
        p.inc(tag + "stv", v.tensor_scalar(B, B, 1e-5, None, ALU.add))

    @block.scalar
    def _(s):
        p.wait(s, tag + "stv")
        B = sc["st"][0:1, 512:1024]
        p.inc(tag + "sch", s.activation(B, B, AF.Ln))
        p.wait(s, tag + "sch")
        p.inc(tag + "sts", s.activation(B, B, AF.Exp, scale=-0.5))

    @block.vector
    def _(v):
        p.wait(v, tag + "sts")
        p.wait(v, tag + "bcm")
        A, B, C = sc["st"][0:1, 0:512], sc["st"][0:1, 512:1024], sc["st"][0:1, 1024:1536]
        p.inc(tag + "vch", v.tensor_copy(sc["a16"][0:1, :], B))
        p.inc(tag + "vch", v.tensor_mul(C, A, B))
        p.wait(v, tag + "vch")
        p.inc(tag + "stat", v.tensor_scalar(sc["b16"][0:1, :], C, -1.0, None, ALU.mult))

    @block.tensor
    def _(t):
        p.wait(t, tag + "stat")
        p.wait(t, tag + "bce")
        t.matmul(pb["psA"][:, :], sc["onesrow16"][:, :], sc["a16"][0:1, :], start=True, stop=True)
        ins = t.matmul(pb["psB"][:, :], sc["onesrow16"][:, :], sc["b16"][0:1, :], start=True, stop=True)
        p.inc(tag + "bcm", ins, 1)

    @block.vector
    def _(v):
        p.wait(v, tag + "bcm")
        v.tensor_copy(sc["A16"][:, :], pb["psA"][:, :])
        p.inc(tag + "bce", v.tensor_copy(sc["B16"][:, :], pb["psB"][:, :]), 1)
        if xn_guard is not None:
            nm, lag = xn_guard
            p.wait(v, nm, max(0, p.val(nm) - lag))
        p.wait(v, tag + "bce")
        a_bc = sc["A16"][:, :].unsqueeze(1).broadcast_to([128, NG, 512])
        b_bc = sc["B16"][:, :].unsqueeze(1).broadcast_to([128, NG, 512])
        p.inc(tag + "vch", v.tensor_tensor(xnc[:, :, :], xb[:, :, :], a_bc, ALU.mult))
        p.wait(v, tag + "vch")
        p.inc(tag + "xn", v.tensor_tensor(xnc[:, :, :], xnc[:, :, :], b_bc, ALU.add), 1)


def alloc_ln_scratch(p, stack, tag):
    nc = p.nc
    sc = {}
    sc["xsq"] = stack.enter_context(nc.sbuf_tensor(uniq(tag + "xsq"), [128, NG, 512], BF16))
    sc["st"] = stack.enter_context(nc.sbuf_tensor(uniq(tag + "st"), [1, 1536], F32))
    sc["a16"] = stack.enter_context(nc.sbuf_tensor(uniq(tag + "a16"), [1, 512], BF16))
    sc["b16"] = stack.enter_context(nc.sbuf_tensor(uniq(tag + "b16"), [1, 512], BF16))
    sc["A16"] = stack.enter_context(nc.sbuf_tensor(uniq(tag + "A16"), [128, 512], BF16))
    sc["B16"] = stack.enter_context(nc.sbuf_tensor(uniq(tag + "B16"), [128, 512], BF16))
    return sc


def emit_ln_consts(p, block, sc, tag):
    @block.sync
    def _(s):
        p.inc(tag + "cld", s.dma_start(out=sc["ones16"][:, :], in_=p.io["consts"]["ONESCOL16"]), 16)
        p.inc(tag + "cld", s.dma_start(out=sc["onesrow16"][:, :], in_=p.io["consts"]["ONESROW16"]), 16)


def emit_l0attn(p, x_in, x_out, nb):
    nc = p.nc
    cs = p.io["consts"]
    from contextlib import ExitStack
    with ExitStack() as es:
        def sb(nm, shp, dt):
            return es.enter_context(nc.sbuf_tensor(uniq(nm), shp, dt))
        Wo = sb("a_wo", [128, NG, D], BF16)
        bo = sb("a_bo", [128, NG], F32)
        Qb = sb("a_q", [128, NG, T], BF16)
        Kb = sb("a_k", [128, NG, T], BF16)
        Vb = sb("a_v", [128, 16, D], BF16)
        ones16 = sb("a_ones", [128, 1], BF16)
        onesrow16 = sb("a_onesr", [1, 128], BF16)
        onesmat = sb("a_onesm", [128, 128], BF16)
        block = es.enter_context(nc.Block())

        @block.sync
        def _(s):
            p.inc("acld", s.dma_start(out=ones16[:, :], in_=cs["ONESCOL16"]), 16)
            p.inc("acld", s.dma_start(out=onesrow16[:, :], in_=cs["ONESROW16"]), 16)
            p.inc("acld", s.dma_start(out=onesmat[:, :], in_=cs["ONESMAT16"]), 16)
            p.inc("awld", s.dma_start(out=bo[:, :], in_=p.io["bo_fm"]), 16)

        @block.gpsimd
        def _(g):
            p.inc("agwo", g.dma_start(
                out=Wo[:, :, :],
                in_=p.io["L0_attn_Wo"].rearrange("(kg p) m -> p kg m", p=128)), 16)

        for b in range(nb):
            _l0attn_batch(p, block, b, x_in, x_out, Wo, bo,
                          Qb, Kb, Vb, ones16, onesrow16, onesmat)

        @block.sync
        def _(s):
            p.wait(s, "aost")


def _l0attn_batch(p, block, b, x_in, x_out, Wo, bo,
                  Qb, Kb, Vb, ones16, onesrow16, onesmat):
    nc = p.nc
    from contextlib import ExitStack
    with ExitStack() as es:
        def sb(nm, shp, dt):
            return es.enter_context(nc.sbuf_tensor(uniq(nm), shp, dt))
        Wq = sb("a_wq", [128, NG, D], BF16)
        Wk = sb("a_wk", [128, NG, D], BF16)
        Wv = sb("a_wv", [128, NG, D], BF16)
        xb1 = sb("a_xb", [128, NG, 512], BF16)
        xn2 = sb("a_xn", [128, 2, NG, 512], BF16)
        sc = alloc_ln_scratch(p, es, "a_ln")
        sc["ones16"] = ones16
        sc["onesrow16"] = onesrow16
        lnsum = es.enter_context(nc.psum_tensor(uniq("a_lnsum"), [1, 512], F32))
        lnsq = es.enter_context(nc.psum_tensor(uniq("a_lnsq"), [1, 512], F32))
        psA = es.enter_context(nc.psum_tensor(uniq("a_psA"), [128, 512], F32))
        psB = es.enter_context(nc.psum_tensor(uniq("a_psB"), [128, 512], F32))
        pb = {"lnsum": lnsum, "lnsq": lnsq, "psA": psA, "psB": psB}
        pj = [es.enter_context(nc.psum_tensor(uniq(f"a_pj{i}"), [128, 512], F32))
              for i in range(3)]

        @block.gpsimd
        def _(g):
            if b > 0:
                p.wait(g, "aost")
            p.inc("awld2", g.dma_start(
                out=Wq[:, :, :], in_=p.io["L0_attn_Wq"].rearrange("(kg p) m -> p kg m", p=128)), 16)
            p.inc("awld2", g.dma_start(
                out=Wk[:, :, :], in_=p.io["L0_attn_Wk"].rearrange("(kg p) m -> p kg m", p=128)), 16)
            p.inc("awld2", g.dma_start(
                out=Wv[:, :, :], in_=p.io["L0_attn_Wv"].rearrange("(kg p) m -> p kg m", p=128)), 16)

        awld2_base = p.val("awld2")
        for tc in range(4):
            xb = xb1
            xnc = xn2[:, tc % 2]

            @block.gpsimd
            def _(g, tc=tc, xb=xb):
                if b > 0 and tc == 0:
                    p.wait(g, "aost")
                if b > 0 or tc >= 1:
                    p.wait(g, "axn")
                p.inc("agxb", g.dma_start(
                    out=xb[:, :, :],
                    in_=x_in[b, :, tc * 512:(tc + 1) * 512].rearrange("(dg p) t -> p dg t", p=128)), 16)

            emit_ln_chunk(p, block, "a", xb, xnc, sc, pb, xn_guard=("apmm", 24))

            @block.tensor
            def _(t, tc=tc, xnc=xnc):
                base_ev = p.val("apev")
                p.wait(t, "axn")
                t.wait_ge(p.s("awld2")[0], awld2_base)
                p.wait(t, "acld")
                k = 0
                for (W, dst, kind) in [(Wq, Qb, "qk"), (Wk, Kb, "qk"), (Wv, Vb, "v")]:
                    for gidx in range(NG):
                        bank = pj[k % 3]
                        t.wait_ge(p.s("apev")[0], base_ev if k < 3 else base_ev + k - 2)
                        if kind == "qk":
                            for kg in range(NG):
                                ins = t.matmul(bank[:, :], W[:, kg, gidx * 128:(gidx + 1) * 128],
                                               xnc[:, kg, :], start=(kg == 0), stop=(kg == NG - 1))
                        else:
                            half, tt4 = gidx % 2, gidx // 2
                            for kg in range(NG):
                                ins = t.matmul(bank[:, :], xnc[:, kg, tt4 * 128:(tt4 + 1) * 128],
                                               W[:, kg, half * 512:(half + 1) * 512],
                                               start=(kg == 0), stop=(kg == NG - 1))
                        p.inc("apmm", ins, 1)
                        k += 1

            @block.scalar
            def _(s, tc=tc):
                base_mm = p.val("apmm") - 24
                if b > 0 and tc == 0:
                    p.wait(s, "aapv")
                k = 0
                for (W, dst, kind) in [(Wq, Qb, "qk"), (Wk, Kb, "qk"), (Wv, Vb, "v")]:
                    for gidx in range(NG):
                        bank = pj[k % 3]
                        s.wait_ge(p.s("apmm")[0], base_mm + k + 1)
                        if kind == "qk":
                            ins = s.copy(dst[:, gidx, tc * 512:(tc + 1) * 512], bank[:, :])
                        else:
                            half, tt4 = gidx % 2, gidx // 2
                            vt = tc * 4 + tt4
                            ins = s.copy(dst[:, vt, half * 512:(half + 1) * 512], bank[:, :])
                        p.inc("apev", ins, 1)
                        k += 1

    with ExitStack() as es:
        def sb(nm, shp, dt):
            return es.enter_context(nc.sbuf_tensor(uniq(nm), shp, dt))
        Cb = sb("a_ctx", [128, NG, T], BF16)
        PT2 = sb("a_pt", [128, 16, 512], BF16)
        MSK = sb("a_msk", [128, 4, 512], BF16)
        rr32 = sb("a_rr32", [128, 512], F32)

        @block.sync
        def _(s):
            p.wait(s, "apev")
            for d_ in range(4):
                p.inc("amld", s.dma_start(out=MSK[:, d_, :], in_=p.io["consts"][f"CMASK{d_*128}"]), 16)
        from contextlib import ExitStack as _ES
        es2 = _ES()
        sbank = [es2.enter_context(nc.psum_tensor(uniq(f"a_s{i}"), [128, 512], F32)) for i in range(4)]
        ctxP = es2.enter_context(nc.psum_tensor(uniq("a_ctxP"), [128, 512], F32))
        rsP = es2.enter_context(nc.psum_tensor(uniq("a_rsP"), [128, 512], F32))
        njts = [4, 8, 12, 16]
        cum = [0, 4, 12, 24]
        for h in range(NH):
            hp, hr = h // 2, (h % 2) * 64
            qs = Qb[hr:hr + 64, hp, :]
            ks = Kb[hr:hr + 64, hp, :]
            base_aex = p.val("aaex")
            base_apv = p.val("aapv")
            base_amk = p.val("aamk")
            base_amm = p.val("aamm")
            base_arc = p.val("aarc")
            base_acn = p.val("aacn")

            def nmask(lh):
                n = 0
                for c2 in range(4):
                    for j2 in range(njts[c2]):
                        if cum[c2] + j2 >= lh:
                            return n
                        if j2 >= 4 * c2:
                            n += 1
                return n

            @block.tensor
            def _(t, h=h, hr=hr, qs=qs, ks=ks, base_aex=base_aex, base_amk=base_amk,
                  base_arc=base_arc, base_acn=base_acn, nmask=nmask):
                if h == 0:
                    p.wait(t, "apev")
                for ci in range(4):
                    i0 = ci * 512
                    for jt in range(njts[ci]):
                        lh = cum[ci] + jt
                        t.wait_ge(p.s("aaex")[0], base_aex + max(0, lh - 3))
                        ins = t.matmul(sbank[lh % 4][:, :], ks[:, jt * 128:(jt + 1) * 128],
                                       qs[:, i0:i0 + 512], start=True, stop=True,
                                       tile_position=(hr, 0))
                        p.inc("aamm", ins, 1)
                    t.wait_ge(p.s("aarc")[0], base_arc + ci)
                    for jt in range(njts[ci]):
                        lh = cum[ci] + jt
                        if jt >= 4 * ci:
                            t.wait_ge(p.s("aamk")[0], base_amk + nmask(lh) + 1)
                        else:
                            t.wait_ge(p.s("aaex")[0], base_aex + lh + 1)
                        t.matmul(rsP[:, :], onesmat[:, :], PT2[:, lh % 16, :],
                                 start=(jt == 0), stop=(jt == njts[ci] - 1))
                    t.wait_ge(p.s("aacn")[0], base_acn + ci)
                    for jt in range(njts[ci]):
                        lh = cum[ci] + jt
                        ins = t.matmul(ctxP[hr:hr + 64, :], Vb[:, jt, h * 64:(h + 1) * 64],
                                       PT2[:, lh % 16, :],
                                       start=(jt == 0), stop=(jt == njts[ci] - 1),
                                       tile_position=(0, hr))
                        p.inc("aapv", ins, 1)

            @block.scalar
            def _(s, h=h, base_amm=base_amm, base_apv=base_apv):
                for lh in range(40):
                    s.wait_ge(p.s("aamm")[0], base_amm + lh + 1)
                    if lh >= 2:
                        s.wait_ge(p.s("aapv")[0], base_apv + max(0, lh - 15))
                    ins = s.activation(PT2[:, lh % 16, :], sbank[lh % 4][:, :], AF.Exp, scale=0.125)
                    p.inc("aaex", ins, 1)

            @block.vector
            def _(v, h=h, hp=hp, hr=hr, base_aex=base_aex, base_apv=base_apv):
                if h == 0:
                    p.wait(v, "amld")
                for ci in range(4):
                    for jt in range(njts[ci]):
                        lh = cum[ci] + jt
                        if jt >= 4 * ci:
                            v.wait_ge(p.s("aaex")[0], base_aex + lh + 1)
                            d_ = jt - 4 * ci
                            ins = v.tensor_mul(PT2[:, lh % 16, :], PT2[:, lh % 16, :], MSK[:, d_, :])
                            p.inc("aamk", ins, 1)
                    v.wait_ge(p.s("aapv")[0], base_apv + cum[ci] + njts[ci])
                    p.inc("aarc", v.reciprocal(rr32[:, :], rsP[:, :]), 1)
                    if b > 0 and h == 0 and ci == 0:
                        p.wait(v, "aomm")
                    p.wait(v, "aarc")
                    p.inc("aacn", v.tensor_mul(Cb[hr:hr + 64, hp, ci * 512:(ci + 1) * 512],
                                               ctxP[hr:hr + 64, :], rr32[hr:hr + 64, :]), 1)

        es2.close()
        xc = sb("a_xc", [128, NG, 512], F32)
        ot = sb("a_ot", [128, NG, 512], F32)
        ov2 = sb("a_ov", [128, 2, 512], F32)
        po0 = es.enter_context(nc.psum_tensor(uniq("a_po0"), [128, 512], F32))
        po1 = es.enter_context(nc.psum_tensor(uniq("a_po1"), [128, 512], F32))
        po = [po0, po1]
        for tc in range(4):
            @block.sync
            def _(s, tc=tc):
                if tc > 0:
                    p.wait(s, "aoev")
                else:
                    p.wait(s, "aacn")
                p.inc("aoxc", s.dma_start(
                    out=xc[:, :, :],
                    in_=x_in[b, :, tc * 512:(tc + 1) * 512].rearrange("(dg p) t -> p dg t", p=128)), 16)

            @block.tensor
            def _(t, tc=tc):
                base_oev = p.val("aoev")
                p.wait(t, "aacn")
                p.wait(t, "awld")
                p.wait(t, "agwo")
                for mg in range(NG):
                    t.wait_ge(p.s("aoev")[0], base_oev if mg < 2 else base_oev + mg - 1)
                    for kg in range(NG):
                        ins = t.matmul(po[mg % 2][:, :], Wo[:, kg, mg * 128:(mg + 1) * 128],
                                       Cb[:, kg, tc * 512:(tc + 1) * 512],
                                       start=(kg == 0), stop=(kg == NG - 1))
                    p.inc("aomm", ins, 1)

            @block.vector
            def _(v, tc=tc):
                base_omm = p.val("aomm") - NG
                p.wait(v, "aoxc")
                if tc > 0:
                    p.wait(v, "aost")
                for mg in range(NG):
                    v.wait_ge(p.s("aomm")[0], base_omm + mg + 1)
                    p.inc("avch", v.tensor_scalar(ov2[:, mg % 2, :], po[mg % 2][:, :],
                                                  bo[:, mg:mg + 1], None, ALU.add))
                    p.wait(v, "avch")
                    p.inc("aoev", v.tensor_add(ot[:, mg, :], ov2[:, mg % 2, :], xc[:, mg, :]), 1)

            @block.scalar
            def _(s, tc=tc):
                p.wait(s, "aoev")
                p.inc("aost", s.dma_start(
                    out=x_out[b, :, tc * 512:(tc + 1) * 512].rearrange("(dg p) t -> p dg t", p=128),
                    in_=ot[:, :, :]), 16)


def emit_l0ffn(p, x_in, x_out, nb):
    nc = p.nc
    cs = p.io["consts"]
    from contextlib import ExitStack
    with ExitStack() as es:
        def sb(nm, shp, dt):
            return es.enter_context(nc.sbuf_tensor(uniq(nm), shp, dt))
        ones16 = sb("f_ones", [128, 1], BF16)
        onesrow16 = sb("f_onesr", [1, 128], BF16)
        onesrow32 = sb("f_onesr32", [1, 128], F32)
        onesmat = sb("f_onesm", [128, 128], BF16)
        ident16 = sb("f_id16", [128, 128], BF16)
        Wr = sb("f_wr", [128, NG, 8], BF16)
        tmp1 = sb("f_tmp1", [1, 1], F32)
        invt = sb("f_invt", [128, 1], F32)
        block = es.enter_context(nc.Block())

        @block.sync
        def _(s):
            p.inc("fcld", s.dma_start(out=ones16[:, :], in_=cs["ONESCOL16"]), 16)
            p.inc("fcld", s.dma_start(out=onesrow16[:, :], in_=cs["ONESROW16"]), 16)
            p.inc("fcld", s.dma_start(out=onesrow32[:, :], in_=cs["ONESROW32"]), 16)
            p.inc("fcld", s.dma_start(out=onesmat[:, :], in_=cs["ONESMAT16"]), 16)
            p.inc("fcld", s.dma_start(out=ident16[:, :], in_=cs["IDENT16"]), 16)
            p.inc("fcld", s.dma_start(out=tmp1[:, :], in_=p.io["L0_ffn_temperature"].rearrange("(one e) -> one e", one=1)), 16)

        @block.gpsimd
        def _(g):
            p.inc("gwr", g.dma_start(
                out=Wr[:, :, :], in_=p.io["L0_ffn_Wr"].rearrange("(kg p) e -> p kg e", p=128)), 16)

        with nc.psum_tensor(uniq("f_pt"), [128, 1], F32) as ptb:
            @block.tensor
            def _(t):
                p.wait(t, "fcld")
                p.inc("ftm", t.matmul(ptb[:, :], onesrow32[:, :], tmp1[:, :], start=True, stop=True), 1)

            @block.vector
            def _(v):
                p.wait(v, "ftm")
                p.inc("fvch", v.reciprocal(invt[:, :], ptb[:, :]))
                p.wait(v, "fvch")
                v.nop()

        for b in range(nb):
            _l0ffn_batch(p, block, b, x_in, x_out, ones16, onesrow16, onesmat,
                         ident16, Wr, invt)

        @block.sync
        def _(s):
            p.wait(s, "fst")


def _l0ffn_batch(p, block, b, x_in, x_out, ones16, onesrow16, onesmat, ident16, Wr, invt):
    nc = p.nc
    from contextlib import ExitStack
    with ExitStack() as es0:
        def sb0(nm, shp, dt):
            return es0.enter_context(nc.sbuf_tensor(uniq(nm), shp, dt))
        xn = sb0("f_xn", [128, NG, T], BF16)
        acc = sb0("f_acc", [128, NG, T], F32)
        w_all = sb0("f_wall", [128, 16, 8], F32)

        with ExitStack() as es:
            def sb(nm, shp, dt):
                return es.enter_context(nc.sbuf_tensor(uniq(nm), shp, dt))
            xb = sb("f_xb", [128, NG, 512], BF16)
            sc = alloc_ln_scratch(p, es, "f_ln")
            sc["ones16"] = ones16
            sc["onesrow16"] = onesrow16
            lnsum = es.enter_context(nc.psum_tensor(uniq("f_lnsum"), [1, 512], F32))
            lnsq = es.enter_context(nc.psum_tensor(uniq("f_lnsq"), [1, 512], F32))
            psA = es.enter_context(nc.psum_tensor(uniq("f_psA"), [128, 512], F32))
            psB = es.enter_context(nc.psum_tensor(uniq("f_psB"), [128, 512], F32))
            pb = {"lnsum": lnsum, "lnsq": lnsq, "psA": psA, "psB": psB}
            if b == 0:
                @block.tensor
                def _(t):
                    p.wait(t, "fvch")

            for tc in range(4):
                xnc = xn[:, :, tc * 512:(tc + 1) * 512]

                @block.gpsimd
                def _(g, tc=tc):
                    if b > 0 and tc == 0:
                        p.wait(g, "fst")
                    if tc >= 1:
                        p.wait(g, "fxn")
                    p.inc("fgxb", g.dma_start(
                        out=xb[:, :, :],
                        in_=x_in[b, :, tc * 512:(tc + 1) * 512].rearrange("(dg p) t -> p dg t", p=128)), 16)

                emit_ln_chunk(p, block, "f", xb, xnc, sc, pb, xn_guard=None)

            @block.gpsimd
            def _(g):
                if b > 0:
                    p.wait(g, "fst")
                p.inc("fmz", g.memset(acc[:, :, :], 0.0), 1)

        with (
            nc.psum_tensor(uniq("f_rz"), [128, 128], F32) as rz,
            nc.psum_tensor(uniq("f_rw"), [8, 128], F32) as rw_unused,
        ):
            with ExitStack() as es:
                rt = es.enter_context(nc.sbuf_tensor(uniq("f_rt"), [128, 16, 8], F32))
                r1 = es.enter_context(nc.sbuf_tensor(uniq("f_r1"), [128, 16, 8], F32))
                r2 = es.enter_context(nc.sbuf_tensor(uniq("f_r2"), [128, 16, 8], F32))
                r3 = es.enter_context(nc.sbuf_tensor(uniq("f_r3"), [128, 16, 8], F32))
                q1 = es.enter_context(nc.sbuf_tensor(uniq("f_q1"), [128, 16], F32))
                q2 = es.enter_context(nc.sbuf_tensor(uniq("f_q2"), [128, 16], F32))

                @block.tensor
                def _(t):
                    p.wait(t, "fxn")
                    p.wait(t, "gwr")
                    for tt in range(16):
                        for kg in range(NG):
                            ins = t.matmul(rz[:, tt * 8:(tt + 1) * 8],
                                           xn[:, kg, tt * 128:(tt + 1) * 128],
                                           Wr[:, kg, :], start=(kg == 0), stop=(kg == NG - 1))
                    p.inc("frz", ins, 1)

                @block.scalar
                def _(s):
                    p.wait(s, "frz")
                    p.inc("frsig", s.activation(rt[:, :, :], rz[:, :], AF.Sigmoid, scale=invt[:, :]), 1)

                @block.vector
                def _(v):
                    p.wait(v, "frsig")
                    p.inc("fvch", v.tensor_scalar(r1[:, :, :], rt[:, :, :], -0.3678794503211975, None, ALU.add))
                    p.wait(v, "fvch")
                    p.inc("fgc", v.tensor_scalar(r1[:, :, :], r1[:, :, :], --9.149754731652138e-09, None, ALU.add), 1)

                @block.scalar
                def _(s):
                    p.wait(s, "fgc")
                    p.inc("fabs", s.activation(r1[:, :, :], r1[:, :, :], AF.Abs), 1)

                @block.vector
                def _(v):
                    p.wait(v, "frsig")
                    p.inc("fvch", v.tensor_scalar(r2[:, :, :], rt[:, :, :], GOLDEN_LOWER, None, ALU.is_ge))
                    p.wait(v, "fvch")
                    p.inc("fvch", v.tensor_scalar(r3[:, :, :], rt[:, :, :], GOLDEN_UPPER, None, ALU.is_le))
                    p.wait(v, "fvch")
                    p.inc("frd", v.tensor_mul(r2[:, :, :], r2[:, :, :], r3[:, :, :]), 1)

                @block.scalar
                def _(s):
                    p.wait(s, "frd")
                    p.wait(s, "fabs")
                    p.inc("fsch", s.activation(rt[:, :, :], r1[:, :, :], AF.Exp, scale=-10.0))
                    p.wait(s, "fsch")
                    p.inc("fre", s.activation(r3[:, :, :], r1[:, :, :], AF.Exp, scale=-1.0 / 0.3), 1)

                @block.vector
                def _(v):
                    p.wait(v, "fre")
                    p.inc("fvch", v.tensor_mul(r1[:, :, :], rt[:, :, :], r2[:, :, :]))
                    p.wait(v, "fvch")
                    p.inc("fvch", v.tensor_reduce(q1[:, :], r1[:, :, :], AX.X, ALU.add))
                    p.wait(v, "fvch")
                    p.inc("fvch", v.tensor_scalar(q1[:, :], q1[:, :], 1e-8, None, ALU.is_lt))
                    p.wait(v, "fvch")
                    p.inc("fvch", v.tensor_reduce(q2[:, :], r3[:, :, :], AX.X, ALU.max))
                    p.wait(v, "fvch")
                    p.inc("fvch", v.tensor_tensor(r2[:, :, :], r3[:, :, :],
                                                  q2[:, :].unsqueeze(2).broadcast_to([128, 16, 8]), ALU.is_ge))
                    p.wait(v, "fvch")
                    p.inc("fvch", v.scalar_tensor_tensor(r2[:, :, :], r2[:, :, :], -1e30, r3[:, :, :],
                                                         ALU.mult, ALU.add))
                    p.wait(v, "fvch")
                    p.inc("fvch", v.tensor_reduce(q2[:, :], r2[:, :, :], AX.X, ALU.max))
                    p.wait(v, "fvch")
                    p.inc("fvch", v.tensor_tensor(r2[:, :, :], r3[:, :, :],
                                                  q2[:, :].unsqueeze(2).broadcast_to([128, 16, 8]), ALU.is_ge))
                    p.wait(v, "fvch")
                    p.inc("fvch", v.tensor_mul(r3[:, :, :], r3[:, :, :], r2[:, :, :]))
                    p.wait(v, "fvch")
                    p.inc("fvch", v.tensor_reduce(q2[:, :], r3[:, :, :], AX.X, ALU.add))
                    p.wait(v, "fvch")
                    p.inc("fvch", v.tensor_scalar(q2[:, :], q2[:, :], 1e-8, None, ALU.max))
                    p.wait(v, "fvch")
                    p.inc("fvch", v.reciprocal(q2[:, :], q2[:, :]))
                    p.wait(v, "fvch")
                    p.inc("fvch", v.tensor_tensor(r3[:, :, :], r3[:, :, :],
                                                  q2[:, :].unsqueeze(2).broadcast_to([128, 16, 8]), ALU.mult))
                    p.wait(v, "fvch")
                    p.inc("fvch", v.tensor_sub(r3[:, :, :], r3[:, :, :], r1[:, :, :]))
                    p.wait(v, "fvch")
                    p.inc("fvch", v.tensor_tensor(r3[:, :, :], r3[:, :, :],
                                                  q1[:, :].unsqueeze(2).broadcast_to([128, 16, 8]), ALU.mult))
                    p.wait(v, "fvch")
                    p.inc("fvch", v.tensor_add(r1[:, :, :], r1[:, :, :], r3[:, :, :]))
                    p.wait(v, "fvch")
                    p.inc("fvch", v.tensor_reduce(q2[:, :], r1[:, :, :], AX.X, ALU.add))
                    p.wait(v, "fvch")
                    p.inc("fvch", v.tensor_scalar(q2[:, :], q2[:, :], 1e-8, None, ALU.max))
                    p.wait(v, "fvch")
                    p.inc("fvch", v.reciprocal(q2[:, :], q2[:, :]))
                    p.wait(v, "fvch")
                    p.inc("frw", v.tensor_tensor(w_all[:, :, :], r1[:, :, :],
                                                 q2[:, :].unsqueeze(2).broadcast_to([128, 16, 8]), ALU.mult), 1)

        with ExitStack() as es:
            def sb(nm, shp, dt):
                return es.enter_context(nc.sbuf_tensor(uniq(nm), shp, dt))
            Wg = sb("f_wg", [128, NG, D], BF16)
            Wu = sb("f_wu", [128, NG, D], BF16)
            Wd = sb("f_wd", [128, NG, D], BF16)
            h_all = sb("f_h", [128, NG, T], BF16)
            tsil = sb("f_tsil", [128, 2, 512], F32)
            h1t = sb("f_h1", [128, 2, 512], F32)
            W16 = sb("f_w16", [128, 2, 512], BF16)
            Dt = sb("f_dt", [128, 4, 128], BF16)
            pg = [es.enter_context(nc.psum_tensor(uniq(f"f_pg{i}"), [128, 512], F32)) for i in range(2)]
            pu = [es.enter_context(nc.psum_tensor(uniq(f"f_pu{i}"), [128, 512], F32)) for i in range(2)]
            po = [es.enter_context(nc.psum_tensor(uniq(f"f_po{i}"), [128, 512], F32)) for i in range(3)]
            psW = es.enter_context(nc.psum_tensor(uniq("f_psW"), [128, 512], F32))

            for e in range(8):
                @block.gpsimd
                def _(g, e=e):
                    if e == 0:
                        p.wait(g, "fxn")
                        p.wait(g, "frw")
                    if e > 0 or b > 0:
                        p.wait(g, "fgm", p.val("fgm"))
                        p.wait(g, "fum", p.val("fum"))
                    p.inc("gwg", g.dma_start(
                        out=Wg[:, :, :],
                        in_=p.io["L0_ffn_Wg"][e].rearrange("(kg p) i -> p kg i", p=128)), 16)
                    p.inc("gwu", g.dma_start(
                        out=Wu[:, :, :],
                        in_=p.io["L0_ffn_Wu"][e].rearrange("(kg p) i -> p kg i", p=128)), 16)
                    if e > 0 or b > 0:
                        p.wait(g, "fdm", p.val("fdm"))
                    p.inc("gwd", g.dma_start(
                        out=Wd[:, :, :],
                        in_=p.io["L0_ffn_Wd"][e].rearrange("(ig p) d -> p ig d", p=128)), 16)

                base_gwg = p.val("gwg")
                base_gwu = p.val("gwu")
                base_gwd = p.val("gwd")

                for tp in range(2):
                    for sub in range(2):
                        @block.vector
                        def _(v, e=e, tp=tp, sub=sub):
                            if e == 0 and tp == 0 and sub == 0:
                                p.wait(v, "frw")
                            p.wait(v, "fbcw", p.val("fbcw"))
                            for tt4 in range(4):
                                tt = (2 * tp + sub) * 4 + tt4
                                p.inc("fD", v.tensor_scalar(Dt[:, tt4, :], ident16[:, :],
                                                            w_all[:, tt, e:e + 1], None, ALU.mult), 1)

                        @block.tensor
                        def _(t, e=e, tp=tp, sub=sub):
                            p.wait(t, "fD")
                            p.wait(t, "fw16", p.val("fw16"))
                            for tt4 in range(4):
                                ins = t.matmul(psW[:, tt4 * 128:(tt4 + 1) * 128], onesmat[:, :],
                                               Dt[:, tt4, :], start=True, stop=True)
                            p.inc("fbcw", ins, 1)

                        @block.vector
                        def _(v, e=e, tp=tp, sub=sub):
                            p.wait(v, "fbcw")
                            p.inc("fw16", v.tensor_copy(W16[:, sub, :], psW[:, :]), 1)

                    base_sil = p.val("fsil")
                    base_h1 = p.val("fh1")
                    base_gm = p.val("fgm")
                    base_um = p.val("fum")

                    @block.tensor
                    def _(t, e=e, tp=tp, base_sil=base_sil, base_h1=base_h1):
                        if e == 0 and tp == 0:
                            p.wait(t, "fxn")
                        if tp == 0:
                            t.wait_ge(p.s("gwg")[0], base_gwg)
                            t.wait_ge(p.s("gwu")[0], base_gwu)
                        for ig in range(NG):
                            t.wait_ge(p.s("fsil")[0], base_sil + 2 * ig)
                            for sub in range(2):
                                tcx = 2 * tp + sub
                                for kg in range(NG):
                                    ins = t.matmul(pg[sub][:, :], Wg[:, kg, ig * 128:(ig + 1) * 128],
                                                   xn[:, kg, tcx * 512:(tcx + 1) * 512],
                                                   start=(kg == 0), stop=(kg == NG - 1))
                                p.inc("fgm", ins, 1)
                            t.wait_ge(p.s("fh1")[0], base_h1 + 2 * ig)
                            for sub in range(2):
                                tcx = 2 * tp + sub
                                for kg in range(NG):
                                    ins = t.matmul(pu[sub][:, :], Wu[:, kg, ig * 128:(ig + 1) * 128],
                                                   xn[:, kg, tcx * 512:(tcx + 1) * 512],
                                                   start=(kg == 0), stop=(kg == NG - 1))
                                p.inc("fum", ins, 1)

                    @block.scalar
                    def _(s, e=e, tp=tp, base_sil=base_sil, base_h1=base_h1,
                          base_gm=base_gm):
                        for ig in range(NG):
                            for sub in range(2):
                                s.wait_ge(p.s("fgm")[0], base_gm + 2 * ig + sub + 1)
                                if ig >= 1:
                                    s.wait_ge(p.s("fh1")[0], base_h1 + 2 * (ig - 1) + sub + 1)
                                p.inc("fsil", s.activation(tsil[:, sub, :], pg[sub][:, :],
                                                           AF.Silu), 1)

                    @block.vector
                    def _(v, e=e, tp=tp, base_sil=base_sil, base_um=base_um):
                        p.wait(v, "fw16")
                        for ig in range(NG):
                            for sub in range(2):
                                v.wait_ge(p.s("fsil")[0], base_sil + 2 * ig + sub + 1)
                                v.wait_ge(p.s("fum")[0], base_um + 2 * ig + sub + 1)
                                p.inc("fh1", v.tensor_mul(h1t[:, sub, :], tsil[:, sub, :],
                                                          pu[sub][:, :]), 1)
                                p.wait(v, "fh1")
                                p.inc("fh", v.tensor_mul(
                                    h_all[:, ig, (2 * tp + sub) * 512:(2 * tp + sub + 1) * 512],
                                    h1t[:, sub, :], W16[:, sub, :]), 1)

                base_fh = p.val("fh") - 32

                @block.tensor
                def _(t, e=e):
                    base_dm = p.val("fdm")
                    t.wait_ge(p.s("gwd")[0], base_gwd)
                    g = 0
                    for tp in range(2):
                        t.wait_ge(p.s("fh")[0], base_fh + 16 * (tp + 1))
                        for dg in range(NG):
                            for sub in range(2):
                                tcx = 2 * tp + sub
                                G = base_dm + g
                                bank = po[G % 3]
                                t.wait_ge(p.s("fac")[0], max(0, G - 2))
                                for ig in range(NG):
                                    ins = t.matmul(bank[:, :], Wd[:, ig, dg * 128:(dg + 1) * 128],
                                                   h_all[:, ig, tcx * 512:(tcx + 1) * 512],
                                                   start=(ig == 0), stop=(ig == NG - 1))
                                p.inc("fdm", ins, 1)
                                g += 1

                @block.vector
                def _(v, e=e):
                    base_dm = p.val("fdm") - 32
                    if e == 0:
                        p.wait(v, "fmz")
                    g = 0
                    for tp in range(2):
                        for dg in range(NG):
                            for sub in range(2):
                                tcx = 2 * tp + sub
                                G = base_dm + g
                                v.wait_ge(p.s("fdm")[0], G + 1)
                                p.inc("fac", v.tensor_add(acc[:, dg, tcx * 512:(tcx + 1) * 512],
                                                          acc[:, dg, tcx * 512:(tcx + 1) * 512],
                                                          po[G % 3][:, :]), 1)
                                g += 1

        with ExitStack() as es:
            xc = es.enter_context(nc.sbuf_tensor(uniq("f_xc"), [128, NG, 512], F32))
            for tc in range(4):
                @block.sync
                def _(s, tc=tc):
                    if tc > 0:
                        p.wait(s, "fov")
                    else:
                        p.wait(s, "fac")
                    p.inc("fxc", s.dma_start(
                        out=xc[:, :, :],
                        in_=x_in[b, :, tc * 512:(tc + 1) * 512].rearrange("(dg p) t -> p dg t", p=128)), 16)

                @block.vector
                def _(v, tc=tc):
                    p.wait(v, "fxc")
                    p.wait(v, "fac")
                    if tc > 0:
                        p.wait(v, "fst")
                    p.inc("fov", v.tensor_add(acc[:, :, tc * 512:(tc + 1) * 512],
                                              acc[:, :, tc * 512:(tc + 1) * 512], xc[:, :, :]), 1)

                @block.scalar
                def _(s, tc=tc):
                    p.wait(s, "fov")
                    p.inc("fst", s.dma_start(
                        out=x_out[b, :, tc * 512:(tc + 1) * 512].rearrange("(dg p) t -> p dg t", p=128),
                        in_=acc[:, :, tc * 512:(tc + 1) * 512]), 16)


GTOK = 204

def emit_l1attn(p, x_in, x_out, nb):
    nc = p.nc
    cs = p.io["consts"]
    from contextlib import ExitStack
    with ExitStack() as es:
        def sb(nm, shp, dt):
            return es.enter_context(nc.sbuf_tensor(uniq(nm), shp, dt))
        Wo = sb("L_wo", [128, NG, D], BF16)
        Qb = sb("L_q", [128, NG, T], BF16)
        Kb = sb("L_k", [128, NG, T], BF16)
        Vb = sb("L_v", [128, 16, D], BF16)
        ID16 = sb("L_id16", [128, 128], BF16)
        ones16 = sb("L_ones", [128, 1], BF16)
        onesrow16 = sb("L_onesr", [1, 128], BF16)
        onesmat = sb("L_onesm", [128, 128], BF16)
        block = es.enter_context(nc.Block())

        @block.sync
        def _(s):
            p.inc("Lcld", s.dma_start(out=ones16[:, :], in_=cs["ONESCOL16"]), 16)
            p.inc("Lcld", s.dma_start(out=onesrow16[:, :], in_=cs["ONESROW16"]), 16)
            p.inc("Lcld", s.dma_start(out=onesmat[:, :], in_=cs["ONESMAT16"]), 16)
            p.inc("Lcld", s.dma_start(out=ID16[:, :], in_=cs["IDENT16"]), 16)

        @block.gpsimd
        def _(g):
            p.inc("Lgw", g.dma_start(
                out=Wo[:, :, :],
                in_=p.io["L1_attn_Wout"].rearrange("(kg p) m -> p kg m", p=128)), 16)

        for b in range(nb):
            _l1attn_batch(p, block, b, x_in, x_out, Wo, Qb, Kb, Vb,
                          ID16, ones16, onesrow16, onesmat)

        @block.sync
        def _(s):
            p.wait(s, "Lost")


def _l1attn_batch(p, block, b, x_in, x_out, Wo, Qb, Kb, Vb, ID16,
                  ones16, onesrow16, onesmat):
    nc = p.nc
    from contextlib import ExitStack
    with ExitStack() as es:
        def sb(nm, shp, dt):
            return es.enter_context(nc.sbuf_tensor(uniq(nm), shp, dt))
        Wq = sb("L_wq", [128, NG, D], BF16)
        Wk = sb("L_wk", [128, NG, D], BF16)
        Wv = sb("L_wv", [128, NG, D], BF16)
        xb1 = sb("L_xb", [128, NG, 512], BF16)
        xn1 = sb("L_xn", [128, NG, 512], BF16)
        qraw = sb("L_qraw", [128, 2, 512], BF16)
        rtmp = sb("L_rtmp", [128, 512], F32)
        C16 = sb("L_c16", [128, T], BF16)
        S16 = sb("L_s16", [128, T], BF16)
        SWP = sb("L_swp", [128, 128], BF16)
        sc = alloc_ln_scratch(p, es, "L_ln")
        sc["ones16"] = ones16
        sc["onesrow16"] = onesrow16
        lnsum = es.enter_context(nc.psum_tensor(uniq("L_lnsum"), [1, 512], F32))
        lnsq = es.enter_context(nc.psum_tensor(uniq("L_lnsq"), [1, 512], F32))
        psA = es.enter_context(nc.psum_tensor(uniq("L_psA"), [128, 512], F32))
        psB = es.enter_context(nc.psum_tensor(uniq("L_psB"), [128, 512], F32))
        psw = es.enter_context(nc.psum_tensor(uniq("L_psw"), [128, 512], F32))
        pb = {"lnsum": lnsum, "lnsq": lnsq, "psA": psA, "psB": psB}
        pj = [es.enter_context(nc.psum_tensor(uniq(f"L_pj{i}"), [128, 512], F32))
              for i in range(3)]

        @block.gpsimd
        def _(g):
            if b > 0:
                p.wait(g, "Lost")
            p.inc("Lgw2", g.dma_start(
                out=Wq[:, :, :], in_=p.io["L1_attn_Wq"].rearrange("(kg p) m -> p kg m", p=128)), 16)
            p.inc("Lgw2", g.dma_start(
                out=Wk[:, :, :], in_=p.io["L1_attn_Wk"].rearrange("(kg p) m -> p kg m", p=128)), 16)
            p.inc("Lgw2", g.dma_start(
                out=Wv[:, :, :], in_=p.io["L1_attn_Wv"].rearrange("(kg p) m -> p kg m", p=128)), 16)
            p.inc("Lgw2", g.dma_start(out=C16[:, :], in_=p.io["consts"]["ROPE_C"]), 16)
            p.inc("Lgw2", g.dma_start(out=S16[:, :], in_=p.io["consts"]["ROPE_S"]), 16)

        @block.sync
        def _(s):
            if b > 0:
                p.wait(s, "Lost")
            p.inc("Lsld", s.dma_start(out=SWP[:, :], in_=p.io["consts"]["SWAPMAT"]), 16)

        base_gw2 = p.val("Lgw2")
        for tc in range(4):
            xnc = xn1

            @block.gpsimd
            def _(g, tc=tc):
                if b > 0 and tc == 0:
                    p.wait(g, "Lost")
                if b > 0 or tc >= 1:
                    p.wait(g, "Lxn")
                p.inc("Lgxb", g.dma_start(
                    out=xb1[:, :, :],
                    in_=x_in[b, :, tc * 512:(tc + 1) * 512].rearrange("(dg p) t -> p dg t", p=128)), 16)

            emit_ln_chunk(p, block, "L", xb1, xnc, sc, pb, xn_guard=("Lpmm", 0))

            @block.tensor
            def _(t, tc=tc, xnc=xnc):
                base_ev = p.val("Lpev")
                base_sm = p.val("Lsm")
                p.wait(t, "Lxn")
                t.wait_ge(p.s("Lgw2")[0], base_gw2)
                p.wait(t, "Lcld")
                p.wait(t, "Lsld")
                k = 0
                for (W, kind) in [(Wq, "q"), (Wk, "k"), (Wv, "v")]:
                    for gidx in range(NG):
                        bank = pj[k % 3]
                        t.wait_ge(p.s("Lpev")[0], base_ev if k < 3 else base_ev + k - 2)
                        if kind in ("q", "k"):
                            for kg in range(NG):
                                ins = t.matmul(bank[:, :], W[:, kg, gidx * 128:(gidx + 1) * 128],
                                               xnc[:, kg, :], start=(kg == 0), stop=(kg == NG - 1))
                        else:
                            half, tt4 = gidx % 2, gidx // 2
                            for kg in range(NG):
                                ins = t.matmul(bank[:, :], xnc[:, kg, tt4 * 128:(tt4 + 1) * 128],
                                               W[:, kg, half * 512:(half + 1) * 512],
                                               start=(kg == 0), stop=(kg == NG - 1))
                        p.inc("Lpmm", ins, 1)
                        k += 1
                        if kind in ("q", "k"):
                            t.wait_ge(p.s("Lpev")[0], base_ev + k)
                            t.wait_ge(p.s("Lsm")[0], base_sm + k - 1)
                            ins = t.matmul(psw[:, :], SWP[:, :], qraw[:, (k - 1) % 2, :],
                                           start=True, stop=True)
                            p.inc("Lsw", ins, 1)

            @block.scalar
            def _(s, tc=tc):
                base_mm = p.val("Lpmm") - 24
                base_rp = p.val("Lrp")
                if b > 0 and tc == 0:
                    p.wait(s, "Lapv")
                k = 0
                for (W, kind) in [(Wq, "q"), (Wk, "k"), (Wv, "v")]:
                    for gidx in range(NG):
                        bank = pj[k % 3]
                        s.wait_ge(p.s("Lpmm")[0], base_mm + k + 1)
                        if kind in ("q", "k"):
                            kqk = k if k < 16 else None
                            if k >= 2 and kqk is not None:
                                s.wait_ge(p.s("Lrp")[0], base_rp + (k - 2) + 1)
                            ins = s.copy(qraw[:, k % 2, :], bank[:, :])
                        else:
                            half, tt4 = gidx % 2, gidx // 2
                            vt = tc * 4 + tt4
                            ins = s.copy(Vb[:, vt, half * 512:(half + 1) * 512], bank[:, :])
                        p.inc("Lpev", ins, 1)
                        k += 1

            @block.vector
            def _(v, tc=tc):
                base_sw = p.val("Lsw") - 16
                base_ev = p.val("Lpev") - 24
                sl = slice(tc * 512, (tc + 1) * 512)
                k = 0
                for (dst, kind) in [(Qb, "q"), (Kb, "k")]:
                    for gidx in range(NG):
                        v.wait_ge(p.s("Lpev")[0], base_ev + k + 1)
                        v.wait_ge(p.s("Lsw")[0], base_sw + k + 1)
                        p.wait(v, "Lrp")
                        p.inc("Lsm", v.tensor_mul(rtmp[:, :], psw[:, :], S16[:, sl]), 1)
                        p.wait(v, "Lsm")
                        p.inc("Lvch", v.tensor_tensor(qraw[:, k % 2, :], qraw[:, k % 2, :],
                                                      C16[:, sl], ALU.mult))
                        p.wait(v, "Lvch")
                        p.inc("Lrp", v.tensor_tensor(dst[:, gidx, sl], qraw[:, k % 2, :],
                                                     rtmp[:, :], ALU.add), 1)
                        k += 1

    with ExitStack() as es:
        def sb(nm, shp, dt):
            return es.enter_context(nc.sbuf_tensor(uniq(nm), shp, dt))
        Cb = sb("L_ctx", [128, NG, T], BF16)
        PTg = sb("L_ptg", [128, 2, 256], BF16)
        KLS = sb("L_kls", [128, T], BF16)
        KLT = sb("L_klt", [128, 2, 15, 64], BF16)
        kv16 = sb("L_kv16", [128, 64], BF16)
        mx = sb("L_mx", [128, 1], F32)
        asum = sb("L_asum", [128, 1], F32)
        rr32 = sb("L_rr32", [128, 256], F32)
        from contextlib import ExitStack as _ES
        es2 = _ES()
        sA2 = [es2.enter_context(nc.psum_tensor(uniq(f"L_sA{i}"), [128, 256], F32)) for i in range(2)]
        rsP = es2.enter_context(nc.psum_tensor(uniq("L_rsP"), [128, 256], F32))
        ctxP = es2.enter_context(nc.psum_tensor(uniq("L_ctxP"), [128, 256], F32))
        ptr = [es2.enter_context(nc.psum_tensor(uniq(f"L_ptr{i}"), [128, 64], F32))
               for i in range(2)]
        kvP = es2.enter_context(nc.psum_tensor(uniq("L_kvP"), [128, 64], F32))
        loP = es2.enter_context(nc.psum_tensor(uniq("L_loP"), [128, 512], F32))

        jcnt = [128, GTOK - 128]
        lpending = [None]
        for h in range(NH):
            hp, hr = h // 2, (h % 2) * 64
            qs = Qb[hr:hr + 64, hp, :]
            ks = Kb[hr:hr + 64, hp, :]
            base_ex = p.val("Laex")
            base_pv = p.val("Lapv")
            base_mm2 = p.val("Lamm")

            @block.tensor
            def _(t, h=h, hr=hr, qs=qs, ks=ks):
                if h == 0:
                    p.wait(t, "Lpev")
                    p.wait(t, "Lrp")
                p.wait(t, "Lnrm")
                for jt in range(2):
                    t.wait_ge(p.s("Laex")[0], base_ex + max(0, jt - 1))
                    ins = t.matmul(sA2[jt][0:jcnt[jt], 0:GTOK], ks[:, jt * 128:jt * 128 + jcnt[jt]],
                                   qs[:, 0:GTOK], start=True, stop=True, tile_position=(hr, 0))
                    p.inc("Lamm", ins, 1)
                for jt in range(2):
                    t.wait_ge(p.s("Laex")[0], base_ex + jt + 1)
                    t.matmul(rsP[:, 0:GTOK], onesmat[0:jcnt[jt], :], PTg[0:jcnt[jt], jt, 0:GTOK],
                             start=(jt == 0), stop=(jt == 1))
                for jt in range(2):
                    ins = t.matmul(ctxP[hr:hr + 64, 0:GTOK],
                                   Vb[0:jcnt[jt], jt, h * 64:(h + 1) * 64],
                                   PTg[0:jcnt[jt], jt, 0:GTOK],
                                   start=(jt == 0), stop=(jt == 1), tile_position=(0, hr))
                    p.inc("Lapv", ins, 1)

            @block.scalar
            def _(s, h=h, base_mm2=base_mm2, base_pv=base_pv):
                for jt in range(2):
                    s.wait_ge(p.s("Lamm")[0], base_mm2 + jt + 1)
                    if h > 0:
                        s.wait_ge(p.s("Lapv")[0], base_pv)
                    ins = s.activation(PTg[0:jcnt[jt], jt, 0:GTOK], sA2[jt][0:jcnt[jt], 0:GTOK],
                                       AF.Exp, scale=0.125)
                    p.inc("Laex", ins, 1)

            @block.vector
            def _(v, h=h, hp=hp, hr=hr, base_pv=base_pv):
                v.wait_ge(p.s("Lapv")[0], base_pv + 2)
                p.inc("Lrc", v.reciprocal(rr32[:, 0:GTOK], rsP[:, 0:GTOK]), 1)
                if b > 0 and h == 0:
                    p.wait(v, "Lomm")
                p.wait(v, "Lrc")
                p.inc("Lnrm", v.tensor_mul(Cb[hr:hr + 64, hp, 0:GTOK],
                                           ctxP[hr:hr + 64, 0:GTOK], rr32[hr:hr + 64, 0:GTOK]), 1)

            lfn = lpending[0]
            if lfn is not None:
                lfn()
            lpending[0] = None

            if h % 2 == 0:
                @block.vector
                def _(v, h=h, hp=hp):
                    p.wait(v, "Lkt", p.val("Lkt"))
                    p.inc("Lvch", v.memset(KLS[:, 128:GTOK], 0.0))
                    p.wait(v, "Lvch")
                    p.inc("Lmx", v.tensor_reduce(mx[:, :], Kb[:, hp, GTOK:T],
                                                 AX.X, ALU.max, negate=True), 1)

                @block.scalar
                def _(s, h=h, hp=hp):
                    p.wait(s, "Lmx")
                    ins = s.activation(KLS[:, GTOK:T], Kb[:, hp, GTOK:T],
                                       AF.Exp, bias=mx[:, :], accum_out=asum[:, :])
                    p.inc("Lke", ins, 1)

                @block.vector
                def _(v, h=h):
                    p.wait(v, "Lke")
                    p.inc("Lvch", v.reciprocal(asum[:, :], asum[:, :]))
                    p.wait(v, "Lvch")
                    p.inc("Lkn", v.tensor_scalar(KLS[:, GTOK:T], KLS[:, GTOK:T],
                                                 asum[:, :], None, ALU.mult), 1)

            kn_target = p.val("Lkn")

            def local_mm(h=h, hr=hr, hp=hp, qs=qs, kn_target=kn_target):
                @block.tensor
                def _(t):
                    t.wait_ge(p.s("Lkn")[0], kn_target)
                    base_kt = p.val("Lkte")
                    for tt in range(1, 16):
                        t.wait_ge(p.s("Lkte")[0], base_kt + tt - 2 if tt >= 3 else base_kt)
                        ins = t.matmul(ptr[tt % 2][:, :],
                                       KLS[hr:hr + 64, tt * 128:(tt + 1) * 128],
                                       ID16[hr:hr + 64, 64:128] if hr == 64 else ID16[0:64, 0:64],
                                       start=True, stop=True, tile_position=(hr, 0))
                        p.inc("Lkt", ins, 1)

                @block.vector
                def _(v):
                    base_kt = p.val("Lkt") - 15
                    for tt in range(1, 16):
                        v.wait_ge(p.s("Lkt")[0], base_kt + tt)
                        p.inc("Lkte", v.tensor_copy(KLT[:, h % 2, tt - 1, :], ptr[tt % 2][:, 0:64]), 1)

                @block.tensor
                def _(t):
                    p.wait(t, "Lkte")
                    for tt in range(1, 16):
                        ins = t.matmul(kvP[hr:hr + 64, :], Vb[:, tt, h * 64:(h + 1) * 64],
                                       KLT[:, h % 2, tt - 1, :], start=(tt == 1), stop=(tt == 15),
                                       tile_position=(0, hr))
                    p.inc("Lkv", ins, 1)

                @block.vector
                def _(v):
                    p.wait(v, "Lkv")
                    p.inc("Lkve", v.tensor_copy(kv16[hr:hr + 64, :], kvP[hr:hr + 64, :]), 1)

                @block.tensor
                def _(t):
                    base_loe = p.val("Lloe")
                    p.wait(t, "Lkve")
                    lo_chunks = [(GTOK, 512), (512, 1024), (1024, 1536), (1536, 2048)]
                    for i_, (c0, c1) in enumerate(lo_chunks):
                        t.wait_ge(p.s("Lloe")[0], base_loe + i_)
                        ins = t.matmul(loP[hr:hr + 64, 0:c1 - c0], kv16[hr:hr + 64, :],
                                       qs[:, c0:c1], start=True, stop=True,
                                       tile_position=(hr, hr))
                        p.inc("Llo", ins, 1)

                @block.vector
                def _(v):
                    base_lo = p.val("Llo") - 4
                    lo_chunks = [(GTOK, 512), (512, 1024), (1024, 1536), (1536, 2048)]
                    for i_, (c0, c1) in enumerate(lo_chunks):
                        v.wait_ge(p.s("Llo")[0], base_lo + i_ + 1)
                        p.inc("Lloe", v.tensor_copy(Cb[hr:hr + 64, hp, c0:c1],
                                                    loP[hr:hr + 64, 0:c1 - c0]), 1)

            lpending[0] = local_mm

        lfn = lpending[0]
        if lfn is not None:
            lfn()
        lpending[0] = None

        es2.close()
        xc = sb("L_xc", [128, NG, 512], F32)
        ot = sb("L_ot", [128, NG, 512], F32)
        po0 = es.enter_context(nc.psum_tensor(uniq("L_po0"), [128, 512], F32))
        po1 = es.enter_context(nc.psum_tensor(uniq("L_po1"), [128, 512], F32))
        po = [po0, po1]
        for tc in range(4):
            @block.sync
            def _(s, tc=tc):
                if tc > 0:
                    p.wait(s, "Loev")
                else:
                    p.wait(s, "Lnrm")
                p.inc("Loxc", s.dma_start(
                    out=xc[:, :, :],
                    in_=x_in[b, :, tc * 512:(tc + 1) * 512].rearrange("(dg p) t -> p dg t", p=128)), 16)

            @block.tensor
            def _(t, tc=tc):
                base_oev = p.val("Loev")
                p.wait(t, "Lnrm")
                p.wait(t, "Lloe")
                p.wait(t, "Lgw")
                for mg in range(NG):
                    t.wait_ge(p.s("Loev")[0], base_oev if mg < 2 else base_oev + mg - 1)
                    for kg in range(NG):
                        ins = t.matmul(po[mg % 2][:, :], Wo[:, kg, mg * 128:(mg + 1) * 128],
                                       Cb[:, kg, tc * 512:(tc + 1) * 512],
                                       start=(kg == 0), stop=(kg == NG - 1))
                    p.inc("Lomm", ins, 1)

            @block.vector
            def _(v, tc=tc):
                base_omm = p.val("Lomm") - NG
                p.wait(v, "Loxc")
                if tc > 0:
                    p.wait(v, "Lost")
                for mg in range(NG):
                    v.wait_ge(p.s("Lomm")[0], base_omm + mg + 1)
                    p.inc("Loev", v.tensor_add(ot[:, mg, :], po[mg % 2][:, :], xc[:, mg, :]), 1)

            @block.scalar
            def _(s, tc=tc):
                p.wait(s, "Loev")
                p.inc("Lost", s.dma_start(
                    out=x_out[b, :, tc * 512:(tc + 1) * 512].rearrange("(dg p) t -> p dg t", p=128),
                    in_=ot[:, :, :]), 16)


def emit_l1ffn(p, x_in, x_out, nb, h1s, wqs1, wqs2, out_td=None):
    nc = p.nc
    cs = p.io["consts"]
    from contextlib import ExitStack
    with ExitStack() as es:
        def sb(nm, shp, dt):
            return es.enter_context(nc.sbuf_tensor(uniq(nm), shp, dt))
        ones16 = sb("M_ones", [128, 1], BF16)
        onesrow16 = sb("M_onesr", [1, 128], BF16)
        onescol32 = sb("M_onesc32", [128, 1], F32)
        onesrow32 = sb("M_onesr32", [1, 128], F32)
        onesmat = sb("M_onesm", [128, 128], BF16)
        ident16 = sb("M_id16", [128, 128], BF16)
        ident32 = sb("M_id32", [128, 128], F32)
        cw = [sb(f"M_cw{k}", [128, 32], F32) for k in range(3)]
        cb = sb("M_cb", [128, 32], F32)
        srep1 = sb("M_sr1", [128, 2], F32)
        srep2 = sb("M_sr2", [128, 2], F32)
        block = es.enter_context(nc.Block())

        @block.sync
        def _(s):
            for nm, dst in [("ONESCOL16", ones16), ("ONESROW16", onesrow16),
                            ("ONESCOL32", onescol32), ("ONESROW32", onesrow32),
                            ("ONESMAT16", onesmat), ("IDENT16", ident16), ("IDENT32", ident32),
                            ("ROPE_C", None)][:7]:
                p.inc("Mcld", s.dma_start(out=dst[:, :], in_=cs[nm]), 16)
            for k in range(3):
                p.inc("Mcld", s.dma_start(out=cw[k][:, :], in_=p.io[f"convw{k}_fm"]), 16)
            p.inc("Mcld", s.dma_start(out=cb[:, :], in_=p.io["convb_fm"]), 16)

        with ExitStack() as esw:
            stage = esw.enter_context(nc.sbuf_tensor(uniq("M_stage"), [128, 2, 4096], F32))
            qst = esw.enter_context(nc.sbuf_tensor(uniq("M_qst"), [128, 2, 4096], BF16))
            ystage = esw.enter_context(nc.sbuf_tensor(uniq("M_ystage"), [128, 4096], F32))
            abuf = esw.enter_context(nc.sbuf_tensor(uniq("M_abuf"), [128, 4096], F32))
            rsum = esw.enter_context(nc.sbuf_tensor(uniq("M_rsum"), [128, 3], F32))
            sc1 = esw.enter_context(nc.sbuf_tensor(uniq("M_sc1"), [1, 4], F32))
            pm = esw.enter_context(nc.psum_tensor(uniq("M_pm"), [128, 2], F32))

            def w_src(wi, kg):
                if wi == 0:
                    return p.io["L1_ffn_w1T"][kg * 128:(kg + 1) * 128, :]
                return p.io["L1_ffn_w2T"][kg * 512:(kg + 1) * 512, :].rearrange(
                    "(c p) w -> p c w", p=128)

            def w_dst(wi, kg, wq):
                if wi == 0:
                    return wq[kg * 128:(kg + 1) * 128, :]
                return wq[kg * 512:(kg + 1) * 512, :].rearrange("(c p) w -> p c w", p=128)

            for wi, (wq, srep, nelem) in enumerate([
                    (wqs1, srep1, 4096 * 1024), (wqs2, srep2, 4096 * 1024)]):
                for kg in range(NG):
                    @block.sync
                    def _(s, wi=wi, kg=kg):
                        if kg > 1 or wi > 0:
                            p.wait(s, "Mab", max(0, p.val("Mab") - 1))
                            p.wait(s, "Myw", p.val("Myw"))
                        p.inc("Mwl" + ("b" if kg % 2 else ""),
                              s.dma_start(out=stage[:, kg % 2, :], in_=w_src(wi, kg)), 16)

                    @block.vector
                    def _(v, wi=wi, kg=kg):
                        p.wait(v, "Mwl" + ("b" if kg % 2 else ""))
                        if kg == 0:
                            p.inc("Mvch", v.memset(rsum[:, wi:wi + 1], 0.0))
                            p.wait(v, "Mvch")
                        p.wait(v, "Mab", p.val("Mab"))
                        p.inc("Mvch", v.tensor_reduce(rsum[:, 2:3], stage[:, kg % 2, :],
                                                      AX.X, ALU.add, apply_absolute_value=True))
                        p.wait(v, "Mvch")
                        p.inc("Mab", v.tensor_add(rsum[:, wi:wi + 1], rsum[:, wi:wi + 1],
                                                  rsum[:, 2:3]), 1)

                @block.tensor
                def _(t, wi=wi):
                    p.wait(t, "Mab")
                    p.wait(t, "Mcld")
                    p.inc("Mtm", t.matmul(pm[0:1, wi:wi + 1], onescol32[:, :], rsum[:, wi:wi + 1],
                                          start=True, stop=True), 1)

                @block.vector
                def _(v, wi=wi, nelem=nelem):
                    p.wait(v, "Mtm")
                    p.inc("Mvch", v.tensor_scalar(sc1[0:1, 0:1], pm[0:1, wi:wi + 1],
                                                  1.0 / nelem, None, ALU.mult))
                    p.wait(v, "Mvch")
                    p.inc("Mvch", v.tensor_scalar(sc1[0:1, 0:1], sc1[0:1, 0:1], 1e-5, None, ALU.max))
                    p.wait(v, "Mvch")
                    p.inc("Mvch", v.reciprocal(sc1[0:1, 1:2], sc1[0:1, 0:1]))
                    p.wait(v, "Mvch")
                    p.inc("Msc", v.tensor_copy(sc1[0:1, 2:4], sc1[0:1, 0:2]), 1)

                @block.tensor
                def _(t, wi=wi):
                    p.wait(t, "Msc")
                    p.inc("Mbc1", t.matmul(pm[:, 0:2], onesrow32[:, :], sc1[0:1, 0:2],
                                           start=True, stop=True), 1)

                @block.vector
                def _(v, wi=wi, srep=srep):
                    p.wait(v, "Mbc1")
                    p.inc("Mvch", v.tensor_copy(srep[:, 0:1], pm[:, 1:2]))
                    p.wait(v, "Mvch")
                    p.inc("Msr", v.tensor_copy(srep[:, 1:2], pm[:, 0:1]), 1)

                for kg in range(NG):
                    @block.sync
                    def _(s, wi=wi, kg=kg):
                        p.wait(s, "Myw", max(0, p.val("Myw") - 1))
                        if kg < 2:
                            p.wait(s, "Mab")
                        p.inc("Mwl2" + ("b" if kg % 2 else ""),
                              s.dma_start(out=stage[:, kg % 2, :], in_=w_src(wi, kg)), 16)

                    @block.vector
                    def _(v, wi=wi, kg=kg, srep=srep):
                        p.wait(v, "Mwl2" + ("b" if kg % 2 else ""))
                        if kg == 0:
                            p.wait(v, "Msr")
                        p.wait(v, "Mqz", p.val("Mqz"))
                        p.inc("Myw", v.tensor_scalar(ystage[:, :], stage[:, kg % 2, :],
                                                     srep[:, 0:1], None, ALU.mult), 1)
                        p.wait(v, "Myw")
                        p.inc("Mvch", v.tensor_scalar(abuf[:, :], ystage[:, :], 0.5, None, ALU.is_ge))
                        p.wait(v, "Mvch")
                        p.inc("Mvch", v.tensor_scalar(ystage[:, :], ystage[:, :], -0.5, None,
                                                      ALU.is_le))
                        p.wait(v, "Mvch")
                        p.wait(v, "Mqs" + ("b" if kg % 2 else ""))
                        p.inc("Mqz", v.tensor_sub(qst[:, kg % 2, :], abuf[:, :], ystage[:, :]), 1)

                    @block.scalar
                    def _(s, wi=wi, kg=kg, wq=wq):
                        p.wait(s, "Mqz")
                        p.inc("Mqs" + ("b" if kg % 2 else ""),
                              s.dma_start(out=w_dst(wi, kg, wq), in_=qst[:, kg % 2, :]), 16)

        for b in range(nb):
            _mark(p, f"l1f_b{b}")
            _l1ffn_batch(p, block, b, x_in, x_out, h1s, wqs1, wqs2, ones16, onesrow16,
                         onesmat, ident16, ident32, cw, cb, srep1, srep2, out_td)

        @block.sync
        def _(s):
            p.wait(s, "Mst")


def _l1ffn_batch(p, block, b, x_in, x_out, h1s, wqs1, wqs2, ones16, onesrow16,
                 onesmat, ident16, ident32, cw, cb, srep1, srep2, out_td=None):
    nc = p.nc
    from contextlib import ExitStack
    _mark(p, f"l1f_b{b}_p1")
    with ExitStack() as es:
        def sb(nm, shp, dt):
            return es.enter_context(nc.sbuf_tensor(uniq(nm), shp, dt))
        w1q = sb("M_w1q", [128, NG, 4096], BF16)
        xb1 = sb("M_xb", [128, NG, 512], BF16)
        xn1 = sb("M_xn", [128, NG, 512], BF16)
        yb = sb("M_y", [128, 2, NG, 512], BF16)
        ab = sb("M_a", [128, NG, 512], BF16)
        accb = sb("M_acc", [128, NG, 512], BF16)
        sgb = sb("M_sg", [128, NG, 512], BF16)
        mb = sb("M_mb", [128, NG, 512], BF16)
        Sbc = sb("M_sbc", [128, 512], BF16)
        Rbc = sb("M_rbc", [128, 512], BF16)
        mt = sb("M_mt", [128, 4], F32)
        st2 = sb("M_st2", [128, 4], F32)
        Dt = sb("M_dt", [128, 128], BF16)
        h1sb = sb("M_h1sb", [128, NG, 512], BF16)
        sc = alloc_ln_scratch(p, es, "M_ln")
        sc["ones16"] = ones16
        sc["onesrow16"] = onesrow16
        lnsum = es.enter_context(nc.psum_tensor(uniq("M_lnsum"), [1, 512], F32))
        lnsq = es.enter_context(nc.psum_tensor(uniq("M_lnsq"), [1, 512], F32))
        psA = es.enter_context(nc.psum_tensor(uniq("M_psA"), [128, 512], F32))
        psB = es.enter_context(nc.psum_tensor(uniq("M_psB"), [128, 512], F32))
        pq1 = es.enter_context(nc.psum_tensor(uniq("M_pq1"), [128, 512], F32))
        pw = [es.enter_context(nc.psum_tensor(uniq(f"M_pw{i}"), [128, 512], F32)) for i in range(3)]
        pb = {"lnsum": lnsum, "lnsq": lnsq, "psA": psA, "psB": psB}

        @block.gpsimd
        def _(g):
            if b == 0:
                p.wait(g, "Mqs")
                p.wait(g, "Mqsb")
            else:
                p.wait(g, "Mst")
            p.inc("Mwq1", g.dma_start(
                out=w1q[:, :, :], in_=wqs1.rearrange("(kg p) h -> p kg h", p=128)), 16)

        for tc in range(4):
            sl = slice(tc * 512, (tc + 1) * 512)

            @block.gpsimd
            def _(g, tc=tc):
                if b > 0 and tc == 0:
                    p.wait(g, "Mst")
                if b > 0 or tc >= 1:
                    p.wait(g, "cxn")
                p.inc("Mgxb", g.dma_start(
                    out=xb1[:, :, :],
                    in_=x_in[b, :, tc * 512:(tc + 1) * 512].rearrange("(dg p) t -> p dg t", p=128)), 16)

            emit_ln_chunk(p, block, "M", xb1, xn1, sc, pb, xn_guard=("Mw1", 0))

            @block.vector
            def _(v, tc=tc):
                p.wait(v, "Mxn")
                p.wait(v, "Mtr", p.val("Mtr"))
                p.inc("Mgm", v.tensor_reduce(ab[:, 0, :], xn1[:, :, :].transpose([0, 2, 1]),
                                             AX.X, ALU.max, apply_absolute_value=True), 1)

            @block.tensor
            def _(t, tc=tc):
                base_mt = p.val("Mmt")
                p.wait(t, "Mgm")
                p.wait(t, "Mbce2", p.val("Mbce2"))
                for i in range(4):
                    if i > 0:
                        t.wait_ge(p.s("Mmt")[0], base_mt + i)
                    ins = t.matmul(pq1[:, i * 128:(i + 1) * 128], ab[:, 0, i * 128:(i + 1) * 128],
                                   ident16[:, :], start=True, stop=True)
                    p.inc("Mtr", ins, 1)

            @block.vector
            def _(v, tc=tc):
                base_tr = p.val("Mtr") - 4
                for i in range(4):
                    v.wait_ge(p.s("Mtr")[0], base_tr + i + 1)
                    p.inc("Mmt", v.tensor_reduce(mt[:, i:i + 1], pq1[:, i * 128:(i + 1) * 128],
                                                  AX.X, ALU.max), 1)
                p.wait(v, "Mmt")
                p.inc("Mvch", v.tensor_scalar(mt[:, :], mt[:, :], 1e-5, None, ALU.max))
                p.wait(v, "Mvch")
                p.inc("Mvch", v.reciprocal(st2[:, :], mt[:, :]))
                p.wait(v, "Mvch")
                p.inc("Mvch", v.tensor_scalar(st2[:, :], st2[:, :], 127.0, None, ALU.mult))
                p.wait(v, "Mvch")
                p.inc("Msq2", v.tensor_scalar(mt[:, :], mt[:, :], 1.0 / 127.0, None, ALU.mult), 1)

            @block.vector
            def _(v, tc=tc):
                p.wait(v, "Msq2")
                p.wait(v, "Mbcm2", p.val("Mbcm2"))
                for i in range(4):
                    p.inc("MD", v.tensor_scalar(mb[:, 0, i * 128:(i + 1) * 128], ident16[:, :],
                                                st2[:, i:i + 1], None, ALU.mult), 1)

            @block.tensor
            def _(t, tc=tc):
                p.wait(t, "MD")
                p.wait(t, "Mmt")
                for i in range(4):
                    ins = t.matmul(pq1[:, i * 128:(i + 1) * 128], onesmat[:, :],
                                   mb[:, 0, i * 128:(i + 1) * 128], start=True, stop=True)
                p.inc("Mbcm2", ins, 1)

            @block.vector
            def _(v, tc=tc):
                p.wait(v, "Mbcm2")
                p.inc("Mbce2", v.tensor_copy(Sbc[:, :], pq1[:, :]), 1)
                for i in range(4):
                    p.inc("MD", v.tensor_scalar(mb[:, 1, i * 128:(i + 1) * 128], ident16[:, :],
                                                mt[:, i:i + 1], None, ALU.mult), 1)

            @block.tensor
            def _(t, tc=tc):
                p.wait(t, "MD")
                p.wait(t, "Mbce2", p.val("Mbce2"))
                for i in range(4):
                    ins = t.matmul(pq1[:, i * 128:(i + 1) * 128], onesmat[:, :],
                                   mb[:, 1, i * 128:(i + 1) * 128], start=True, stop=True)
                p.inc("Mbcm2", ins, 1)

            @block.vector
            def _(v, tc=tc):
                p.wait(v, "Mbcm2")
                p.inc("Mbce2", v.tensor_copy(Rbc[:, :], pq1[:, :]), 1)

            @block.vector
            def _(v, tc=tc):
                p.wait(v, "Mbce2")
                p.wait(v, "Mw1", max(0, p.val("Mw1") - 32))
                p.inc("My", v.tensor_tensor(yb[:, tc % 2, :, :], xn1[:, :, :],
                                            Sbc[:, :].unsqueeze(1).broadcast_to([128, NG, 512]),
                                            ALU.mult), 1)

            @block.scalar
            def _(s, tc=tc):
                p.wait(s, "My")
                p.inc("Msg2", s.sign(sgb[:, :, :], yb[:, tc % 2, :, :]), 1)
                p.wait(s, "Msg2")
                p.inc("Mabs2", s.activation(ab[:, :, :], yb[:, tc % 2, :, :], AF.Abs), 1)

            @block.vector
            def _(v, tc=tc):
                p.wait(v, "Mabs2")
                p.wait(v, "Mbcm2", p.val("Mbcm2"))
                p.inc("Mvch", v.tensor_scalar(accb[:, :, :], ab[:, :, :], 63.5, 64.0,
                                              ALU.is_ge, ALU.mult))
                p.wait(v, "Mvch")
                for B in (32.0, 16.0, 8.0, 4.0, 2.0, 1.0):
                    p.inc("Mvch", v.tensor_scalar(mb[:, :, :], accb[:, :, :], B - 0.5,
                                                  None, ALU.add))
                    p.wait(v, "Mvch")
                    p.inc("Mvch", v.tensor_tensor(mb[:, :, :], mb[:, :, :], ab[:, :, :],
                                                  ALU.is_le))
                    p.wait(v, "Mvch")
                    p.inc("Mvch", v.tensor_scalar(mb[:, :, :], mb[:, :, :], B, None, ALU.mult))
                    p.wait(v, "Mvch")
                    p.inc("Mvch", v.tensor_add(accb[:, :, :], accb[:, :, :], mb[:, :, :]))
                    p.wait(v, "Mvch")
                p.inc("Mvch", v.tensor_mul(sgb[:, :, :], accb[:, :, :], sgb[:, :, :]))
                p.wait(v, "Mvch")
                p.inc("Mxq", v.tensor_tensor(yb[:, tc % 2, :, :], sgb[:, :, :],
                                             Rbc[:, :].unsqueeze(1).broadcast_to([128, NG, 512]),
                                             ALU.mult), 1)

            for hgb in range(4):
                @block.tensor
                def _(t, tc=tc, hgb=hgb):
                    base_e = p.val("Mw1e")
                    if tc == 0 and hgb == 0:
                        p.wait(t, "Mwq1")
                    p.wait(t, "Mxq")
                    for hgi in range(NG):
                        hg = hgb * NG + hgi
                        t.wait_ge(p.s("Mw1e")[0], base_e if hgi < 3 else base_e + hgi - 2)
                        for kg in range(NG):
                            ins = t.matmul(pw[hgi % 3][:, :], w1q[:, kg, hg * 128:(hg + 1) * 128],
                                           yb[:, tc % 2, kg, :], start=(kg == 0), stop=(kg == NG - 1))
                        p.inc("Mw1", ins, 1)

                @block.scalar
                def _(s, tc=tc, hgb=hgb):
                    base_mm = p.val("Mw1") - NG
                    p.wait(s, "Mh1s", p.val("Mh1s"))
                    for hgi in range(NG):
                        s.wait_ge(p.s("Mw1")[0], base_mm + hgi + 1)
                        p.inc("Mw1e", s.copy(h1sb[:, hgi, :], pw[hgi % 3][:, :]), 1)

                @block.sync
                def _(s, tc=tc, hgb=hgb):
                    p.wait(s, "Mw1e")
                    p.inc("Mh1s", s.dma_start(
                        out=h1s[b, hgb * 1024:(hgb + 1) * 1024, tc * 512:(tc + 1) * 512]
                            .rearrange("(hgi p) t -> p hgi t", p=128),
                        in_=h1sb[:, :, :]), 16)

    _mark(p, f"l1f_b{b}_p2")
    with ExitStack() as es:
        def sb(nm, shp, dt):
            return es.enter_context(nc.sbuf_tensor(uniq(nm), shp, dt))
        w2q = sb("M_w2q", [128, 32, D], BF16)
        hc = sb("M_hc", [128, 32, 258], BF16)
        h2 = sb("M_h2", [128, 32, 256], BF16)
        y2 = sb("M_y2", [128, 32, 256], BF16)
        sg2 = sb("M_sg2", [128, 2, 32, 256], BF16)
        macc = sb("M_macc", [128, 32, 256], BF16)
        mbf = sb("M_mbf", [128, 32, 256], BF16)
        dgr = sb("M_dgr", [128, 4, 6, 128], BF16)
        gmb = sb("M_gmb", [128, 256], BF16)
        S2 = sb("M_s2", [128, 256], BF16)
        R2 = sb("M_r2", [128, 2, 256], BF16)
        mt2 = sb("M_mt2", [128, 2], F32)
        st22 = sb("M_st22", [128, 2], F32)
        xc2 = sb("M_xc2", [128, 4, 256], F32)
        oo = sb("M_oo", [128, 4, 256], F32)
        pq2 = es.enter_context(nc.psum_tensor(uniq("M_pq2"), [128, 512], F32))
        cpb = es.enter_context(nc.psum_tensor(uniq("M_cpb"), [128, 512], F32))
        po = [es.enter_context(nc.psum_tensor(uniq(f"M_po{i}"), [128, 512], F32)) for i in range(4)]
        ptb = [es.enter_context(nc.psum_tensor(uniq(f"M_ptb{i}"), [128, 128], F32))
               for i in range(2)] if out_td is not None else None

        @block.gpsimd
        def _(g):
            p.wait(g, "Mh1s")
            p.wait(g, "pev")
            p.wait(g, "Mxq")
            p.inc("Mwq2", g.dma_start(
                out=w2q[:, :, :], in_=wqs2.rearrange("(hg p) d -> p hg d", p=128)), 16)



        pending = [None]

        def emit_deferred():
            fn = pending[0]
            if fn is not None:
                fn()
            pending[0] = None

        def conv_stage(tcn):
            if tcn >= 8:
                return
            c0n, c1n = tcn * 256, (tcn + 1) * 256

            @block.sync
            def _(s, tcn=tcn, c0n=c0n, c1n=c1n):
                if tcn == 0:
                    p.wait(s, "Mw1e")
                    p.wait(s, "Mh1s")
                    p.wait(s, "Mxq")
                if tcn > 0:
                    p.wait(s, "Mcg", p.val("Mcg"))
                lo = max(0, c0n - 1)
                hi = min(T, c1n + 1)
                p.inc("Mh1l", s.dma_start(
                    out=hc[:, :, (1 if tcn == 0 else 0):(257 if tcn == 7 else 258)],
                    in_=h1s[b, :, lo:hi].rearrange("(hg p) t -> p hg t", p=128)), 16)

            if tcn in (0, 7):
                @block.gpsimd
                def _(g, tcn=tcn):
                    p.wait(g, "Mh1l")
                    p.inc("Mgch", g.memset(hc[:, :, 0:1] if tcn == 0 else hc[:, :, 257:258],
                                           0.0), 1)

            @block.gpsimd
            def _(g, tcn=tcn):
                base_cg = p.val("Mcg")
                for r in range(16):
                    g.wait_ge(p.s("Mcg")[0], max(0, base_cg + r - 3))
                    p.inc(f"Mdg{r % 4}", g.dma_start(
                        out=dgr[:, r % 4, :, :],
                        in_=p.io["convdiag"][6 * r:6 * r + 6].rearrange("n p m -> p n m")), 16)

            @block.tensor
            def _(t, tcn=tcn):
                base_sl2 = p.val("Msl2")
                base_dg = {s_: p.val(f"Mdg{s_}") - 64 for s_ in range(4)}
                p.wait(t, "Mh1l")
                if tcn in (0, 7):
                    p.wait(t, "Mgch")
                p.wait(t, "Mbce3", p.val("Mbce3"))
                for r in range(16):
                    bank = cpb if r % 2 == 0 else pq2
                    t.wait_ge(p.s("Msl2")[0], max(0, base_sl2 + 2 * r - 2))
                    t.wait_ge(p.s(f"Mdg{r % 4}")[0], base_dg[r % 4] + 16 * (r // 4 + 1))
                    for h2i in range(2):
                        hg = 2 * r + h2i
                        for tap in range(3):
                            ins = t.matmul(bank[:, h2i * 256:(h2i + 1) * 256],
                                           dgr[:, r % 4, h2i * 3 + tap, :],
                                           hc[:, hg, tap:tap + 256],
                                           start=(tap == 0), stop=(tap == 2))
                    p.inc("Mcg", ins, 1)

        conv_stage(0)
        for tc in range(8):
            c0, c1 = tc * 256, (tc + 1) * 256

            @block.scalar
            def _(s, tc=tc):
                base_cg = p.val("Mcg") - 16
                p.wait(s, "qy", p.val("qy"))
                p.wait(s, "Msg3", p.val("Msg3"))
                for r in range(16):
                    bank = cpb if r % 2 == 0 else pq2
                    s.wait_ge(p.s("Mcg")[0], base_cg + r + 1)
                    for h2i in range(2):
                        hg = 2 * r + h2i
                        p.inc("Msl2", s.activation(h2[:, hg, :],
                                                   bank[:, h2i * 256:(h2i + 1) * 256],
                                                   AF.Silu, bias=cb[:, hg:hg + 1],
                                                   scale=srep1[:, 1:2]), 1)
                p.wait(s, "Msl2")
                p.wait(s, "Mq2", p.val("Mq2"))
                p.inc("Mabh", s.activation(macc[:, :, :], h2[:, :, :], AF.Abs), 1)
                p.wait(s, "Mw2", max(0, p.val("Mw2") - 8))
                p.inc("Msg3", s.sign(sg2[:, tc % 2, :, :], h2[:, :, :]), 1)

            @block.vector
            def _(v, tc=tc):
                p.wait(v, "Mabh")
                p.wait(v, "Mtr2", p.val("Mtr2"))
                p.wait(v, "Mbcm3", p.val("Mbcm3"))
                p.inc("Mvch", v.tensor_tensor(mbf[:, 0:16, :], macc[:, 0:16, :],
                                              macc[:, 16:32, :], ALU.max))
                p.wait(v, "Mvch")
                for w in (8, 4, 2):
                    p.inc("Mvch", v.tensor_tensor(mbf[:, 0:w, :], mbf[:, 0:w, :],
                                                  mbf[:, w:2 * w, :], ALU.max))
                    p.wait(v, "Mvch")
                p.inc("Mgm2", v.tensor_tensor(gmb[:, :], mbf[:, 0, :], mbf[:, 1, :],
                                              ALU.max), 1)

            @block.tensor
            def _(t, tc=tc):
                base_mt = p.val("Mmt2")
                p.wait(t, "Mgm2")
                p.wait(t, "Mbce3", p.val("Mbce3"))
                for i in range(2):
                    if i > 0:
                        t.wait_ge(p.s("Mmt2")[0], base_mt + i)
                    ins = t.matmul(pq2[:, i * 128:(i + 1) * 128], gmb[:, i * 128:(i + 1) * 128],
                                   ident16[:, :], start=True, stop=True)
                    p.inc("Mtr2", ins, 1)

            @block.vector
            def _(v, tc=tc):
                base_tr = p.val("Mtr2") - 2
                for i in range(2):
                    v.wait_ge(p.s("Mtr2")[0], base_tr + i + 1)
                    p.inc("Mmt2", v.tensor_reduce(mt2[:, i:i + 1], pq2[:, i * 128:(i + 1) * 128],
                                                  AX.X, ALU.max), 1)
                p.wait(v, "Mmt2")
                p.inc("Mvch", v.tensor_scalar(mt2[:, :], mt2[:, :], 1e-5, None, ALU.max))
                p.wait(v, "Mvch")
                p.inc("Mvch", v.reciprocal(st22[:, :], mt2[:, :]))
                p.wait(v, "Mvch")
                p.inc("Mvch", v.tensor_scalar(st22[:, :], st22[:, :], 127.0, None, ALU.mult))
                p.wait(v, "Mvch")
                p.inc("Mvch", v.tensor_scalar(mt2[:, :], mt2[:, :], srep2[:, 1:2], 1.0 / 127.0,
                                              ALU.mult, ALU.mult))
                p.wait(v, "Mvch")
                p.inc("Msq3", v.tensor_copy(st22[:, :], st22[:, :]), 1)

            @block.vector
            def _(v, tc=tc):
                p.wait(v, "Msq3")
                p.wait(v, "Mbcm3", p.val("Mbcm3"))
                for j, (vecsrc, i) in enumerate([(st22, 0), (st22, 1), (mt2, 0), (mt2, 1)]):
                    p.inc("MD2", v.tensor_scalar(
                        mbf[:, j // 2, (j % 2) * 128:(j % 2 + 1) * 128], ident16[:, :],
                        vecsrc[:, i:i + 1], None, ALU.mult), 1)

            @block.tensor
            def _(t, tc=tc):
                p.wait(t, "MD2")
                p.wait(t, "Mmt2")
                for j in range(4):
                    ins = t.matmul(pq2[:, j * 128:(j + 1) * 128], onesmat[:, :],
                                   mbf[:, j // 2, (j % 2) * 128:(j % 2 + 1) * 128],
                                   start=True, stop=True)
                p.inc("Mbcm3", ins, 1)

            @block.vector
            def _(v, tc=tc):
                p.wait(v, "Mbcm3")
                p.inc("Mbce3", v.tensor_copy(S2[:, :], pq2[:, 0:256]))
                p.wait(v, "Mbce3")
                p.inc("Mbce3", v.tensor_copy(R2[:, tc % 2, :], pq2[:, 256:512]), 1)

            emit_deferred()
            conv_stage(tc + 1)

            @block.vector
            def _(v, tc=tc):
                p.wait(v, "Mbce3")
                p.inc("My2", v.tensor_tensor(y2[:, :, :], macc[:, :, :],
                                             S2[:, :].unsqueeze(1).broadcast_to([128, 32, 256]),
                                             ALU.mult), 1)

            def ladder(eng, chain, lo, hi, tc=tc):
                p.wait(eng, "My2")
                p.wait(eng, "Msg3", p.val("Msg3"))
                p.inc(chain, eng.tensor_scalar(macc[:, lo:hi, :], y2[:, lo:hi, :], 63.5, 64.0,
                                               ALU.is_ge, ALU.mult))
                p.wait(eng, chain)
                for B in (32.0, 16.0, 8.0, 4.0, 2.0, 1.0):
                    p.inc(chain, eng.tensor_scalar(mbf[:, lo:hi, :], macc[:, lo:hi, :], B - 0.5,
                                                   None, ALU.add))
                    p.wait(eng, chain)
                    p.inc(chain, eng.tensor_tensor(mbf[:, lo:hi, :], mbf[:, lo:hi, :],
                                                   y2[:, lo:hi, :], ALU.is_le))
                    p.wait(eng, chain)
                    p.inc(chain, eng.tensor_scalar(mbf[:, lo:hi, :], mbf[:, lo:hi, :], B,
                                                   None, ALU.mult))
                    p.wait(eng, chain)
                    p.inc(chain, eng.tensor_add(macc[:, lo:hi, :], macc[:, lo:hi, :],
                                                mbf[:, lo:hi, :]))
                    p.wait(eng, chain)
                p.inc("Mq2", eng.tensor_mul(sg2[:, tc % 2, lo:hi, :], macc[:, lo:hi, :],
                                            sg2[:, tc % 2, lo:hi, :]), 1)

            @block.vector
            def _(v, tc=tc):
                p.wait(v, "Mbcm3", p.val("Mbcm3"))
                ladder(v, "Mlv", 0, 11, tc)
                ladder(v, "Mlv", 11, 22, tc)
                ladder(v, "Mlv", 22, 32, tc)

            @block.tensor
            def _(t, tc=tc):
                base_ev = p.val("Mov")
                if tc == 0:
                    p.wait(t, "Mwq2")
                p.wait(t, "Mq2")
                t.wait_ge(p.s("Mov")[0], base_ev)
                for dg in range(NG):
                    dst = po[dg % 4][:, (dg // 4) * 256:(dg // 4) * 256 + 256]
                    for hg in range(32):
                        ins = t.matmul(dst, w2q[:, hg, dg * 128:(dg + 1) * 128],
                                       sg2[:, tc % 2, hg, :], start=(hg == 0), stop=(hg == 31))
                    p.inc("Mw2", ins, 1)

            def deferred(tc=tc, c0=c0, c1=c1):
                base_mm = p.val("Mw2") - NG
                for half in range(2):
                    @block.sync
                    def _(s, half=half):
                        p.wait(s, "Mov")
                        if out_td is not None:
                            p.wait(s, "Mst")
                        p.inc("Mxc2", s.dma_start(
                            out=xc2[:, :, :],
                            in_=x_in[b, half * 512:(half + 1) * 512, c0:c1]
                                .rearrange("(dg p) t -> p dg t", p=128)), 16)

                    @block.vector
                    def _(v, half=half):
                        p.wait(v, "Mst")
                        for dgi in range(4):
                            dg = half * 4 + dgi
                            v.wait_ge(p.s("Mw2")[0], base_mm + dg + 1)
                            src_ = po[dg % 4][:, (dg // 4) * 256:(dg // 4) * 256 + 256]
                            p.inc("Mev2", v.tensor_tensor(oo[:, dgi, :], src_,
                                                          R2[:, tc % 2, :], ALU.mult), 1)

                    @block.gpsimd
                    def _(g, half=half):
                        base_ev2 = p.val("Mev2") - 4
                        p.wait(g, "Mxc2")
                        for dgi in range(4):
                            g.wait_ge(p.s("Mev2")[0], base_ev2 + dgi + 1)
                            p.inc("Mov", g.tensor_add(oo[:, dgi, :], oo[:, dgi, :],
                                                      xc2[:, dgi, :]), 1)

                    if out_td is None:
                        @block.scalar
                        def _(s, half=half):
                            p.wait(s, "Mov")
                            p.inc("Mst", s.dma_start(
                                out=x_out[b, half * 512:(half + 1) * 512, c0:c1]
                                    .rearrange("(dg p) t -> p dg t", p=128),
                                in_=oo[:, :, :]), 16)
                    else:
                        @block.tensor
                        def _(t, half=half):
                            base_te = p.val("Mte")
                            p.wait(t, "Mov")
                            k = 0
                            for dgi in range(4):
                                for tt in range(2):
                                    if k >= 2:
                                        t.wait_ge(p.s("Mte")[0], base_te + k - 1)
                                    ins = t.transpose(ptb[k % 2][:, :],
                                                      oo[:, dgi, tt * 128:(tt + 1) * 128],
                                                      ident32[:, :])
                                    p.inc("Mtt", ins, 1)
                                    k += 1

                        @block.vector
                        def _(v, half=half):
                            base_tt = p.val("Mtt") - 8
                            k = 0
                            for dgi in range(4):
                                for tt in range(2):
                                    v.wait_ge(p.s("Mtt")[0], base_tt + k + 1)
                                    p.inc("Mte", v.tensor_copy(
                                        xc2[:, tt * 2 + dgi // 2,
                                            (dgi % 2) * 128:(dgi % 2) * 128 + 128],
                                        ptb[k % 2][:, :]), 1)
                                    k += 1

                        @block.scalar
                        def _(s, half=half):
                            p.wait(s, "Mte")
                            for tt in range(2):
                                p.inc("Mst", s.dma_start(
                                    out=out_td[b, c0 + tt * 128:c0 + (tt + 1) * 128,
                                               half * 512:(half + 1) * 512]
                                        .rearrange("t (a c) -> t a c", a=2),
                                    in_=xc2[:, tt * 2:tt * 2 + 2, :]), 16)

            pending[0] = deferred

        emit_deferred()


def build_program(nb, stages=("trin", "l0a", "l0f", "l1a", "l1f", "trout")):
    from contextlib import ExitStack
    nc = bass.Bass("TRN2", target_bir_lowering=False)
    stack = ExitStack()
    p = P(nc, stack)
    p.io = declare_io(nc, nb)
    x_in = nc.dram_tensor("x", [nb, T, D], F32, kind="ExternalInput").ap()
    out = nc.dram_tensor("out", [nb, T, D], F32, kind="ExternalOutput").ap()
    xr0 = nc.dram_tensor("xr0", [nb, D, T], F32, kind="Internal").ap()
    xr1 = nc.dram_tensor("xr1", [nb, D, T], F32, kind="Internal").ap()
    xr2 = nc.dram_tensor("xr2", [nb, D, T], F32, kind="Internal").ap()
    xr3 = nc.dram_tensor("xr3", [nb, D, T], F32, kind="Internal").ap()
    xr4 = nc.dram_tensor("xr4", [nb, D, T], F32, kind="Internal").ap()
    h1s = nc.dram_tensor("h1s", [nb, 4096, T], BF16, kind="Internal").ap()
    wqs1 = nc.dram_tensor("wqs1", [D, 4096], BF16, kind="Internal").ap()
    wqs2 = nc.dram_tensor("wqs2", [4096, D], BF16, kind="Internal").ap()
    scratch = [xr0, xr1, xr2, xr3, xr4]
    with stack:
        cur = None
        if "trin" in stages:
            emit_tr_td2fm(p, x_in, xr0, nb, "ti")
            cur = 0
        if "l0a" in stages:
            emit_l0attn(p, scratch[cur], scratch[cur + 1], nb)
            cur += 1
        if "l0f" in stages:
            emit_l0ffn(p, scratch[cur], scratch[cur + 1], nb)
            cur += 1
        if "l1a" in stages:
            emit_l1attn(p, scratch[cur], scratch[cur + 1], nb)
            cur += 1
        if "l1f" in stages:
            emit_l1ffn(p, scratch[cur], scratch[cur + 1], nb, h1s, wqs1, wqs2,
                       out_td=(out if "trout" in stages else None))
            cur += 1
        elif "trout" in stages:
            emit_tr_fm2td(p, scratch[cur], out, nb, "to")
            cur = None
        if "l1f" in stages and "trout" not in stages:
            pass
    return nc


def kernel(**inputs):
    nb = 2
    maps = shard_inputs(inputs, nb)
    nc = build_program(nb)
    res = run_bass_kernel_spmd(nc, maps, core_ids=list(range(NCORES)))
    return np.concatenate([r["out"] for r in res.results], axis=0).astype(np.float32)


# version 21
# speedup vs baseline: 1.0419x; 1.0146x over previous
import numpy as np
import concourse.bass as bass
import concourse.mybir as mybir
from concourse.bass_utils import run_bass_kernel_spmd

F32 = mybir.dt.float32
BF16 = mybir.dt.bfloat16
AF = mybir.ActivationFunctionType
ALU = mybir.AluOpType
AX = mybir.AxisListType

D = 1024
T = 2048
NG = 8
NH = 16
HD = 64
NCORES = 8

PROF_MARKS = []


def _mark(p, label):
    PROF_MARKS.append((label, p.nc.n_instructions()))


GOLDEN_CENTER = 0.36787944117144233
GOLDEN_LOWER = 0.21231792754821535
GOLDEN_UPPER = 0.5


def host_consts():
    import ml_dtypes
    bf = ml_dtypes.bfloat16
    c = {}
    c["IDENT32"] = np.eye(128, dtype=np.float32)
    c["IDENT16"] = np.eye(128, dtype=bf)
    c["ONESCOL16"] = np.ones((128, 1), dtype=bf)
    c["ONESROW16"] = np.ones((1, 128), dtype=bf)
    c["ONESCOL32"] = np.ones((128, 1), dtype=np.float32)
    c["ONESROW32"] = np.ones((1, 128), dtype=np.float32)
    c["ONESMAT16"] = np.ones((128, 128), dtype=bf)
    j = np.arange(128)[:, None]
    i = np.arange(512)[None, :]
    for d_ in (0, 128, 256, 384):
        c[f"CMASK{d_}"] = ((j + d_) <= i).astype(bf)
    invfreq = (np.float32(1.0) / (np.float32(10000.0) **
               (np.arange(0, HD, 2, dtype=np.float32) / np.float32(HD))))
    pos = np.arange(T, dtype=np.float32)
    fr = (pos[None, :] * invfreq[:, None]).astype(np.float32)
    cos = np.cos(fr.astype(np.float64)).astype(np.float32)
    sin = np.sin(fr.astype(np.float64)).astype(np.float32)
    C = np.concatenate([cos, cos, cos, cos], axis=0)
    S = np.concatenate([-sin, sin, -sin, sin], axis=0)
    c["ROPE_C"] = C
    c["ROPE_S"] = S
    perm = np.arange(128)
    perm = perm + np.where((perm % 64) < 32, 32, -32)
    SW = np.zeros((128, 128), dtype=bf)
    SW[perm, np.arange(128)] = 1.0
    c["SWAPMAT"] = SW
    return c


def shard_inputs(inputs, nb=2):
    import ml_dtypes
    consts = host_consts()
    x = np.ascontiguousarray(inputs["L0_attn_x"])
    ncores = x.shape[0] // nb
    weights = {}
    for k, v in inputs.items():
        if k == "L0_attn_x":
            continue
        weights[k] = np.ascontiguousarray(v)
    weights["L1_ffn_w1T"] = np.ascontiguousarray(inputs["L1_ffn_w1"].T)
    weights["L1_ffn_w2T"] = np.ascontiguousarray(inputs["L1_ffn_w2"].T)
    weights["bo_fm"] = np.ascontiguousarray(inputs["L0_attn_bo"].reshape(NG, 128).T)
    cw = inputs["L1_ffn_conv_w"]
    for k_ in range(3):
        weights[f"convw{k_}_fm"] = np.ascontiguousarray(cw[:, 0, k_].reshape(32, 128).T)
    weights["convb_fm"] = np.ascontiguousarray(inputs["L1_ffn_conv_b"].reshape(32, 128).T)
    cd = np.zeros((96, 128, 128), dtype=np.float32)
    idx = np.arange(128)
    for hg_ in range(32):
        for k_ in range(3):
            cd[hg_ * 3 + k_, idx, idx] = cw[hg_ * 128:(hg_ + 1) * 128, 0, k_]
    weights["convdiag"] = cd
    weights.update(consts)
    maps = []
    for ci in range(ncores):
        m = dict(weights)
        m["x"] = np.ascontiguousarray(x[ci * nb:(ci + 1) * nb])
        maps.append(m)
    return maps


_UID = [0]

def uniq(nm):
    _UID[0] += 1
    return f"{nm}_{_UID[0]}"


_SEM_ALIAS = {}
for _t in ("a", "f", "L", "M"):
    for _r in ("gxb", "lnmm", "sq", "stv", "sts", "stat", "bcm", "bce", "xn",
               "vch", "sch", "cld"):
        _SEM_ALIAS[_t + _r] = "c" + _r
for _t in ("ti", "to"):
    for _r in ("ld", "tr", "ev", "st"):
        _SEM_ALIAS[_t + _r] = "t" + _r
for _n, _c in [
    ("apmm", "pmm"), ("Lpmm", "pmm"), ("Mw1", "pmm"), ("fgm", "pmm"),
    ("apev", "pev"), ("Lpev", "pev"), ("Mw1e", "pev"),
    ("fum", "pm2"), ("Mw2", "pm2"),
    ("aaex", "ex"), ("Laex", "ex"),
    ("aapv", "pv"), ("Lapv", "pv"),
    ("aamm", "smm"), ("Lamm", "smm"),
    ("aabc", "abc"), ("Labc", "abc"),
    ("aarc", "rc"), ("Lrc", "rc"),
    ("aacn", "nrm"), ("Lnrm", "nrm"),
    ("aoev", "oev"), ("Loev", "oev"), ("fov", "oev"), ("Mov", "oev"),
    ("aomm", "omm"), ("Lomm", "omm"),
    ("aost", "ost"), ("Lost", "ost"), ("fst", "ost"), ("Mst", "ost"),
    ("aoxc", "oxc"), ("Loxc", "oxc"), ("fxc", "oxc"), ("Mxc2", "oxc"),
    ("awld2", "gw"), ("agwo", "gw"), ("Lgw", "gw"), ("Lgw2", "gw"),
    ("gwg", "gw1"), ("gwu", "gw2"), ("gwd", "gw3"), ("gwr", "gw"),
    ("awld", "scl"), ("amld", "scl"), ("Lsld", "scl"), ("Mcld", "scl"),
    ("Mwl", "swl"), ("Mwl2", "swl"),
    ("fsil", "sil"), ("Msil", "sil"),
    ("fD", "Dg"), ("MD", "Dg"), ("MD2", "Dg"),
    ("fbcw", "bcw"), ("Mbcm2", "bcw"), ("Mbcm3", "bcw"),
    ("fw16", "bcev"), ("Mbce2", "bcev"), ("Mbce3", "bcev"),
    ("Mabs2", "sabs"), ("Mabs3", "sabs"), ("fabs", "sabs"),
    ("Msg2", "ssg"), ("Msg3", "ssg"), ("Msg", "ssg"),
    ("Mgm", "gm"), ("Mgm2", "gm"),
    ("Mtr", "qtr"), ("Mtr2", "qtr"),
    ("Mmt", "qmt"), ("Mmt2", "qmt"),
    ("Msq2", "qsq"), ("Msq3", "qsq"),
    ("My", "qy"), ("My2", "qy"),
    ("Mq2", "qq"), ("Mxq", "qq"),
]:
    _SEM_ALIAS[_n] = _c


class P:
    def __init__(self, nc, stack):
        self.nc = nc
        self.stack = stack
        self.sems = {}

    def s(self, name):
        name = _SEM_ALIAS.get(name, name)
        if name not in self.sems:
            h = self.stack.enter_context(self.nc.semaphore(name))
            self.sems[name] = [h, 0]
        return self.sems[name]

    def inc(self, name, instr, n=1):
        rec = self.s(name)
        rec[1] += n
        instr.then_inc(rec[0], n)
        return rec[1]

    def val(self, name):
        return self.s(name)[1]

    def wait(self, eng, name, val=None):
        rec = self.s(name)
        eng.wait_ge(rec[0], rec[1] if val is None else val)

    def sb(self, name, shape, dtype):
        return self.stack.enter_context(self.nc.sbuf_tensor(name, shape, dtype))

    def ps(self, name, shape, dtype=F32):
        return self.stack.enter_context(self.nc.psum_tensor(name, shape, dtype))


def declare_io(nc, nb, fm_external=False):
    io = {}
    io["consts"] = {}
    for nm, shp, dt in [
        ("IDENT32", [128, 128], F32), ("IDENT16", [128, 128], BF16),
        ("ONESCOL16", [128, 1], BF16), ("ONESROW16", [1, 128], BF16),
        ("ONESCOL32", [128, 1], F32), ("ONESROW32", [1, 128], F32),
        ("ONESMAT16", [128, 128], BF16),
        ("CMASK0", [128, 512], BF16), ("CMASK128", [128, 512], BF16),
        ("CMASK256", [128, 512], BF16), ("CMASK384", [128, 512], BF16),
        ("ROPE_C", [128, T], F32), ("ROPE_S", [128, T], F32),
        ("SWAPMAT", [128, 128], BF16),
    ]:
        io["consts"][nm] = nc.dram_tensor(nm, shp, F32 if dt == F32 else dt, kind="ExternalInput").ap()
    for nm, shp in [
        ("L0_attn_Wq", [D, D]), ("L0_attn_Wk", [D, D]), ("L0_attn_Wv", [D, D]),
        ("L0_attn_Wo", [D, D]), ("L0_attn_bo", [D]), ("bo_fm", [128, 8]),
        ("L0_ffn_Wg", [8, D, D]), ("L0_ffn_Wu", [8, D, D]), ("L0_ffn_Wd", [8, D, D]),
        ("L0_ffn_Wr", [D, 8]), ("L0_ffn_temperature", [1]),
        ("L1_attn_Wq", [D, D]), ("L1_attn_Wk", [D, D]), ("L1_attn_Wv", [D, D]),
        ("L1_attn_Wout", [D, D]),
        ("L1_ffn_w1", [4096, D]), ("L1_ffn_w1T", [D, 4096]),
        ("L1_ffn_w2", [D, 4096]), ("L1_ffn_w2T", [4096, D]),
        ("L1_ffn_conv_w", [4096, 1, 3]), ("L1_ffn_conv_b", [4096]),
        ("convw0_fm", [128, 32]), ("convw1_fm", [128, 32]), ("convw2_fm", [128, 32]),
        ("convb_fm", [128, 32]), ("convdiag", [96, 128, 128]),
    ]:
        io[nm] = nc.dram_tensor(nm, shp, F32, kind="ExternalInput").ap()
    return io


def emit_tr_td2fm(p, src_td, dst_fm, nb, tag):
    nc = p.nc
    with (
        nc.sbuf_tensor(uniq(f"{tag}_in"), [128, 16, D], F32) as tin,
        nc.sbuf_tensor(uniq(f"{tag}_out"), [128, NG, T], F32) as tout,
        nc.sbuf_tensor(uniq(f"{tag}_id"), [128, 128], F32) as ident,
        nc.psum_tensor(uniq(f"{tag}_p0"), [128, 128], F32) as p0,
        nc.psum_tensor(uniq(f"{tag}_p1"), [128, 128], F32) as p1,
        nc.Block() as block,
    ):
        banks = [p0, p1]
        ld, tr, ev, st = f"{tag}ld", f"{tag}tr", f"{tag}ev", f"{tag}st"
        ld0, tr0, ev0, st0 = p.val(ld), p.val(tr), p.val(ev), p.val(st)

        @block.sync
        def _(s):
            p.inc(ld, s.dma_start(out=ident[:, :], in_=p.io["consts"]["IDENT32"]), 16)
            for b in range(nb):
                if b > 0:
                    s.wait_ge(p.s(tr)[0], tr0 + 128 * b)
                p.inc(ld, s.dma_start(
                    out=tin[:, :],
                    in_=src_td[b].rearrange("(tt p) d -> p tt d", p=128)), 16)

        @block.tensor
        def _(t):
            k = 0
            for b in range(nb):
                t.wait_ge(p.s(ld)[0], ld0 + 16 * (b + 2))
                for tt in range(16):
                    for dg in range(NG):
                        bank = banks[k % 2]
                        if k >= 2:
                            t.wait_ge(p.s(ev)[0], ev0 + k - 1)
                        ins = t.transpose(bank[:, :], tin[:, tt, dg * 128:(dg + 1) * 128],
                                          ident[:, :])
                        p.inc(tr, ins, 1)
                        k += 1

        @block.vector
        def _(v):
            k = 0
            for b in range(nb):
                if b > 0:
                    v.wait_ge(p.s(st)[0], st0 + 16 * b)
                for tt in range(16):
                    for dg in range(NG):
                        bank = banks[k % 2]
                        v.wait_ge(p.s(tr)[0], tr0 + k + 1)
                        ins = v.tensor_copy(tout[:, dg, tt * 128:(tt + 1) * 128], bank[:, :])
                        p.inc(ev, ins, 1)
                        k += 1

        @block.scalar
        def _(s):
            for b in range(nb):
                s.wait_ge(p.s(ev)[0], ev0 + 128 * (b + 1))
                p.inc(st, s.dma_start(
                    out=dst_fm[b].rearrange("(dg p) t -> p dg t", p=128),
                    in_=tout[:, :]), 16)
            s.wait_ge(p.s(st)[0], p.val(st))


def emit_tr_fm2td(p, src_fm, dst_td, nb, tag):
    nc = p.nc
    with (
        nc.sbuf_tensor(uniq(f"{tag}_in"), [128, NG, T], F32) as tin,
        nc.sbuf_tensor(uniq(f"{tag}_out"), [128, 16, D], F32) as tout,
        nc.sbuf_tensor(uniq(f"{tag}_id"), [128, 128], F32) as ident,
        nc.psum_tensor(uniq(f"{tag}_p0"), [128, 128], F32) as p0,
        nc.psum_tensor(uniq(f"{tag}_p1"), [128, 128], F32) as p1,
        nc.Block() as block,
    ):
        banks = [p0, p1]
        ld, tr, ev, st = f"{tag}ld", f"{tag}tr", f"{tag}ev", f"{tag}st"
        ld0, tr0, ev0, st0 = p.val(ld), p.val(tr), p.val(ev), p.val(st)

        @block.sync
        def _(s):
            p.inc(ld, s.dma_start(out=ident[:, :], in_=p.io["consts"]["IDENT32"]), 16)
            for b in range(nb):
                if b > 0:
                    s.wait_ge(p.s(tr)[0], tr0 + 128 * b)
                p.inc(ld, s.dma_start(
                    out=tin[:, :],
                    in_=src_fm[b].rearrange("(dg p) t -> p dg t", p=128)), 16)

        @block.tensor
        def _(t):
            k = 0
            for b in range(nb):
                t.wait_ge(p.s(ld)[0], ld0 + 16 * (b + 2))
                for dg in range(NG):
                    for tt in range(16):
                        bank = banks[k % 2]
                        if k >= 2:
                            t.wait_ge(p.s(ev)[0], ev0 + k - 1)
                        ins = t.transpose(bank[:, :], tin[:, dg, tt * 128:(tt + 1) * 128],
                                          ident[:, :])
                        p.inc(tr, ins, 1)
                        k += 1

        @block.vector
        def _(v):
            k = 0
            for b in range(nb):
                if b > 0:
                    v.wait_ge(p.s(st)[0], st0 + 16 * b)
                for dg in range(NG):
                    for tt in range(16):
                        bank = banks[k % 2]
                        v.wait_ge(p.s(tr)[0], tr0 + k + 1)
                        ins = v.tensor_copy(tout[:, tt, dg * 128:(dg + 1) * 128], bank[:, :])
                        p.inc(ev, ins, 1)
                        k += 1

        @block.scalar
        def _(s):
            for b in range(nb):
                s.wait_ge(p.s(ev)[0], ev0 + 128 * (b + 1))
                p.inc(st, s.dma_start(
                    out=dst_td[b].rearrange("(tt p) d -> p tt d", p=128),
                    in_=tout[:, :]), 16)
            s.wait_ge(p.s(st)[0], p.val(st))


def emit_ln_chunk(p, block, tag, xb, xnc, sc, pb, xn_guard=None):
    nc = p.nc

    @block.tensor
    def _(t):
        p.wait(t, tag + "gxb")
        p.wait(t, tag + "cld")
        p.wait(t, tag + "stat")
        for dg in range(NG):
            ins = t.matmul(pb["lnsum"][:, :], sc["ones16"][:, :], xb[:, dg, :],
                           start=(dg == 0), stop=(dg == NG - 1))
        p.inc(tag + "lnmm", ins, 1)

    @block.vector
    def _(v):
        p.wait(v, tag + "gxb")
        p.wait(v, tag + "lnmm")
        p.inc(tag + "sq", v.tensor_mul(sc["xsq"][:, :, :], xb[:, :, :], xb[:, :, :]), 1)

    @block.tensor
    def _(t):
        p.wait(t, tag + "sq")
        for dg in range(NG):
            ins = t.matmul(pb["lnsq"][:, :], sc["ones16"][:, :], sc["xsq"][:, dg, :],
                           start=(dg == 0), stop=(dg == NG - 1))
        p.inc(tag + "lnmm", ins, 1)

    @block.vector
    def _(v):
        p.wait(v, tag + "lnmm")
        st = sc["st"]
        A, B, C = st[0:1, 0:512], st[0:1, 512:1024], st[0:1, 1024:1536]
        p.inc(tag + "vch", v.tensor_scalar(A, pb["lnsum"][0:1, :], 1.0 / D, None, ALU.mult))
        p.wait(v, tag + "vch")
        p.inc(tag + "vch", v.tensor_scalar(B, pb["lnsq"][0:1, :], 1.0 / D, None, ALU.mult))
        p.wait(v, tag + "vch")
        p.inc(tag + "vch", v.tensor_mul(C, A, A))
        p.wait(v, tag + "vch")
        p.inc(tag + "vch", v.tensor_sub(B, B, C))
        p.wait(v, tag + "vch")
        p.inc(tag + "stv", v.tensor_scalar(B, B, 1e-5, None, ALU.add))

    @block.scalar
    def _(s):
        p.wait(s, tag + "stv")
        B = sc["st"][0:1, 512:1024]
        p.inc(tag + "sch", s.activation(B, B, AF.Ln))
        p.wait(s, tag + "sch")
        p.inc(tag + "sts", s.activation(B, B, AF.Exp, scale=-0.5))

    @block.vector
    def _(v):
        p.wait(v, tag + "sts")
        p.wait(v, tag + "bcm")
        A, B, C = sc["st"][0:1, 0:512], sc["st"][0:1, 512:1024], sc["st"][0:1, 1024:1536]
        p.inc(tag + "vch", v.tensor_copy(sc["a16"][0:1, :], B))
        p.inc(tag + "vch", v.tensor_mul(C, A, B))
        p.wait(v, tag + "vch")
        p.inc(tag + "stat", v.tensor_scalar(sc["b16"][0:1, :], C, -1.0, None, ALU.mult))

    @block.tensor
    def _(t):
        p.wait(t, tag + "stat")
        p.wait(t, tag + "bce")
        t.matmul(pb["psA"][:, :], sc["onesrow16"][:, :], sc["a16"][0:1, :], start=True, stop=True)
        ins = t.matmul(pb["psB"][:, :], sc["onesrow16"][:, :], sc["b16"][0:1, :], start=True, stop=True)
        p.inc(tag + "bcm", ins, 1)

    @block.vector
    def _(v):
        p.wait(v, tag + "bcm")
        v.tensor_copy(sc["A16"][:, :], pb["psA"][:, :])
        p.inc(tag + "bce", v.tensor_copy(sc["B16"][:, :], pb["psB"][:, :]), 1)
        if xn_guard is not None:
            nm, lag = xn_guard
            p.wait(v, nm, max(0, p.val(nm) - lag))
        p.wait(v, tag + "bce")
        a_bc = sc["A16"][:, :].unsqueeze(1).broadcast_to([128, NG, 512])
        b_bc = sc["B16"][:, :].unsqueeze(1).broadcast_to([128, NG, 512])
        p.inc(tag + "vch", v.tensor_tensor(xnc[:, :, :], xb[:, :, :], a_bc, ALU.mult))
        p.wait(v, tag + "vch")
        p.inc(tag + "xn", v.tensor_tensor(xnc[:, :, :], xnc[:, :, :], b_bc, ALU.add), 1)


def alloc_ln_scratch(p, stack, tag):
    nc = p.nc
    sc = {}
    sc["xsq"] = stack.enter_context(nc.sbuf_tensor(uniq(tag + "xsq"), [128, NG, 512], BF16))
    sc["st"] = stack.enter_context(nc.sbuf_tensor(uniq(tag + "st"), [1, 1536], F32))
    sc["a16"] = stack.enter_context(nc.sbuf_tensor(uniq(tag + "a16"), [1, 512], BF16))
    sc["b16"] = stack.enter_context(nc.sbuf_tensor(uniq(tag + "b16"), [1, 512], BF16))
    sc["A16"] = stack.enter_context(nc.sbuf_tensor(uniq(tag + "A16"), [128, 512], BF16))
    sc["B16"] = stack.enter_context(nc.sbuf_tensor(uniq(tag + "B16"), [128, 512], BF16))
    return sc


def emit_ln_consts(p, block, sc, tag):
    @block.sync
    def _(s):
        p.inc(tag + "cld", s.dma_start(out=sc["ones16"][:, :], in_=p.io["consts"]["ONESCOL16"]), 16)
        p.inc(tag + "cld", s.dma_start(out=sc["onesrow16"][:, :], in_=p.io["consts"]["ONESROW16"]), 16)


def emit_l0attn(p, x_in, x_out, nb):
    nc = p.nc
    cs = p.io["consts"]
    from contextlib import ExitStack
    with ExitStack() as es:
        def sb(nm, shp, dt):
            return es.enter_context(nc.sbuf_tensor(uniq(nm), shp, dt))
        Wo = sb("a_wo", [128, NG, D], BF16)
        bo = sb("a_bo", [128, NG], F32)
        Qb = sb("a_q", [128, NG, T], BF16)
        Kb = sb("a_k", [128, NG, T], BF16)
        Vb = sb("a_v", [128, 16, D], BF16)
        ones16 = sb("a_ones", [128, 1], BF16)
        onesrow16 = sb("a_onesr", [1, 128], BF16)
        onesmat = sb("a_onesm", [128, 128], BF16)
        block = es.enter_context(nc.Block())

        @block.sync
        def _(s):
            p.inc("acld", s.dma_start(out=ones16[:, :], in_=cs["ONESCOL16"]), 16)
            p.inc("acld", s.dma_start(out=onesrow16[:, :], in_=cs["ONESROW16"]), 16)
            p.inc("acld", s.dma_start(out=onesmat[:, :], in_=cs["ONESMAT16"]), 16)
            p.inc("awld", s.dma_start(out=bo[:, :], in_=p.io["bo_fm"]), 16)

        @block.gpsimd
        def _(g):
            p.inc("agwo", g.dma_start(
                out=Wo[:, :, :],
                in_=p.io["L0_attn_Wo"].rearrange("(kg p) m -> p kg m", p=128)), 16)

        for b in range(nb):
            _l0attn_batch(p, block, b, x_in, x_out, Wo, bo,
                          Qb, Kb, Vb, ones16, onesrow16, onesmat)

        @block.sync
        def _(s):
            p.wait(s, "aost")


def _l0attn_batch(p, block, b, x_in, x_out, Wo, bo,
                  Qb, Kb, Vb, ones16, onesrow16, onesmat):
    nc = p.nc
    from contextlib import ExitStack
    with ExitStack() as es:
        def sb(nm, shp, dt):
            return es.enter_context(nc.sbuf_tensor(uniq(nm), shp, dt))
        Wq = sb("a_wq", [128, NG, D], BF16)
        Wk = sb("a_wk", [128, NG, D], BF16)
        Wv = sb("a_wv", [128, NG, D], BF16)
        xb1 = sb("a_xb", [128, NG, 512], BF16)
        xn2 = sb("a_xn", [128, 2, NG, 512], BF16)
        sc = alloc_ln_scratch(p, es, "a_ln")
        sc["ones16"] = ones16
        sc["onesrow16"] = onesrow16
        lnsum = es.enter_context(nc.psum_tensor(uniq("a_lnsum"), [1, 512], F32))
        lnsq = es.enter_context(nc.psum_tensor(uniq("a_lnsq"), [1, 512], F32))
        psA = es.enter_context(nc.psum_tensor(uniq("a_psA"), [128, 512], F32))
        psB = es.enter_context(nc.psum_tensor(uniq("a_psB"), [128, 512], F32))
        pb = {"lnsum": lnsum, "lnsq": lnsq, "psA": psA, "psB": psB}
        pj = [es.enter_context(nc.psum_tensor(uniq(f"a_pj{i}"), [128, 512], F32))
              for i in range(3)]

        @block.gpsimd
        def _(g):
            if b > 0:
                p.wait(g, "aost")
            p.inc("awld2", g.dma_start(
                out=Wq[:, :, :], in_=p.io["L0_attn_Wq"].rearrange("(kg p) m -> p kg m", p=128)), 16)
            p.inc("awld2", g.dma_start(
                out=Wk[:, :, :], in_=p.io["L0_attn_Wk"].rearrange("(kg p) m -> p kg m", p=128)), 16)
            p.inc("awld2", g.dma_start(
                out=Wv[:, :, :], in_=p.io["L0_attn_Wv"].rearrange("(kg p) m -> p kg m", p=128)), 16)

        awld2_base = p.val("awld2")
        for tc in range(4):
            xb = xb1
            xnc = xn2[:, tc % 2]

            @block.gpsimd
            def _(g, tc=tc, xb=xb):
                if b > 0 and tc == 0:
                    p.wait(g, "aost")
                if b > 0 or tc >= 1:
                    p.wait(g, "axn")
                p.inc("agxb", g.dma_start(
                    out=xb[:, :, :],
                    in_=x_in[b, :, tc * 512:(tc + 1) * 512].rearrange("(dg p) t -> p dg t", p=128)), 16)

            emit_ln_chunk(p, block, "a", xb, xnc, sc, pb, xn_guard=("apmm", 24))

            @block.tensor
            def _(t, tc=tc, xnc=xnc):
                base_ev = p.val("apev")
                p.wait(t, "axn")
                t.wait_ge(p.s("awld2")[0], awld2_base)
                p.wait(t, "acld")
                k = 0
                for (W, dst, kind) in [(Wq, Qb, "qk"), (Wk, Kb, "qk"), (Wv, Vb, "v")]:
                    for gidx in range(NG):
                        bank = pj[k % 3]
                        t.wait_ge(p.s("apev")[0], base_ev if k < 3 else base_ev + k - 2)
                        if kind == "qk":
                            for kg in range(NG):
                                ins = t.matmul(bank[:, :], W[:, kg, gidx * 128:(gidx + 1) * 128],
                                               xnc[:, kg, :], start=(kg == 0), stop=(kg == NG - 1))
                        else:
                            half, tt4 = gidx % 2, gidx // 2
                            for kg in range(NG):
                                ins = t.matmul(bank[:, :], xnc[:, kg, tt4 * 128:(tt4 + 1) * 128],
                                               W[:, kg, half * 512:(half + 1) * 512],
                                               start=(kg == 0), stop=(kg == NG - 1))
                        p.inc("apmm", ins, 1)
                        k += 1

            @block.scalar
            def _(s, tc=tc):
                base_mm = p.val("apmm") - 24
                if b > 0 and tc == 0:
                    p.wait(s, "aapv")
                k = 0
                for (W, dst, kind) in [(Wq, Qb, "qk"), (Wk, Kb, "qk"), (Wv, Vb, "v")]:
                    for gidx in range(NG):
                        bank = pj[k % 3]
                        s.wait_ge(p.s("apmm")[0], base_mm + k + 1)
                        if kind == "qk":
                            ins = s.copy(dst[:, gidx, tc * 512:(tc + 1) * 512], bank[:, :])
                        else:
                            half, tt4 = gidx % 2, gidx // 2
                            vt = tc * 4 + tt4
                            ins = s.copy(dst[:, vt, half * 512:(half + 1) * 512], bank[:, :])
                        p.inc("apev", ins, 1)
                        k += 1

    with ExitStack() as es:
        def sb(nm, shp, dt):
            return es.enter_context(nc.sbuf_tensor(uniq(nm), shp, dt))
        Cb = sb("a_ctx", [128, NG, T], BF16)
        PT2 = sb("a_pt", [128, 16, 512], BF16)
        MSK = sb("a_msk", [128, 4, 512], BF16)
        rr32 = sb("a_rr32", [128, 512], F32)

        @block.sync
        def _(s):
            p.wait(s, "apev")
            for d_ in range(4):
                p.inc("amld", s.dma_start(out=MSK[:, d_, :], in_=p.io["consts"][f"CMASK{d_*128}"]), 16)
        from contextlib import ExitStack as _ES
        es2 = _ES()
        sbank = [es2.enter_context(nc.psum_tensor(uniq(f"a_s{i}"), [128, 512], F32)) for i in range(4)]
        ctxP = es2.enter_context(nc.psum_tensor(uniq("a_ctxP"), [128, 512], F32))
        rsP = es2.enter_context(nc.psum_tensor(uniq("a_rsP"), [128, 512], F32))
        njts = [4, 8, 12, 16]
        cum = [0, 4, 12, 24]
        for h in range(NH):
            hp, hr = h // 2, (h % 2) * 64
            qs = Qb[hr:hr + 64, hp, :]
            ks = Kb[hr:hr + 64, hp, :]
            base_aex = p.val("aaex")
            base_apv = p.val("aapv")
            base_amk = p.val("aamk")
            base_amm = p.val("aamm")
            base_arc = p.val("aarc")
            base_acn = p.val("aacn")

            def nmask(lh):
                n = 0
                for c2 in range(4):
                    for j2 in range(njts[c2]):
                        if cum[c2] + j2 >= lh:
                            return n
                        if j2 >= 4 * c2:
                            n += 1
                return n

            @block.tensor
            def _(t, h=h, hr=hr, qs=qs, ks=ks, base_aex=base_aex, base_amk=base_amk,
                  base_arc=base_arc, base_acn=base_acn, nmask=nmask):
                if h == 0:
                    p.wait(t, "apev")
                for ci in range(4):
                    i0 = ci * 512
                    for jt in range(njts[ci]):
                        lh = cum[ci] + jt
                        t.wait_ge(p.s("aaex")[0], base_aex + max(0, lh - 3))
                        ins = t.matmul(sbank[lh % 4][:, :], ks[:, jt * 128:(jt + 1) * 128],
                                       qs[:, i0:i0 + 512], start=True, stop=True,
                                       tile_position=(hr, 0))
                        p.inc("aamm", ins, 1)
                    t.wait_ge(p.s("aarc")[0], base_arc + ci)
                    for jt in range(njts[ci]):
                        lh = cum[ci] + jt
                        if jt >= 4 * ci:
                            t.wait_ge(p.s("aamk")[0], base_amk + nmask(lh) + 1)
                        else:
                            t.wait_ge(p.s("aaex")[0], base_aex + lh + 1)
                        t.matmul(rsP[:, :], onesmat[:, :], PT2[:, lh % 16, :],
                                 start=(jt == 0), stop=(jt == njts[ci] - 1))
                    t.wait_ge(p.s("aacn")[0], base_acn + ci)
                    for jt in range(njts[ci]):
                        lh = cum[ci] + jt
                        ins = t.matmul(ctxP[hr:hr + 64, :], Vb[:, jt, h * 64:(h + 1) * 64],
                                       PT2[:, lh % 16, :],
                                       start=(jt == 0), stop=(jt == njts[ci] - 1),
                                       tile_position=(0, hr))
                        p.inc("aapv", ins, 1)

            @block.scalar
            def _(s, h=h, base_amm=base_amm, base_apv=base_apv):
                for lh in range(40):
                    s.wait_ge(p.s("aamm")[0], base_amm + lh + 1)
                    if lh >= 2:
                        s.wait_ge(p.s("aapv")[0], base_apv + max(0, lh - 15))
                    ins = s.activation(PT2[:, lh % 16, :], sbank[lh % 4][:, :], AF.Exp, scale=0.125)
                    p.inc("aaex", ins, 1)

            @block.vector
            def _(v, h=h, hp=hp, hr=hr, base_aex=base_aex, base_apv=base_apv):
                if h == 0:
                    p.wait(v, "amld")
                for ci in range(4):
                    for jt in range(njts[ci]):
                        lh = cum[ci] + jt
                        if jt >= 4 * ci:
                            v.wait_ge(p.s("aaex")[0], base_aex + lh + 1)
                            d_ = jt - 4 * ci
                            ins = v.tensor_mul(PT2[:, lh % 16, :], PT2[:, lh % 16, :], MSK[:, d_, :])
                            p.inc("aamk", ins, 1)
                    v.wait_ge(p.s("aapv")[0], base_apv + cum[ci] + njts[ci])
                    p.inc("aarc", v.reciprocal(rr32[:, :], rsP[:, :]), 1)
                    if b > 0 and h == 0 and ci == 0:
                        p.wait(v, "aomm")
                    p.wait(v, "aarc")
                    p.inc("aacn", v.tensor_mul(Cb[hr:hr + 64, hp, ci * 512:(ci + 1) * 512],
                                               ctxP[hr:hr + 64, :], rr32[hr:hr + 64, :]), 1)

        es2.close()
        xc = sb("a_xc", [128, NG, 512], F32)
        ot = sb("a_ot", [128, NG, 512], F32)
        ov2 = sb("a_ov", [128, 2, 512], F32)
        po0 = es.enter_context(nc.psum_tensor(uniq("a_po0"), [128, 512], F32))
        po1 = es.enter_context(nc.psum_tensor(uniq("a_po1"), [128, 512], F32))
        po = [po0, po1]
        for tc in range(4):
            @block.sync
            def _(s, tc=tc):
                if tc > 0:
                    p.wait(s, "aoev")
                else:
                    p.wait(s, "aacn")
                p.inc("aoxc", s.dma_start(
                    out=xc[:, :, :],
                    in_=x_in[b, :, tc * 512:(tc + 1) * 512].rearrange("(dg p) t -> p dg t", p=128)), 16)

            @block.tensor
            def _(t, tc=tc):
                base_oev = p.val("aoev")
                p.wait(t, "aacn")
                p.wait(t, "awld")
                p.wait(t, "agwo")
                for mg in range(NG):
                    t.wait_ge(p.s("aoev")[0], base_oev if mg < 2 else base_oev + mg - 1)
                    for kg in range(NG):
                        ins = t.matmul(po[mg % 2][:, :], Wo[:, kg, mg * 128:(mg + 1) * 128],
                                       Cb[:, kg, tc * 512:(tc + 1) * 512],
                                       start=(kg == 0), stop=(kg == NG - 1))
                    p.inc("aomm", ins, 1)

            @block.vector
            def _(v, tc=tc):
                base_omm = p.val("aomm") - NG
                p.wait(v, "aoxc")
                if tc > 0:
                    p.wait(v, "aost")
                for mg in range(NG):
                    v.wait_ge(p.s("aomm")[0], base_omm + mg + 1)
                    p.inc("avch", v.tensor_scalar(ov2[:, mg % 2, :], po[mg % 2][:, :],
                                                  bo[:, mg:mg + 1], None, ALU.add))
                    p.wait(v, "avch")
                    p.inc("aoev", v.tensor_add(ot[:, mg, :], ov2[:, mg % 2, :], xc[:, mg, :]), 1)

            @block.scalar
            def _(s, tc=tc):
                p.wait(s, "aoev")
                p.inc("aost", s.dma_start(
                    out=x_out[b, :, tc * 512:(tc + 1) * 512].rearrange("(dg p) t -> p dg t", p=128),
                    in_=ot[:, :, :]), 16)


def emit_l0ffn(p, x_in, x_out, nb):
    nc = p.nc
    cs = p.io["consts"]
    from contextlib import ExitStack
    with ExitStack() as es:
        def sb(nm, shp, dt):
            return es.enter_context(nc.sbuf_tensor(uniq(nm), shp, dt))
        ones16 = sb("f_ones", [128, 1], BF16)
        onesrow16 = sb("f_onesr", [1, 128], BF16)
        onesrow32 = sb("f_onesr32", [1, 128], F32)
        onesmat = sb("f_onesm", [128, 128], BF16)
        ident16 = sb("f_id16", [128, 128], BF16)
        Wr = sb("f_wr", [128, NG, 8], BF16)
        tmp1 = sb("f_tmp1", [1, 1], F32)
        invt = sb("f_invt", [128, 1], F32)
        block = es.enter_context(nc.Block())

        @block.sync
        def _(s):
            p.inc("fcld", s.dma_start(out=ones16[:, :], in_=cs["ONESCOL16"]), 16)
            p.inc("fcld", s.dma_start(out=onesrow16[:, :], in_=cs["ONESROW16"]), 16)
            p.inc("fcld", s.dma_start(out=onesrow32[:, :], in_=cs["ONESROW32"]), 16)
            p.inc("fcld", s.dma_start(out=onesmat[:, :], in_=cs["ONESMAT16"]), 16)
            p.inc("fcld", s.dma_start(out=ident16[:, :], in_=cs["IDENT16"]), 16)
            p.inc("fcld", s.dma_start(out=tmp1[:, :], in_=p.io["L0_ffn_temperature"].rearrange("(one e) -> one e", one=1)), 16)

        @block.gpsimd
        def _(g):
            p.inc("gwr", g.dma_start(
                out=Wr[:, :, :], in_=p.io["L0_ffn_Wr"].rearrange("(kg p) e -> p kg e", p=128)), 16)

        with nc.psum_tensor(uniq("f_pt"), [128, 1], F32) as ptb:
            @block.tensor
            def _(t):
                p.wait(t, "fcld")
                p.inc("ftm", t.matmul(ptb[:, :], onesrow32[:, :], tmp1[:, :], start=True, stop=True), 1)

            @block.vector
            def _(v):
                p.wait(v, "ftm")
                p.inc("fvch", v.reciprocal(invt[:, :], ptb[:, :]))
                p.wait(v, "fvch")
                v.nop()

        for b in range(nb):
            _l0ffn_batch(p, block, b, x_in, x_out, ones16, onesrow16, onesmat,
                         ident16, Wr, invt)

        @block.sync
        def _(s):
            p.wait(s, "fst")


def _l0ffn_batch(p, block, b, x_in, x_out, ones16, onesrow16, onesmat, ident16, Wr, invt):
    nc = p.nc
    from contextlib import ExitStack
    with ExitStack() as es0:
        def sb0(nm, shp, dt):
            return es0.enter_context(nc.sbuf_tensor(uniq(nm), shp, dt))
        xn = sb0("f_xn", [128, NG, T], BF16)
        acc = sb0("f_acc", [128, NG, T], F32)
        w_all = sb0("f_wall", [128, 16, 8], F32)

        with ExitStack() as es:
            def sb(nm, shp, dt):
                return es.enter_context(nc.sbuf_tensor(uniq(nm), shp, dt))
            xb = sb("f_xb", [128, NG, 512], BF16)
            sc = alloc_ln_scratch(p, es, "f_ln")
            sc["ones16"] = ones16
            sc["onesrow16"] = onesrow16
            lnsum = es.enter_context(nc.psum_tensor(uniq("f_lnsum"), [1, 512], F32))
            lnsq = es.enter_context(nc.psum_tensor(uniq("f_lnsq"), [1, 512], F32))
            psA = es.enter_context(nc.psum_tensor(uniq("f_psA"), [128, 512], F32))
            psB = es.enter_context(nc.psum_tensor(uniq("f_psB"), [128, 512], F32))
            pb = {"lnsum": lnsum, "lnsq": lnsq, "psA": psA, "psB": psB}
            if b == 0:
                @block.tensor
                def _(t):
                    p.wait(t, "fvch")

            for tc in range(4):
                xnc = xn[:, :, tc * 512:(tc + 1) * 512]

                @block.gpsimd
                def _(g, tc=tc):
                    if b > 0 and tc == 0:
                        p.wait(g, "fst")
                    if tc >= 1:
                        p.wait(g, "fxn")
                    p.inc("fgxb", g.dma_start(
                        out=xb[:, :, :],
                        in_=x_in[b, :, tc * 512:(tc + 1) * 512].rearrange("(dg p) t -> p dg t", p=128)), 16)

                emit_ln_chunk(p, block, "f", xb, xnc, sc, pb, xn_guard=None)

            @block.gpsimd
            def _(g):
                if b > 0:
                    p.wait(g, "fst")
                p.inc("fmz", g.memset(acc[:, :, :], 0.0), 1)

        with (
            nc.psum_tensor(uniq("f_rz"), [128, 128], F32) as rz,
            nc.psum_tensor(uniq("f_rw"), [8, 128], F32) as rw_unused,
        ):
            with ExitStack() as es:
                rt = es.enter_context(nc.sbuf_tensor(uniq("f_rt"), [128, 16, 8], F32))
                r1 = es.enter_context(nc.sbuf_tensor(uniq("f_r1"), [128, 16, 8], F32))
                r2 = es.enter_context(nc.sbuf_tensor(uniq("f_r2"), [128, 16, 8], F32))
                r3 = es.enter_context(nc.sbuf_tensor(uniq("f_r3"), [128, 16, 8], F32))
                q1 = es.enter_context(nc.sbuf_tensor(uniq("f_q1"), [128, 16], F32))
                q2 = es.enter_context(nc.sbuf_tensor(uniq("f_q2"), [128, 16], F32))

                @block.tensor
                def _(t):
                    p.wait(t, "fxn")
                    p.wait(t, "gwr")
                    for tt in range(16):
                        for kg in range(NG):
                            ins = t.matmul(rz[:, tt * 8:(tt + 1) * 8],
                                           xn[:, kg, tt * 128:(tt + 1) * 128],
                                           Wr[:, kg, :], start=(kg == 0), stop=(kg == NG - 1))
                    p.inc("frz", ins, 1)

                @block.scalar
                def _(s):
                    p.wait(s, "frz")
                    p.inc("frsig", s.activation(rt[:, :, :], rz[:, :], AF.Sigmoid, scale=invt[:, :]), 1)

                @block.vector
                def _(v):
                    p.wait(v, "frsig")
                    p.inc("fvch", v.tensor_scalar(r1[:, :, :], rt[:, :, :], -0.3678794503211975, None, ALU.add))
                    p.wait(v, "fvch")
                    p.inc("fgc", v.tensor_scalar(r1[:, :, :], r1[:, :, :], --9.149754731652138e-09, None, ALU.add), 1)

                @block.scalar
                def _(s):
                    p.wait(s, "fgc")
                    p.inc("fabs", s.activation(r1[:, :, :], r1[:, :, :], AF.Abs), 1)

                @block.vector
                def _(v):
                    p.wait(v, "frsig")
                    p.inc("fvch", v.tensor_scalar(r2[:, :, :], rt[:, :, :], GOLDEN_LOWER, None, ALU.is_ge))
                    p.wait(v, "fvch")
                    p.inc("fvch", v.tensor_scalar(r3[:, :, :], rt[:, :, :], GOLDEN_UPPER, None, ALU.is_le))
                    p.wait(v, "fvch")
                    p.inc("frd", v.tensor_mul(r2[:, :, :], r2[:, :, :], r3[:, :, :]), 1)

                @block.scalar
                def _(s):
                    p.wait(s, "frd")
                    p.wait(s, "fabs")
                    p.inc("fsch", s.activation(rt[:, :, :], r1[:, :, :], AF.Exp, scale=-10.0))
                    p.wait(s, "fsch")
                    p.inc("fre", s.activation(r3[:, :, :], r1[:, :, :], AF.Exp, scale=-1.0 / 0.3), 1)

                @block.vector
                def _(v):
                    p.wait(v, "fre")
                    p.inc("fvch", v.tensor_mul(r1[:, :, :], rt[:, :, :], r2[:, :, :]))
                    p.wait(v, "fvch")
                    p.inc("fvch", v.tensor_reduce(q1[:, :], r1[:, :, :], AX.X, ALU.add))
                    p.wait(v, "fvch")
                    p.inc("fvch", v.tensor_scalar(q1[:, :], q1[:, :], 1e-8, None, ALU.is_lt))
                    p.wait(v, "fvch")
                    p.inc("fvch", v.tensor_reduce(q2[:, :], r3[:, :, :], AX.X, ALU.max))
                    p.wait(v, "fvch")
                    p.inc("fvch", v.tensor_tensor(r2[:, :, :], r3[:, :, :],
                                                  q2[:, :].unsqueeze(2).broadcast_to([128, 16, 8]), ALU.is_ge))
                    p.wait(v, "fvch")
                    p.inc("fvch", v.scalar_tensor_tensor(r2[:, :, :], r2[:, :, :], -1e30, r3[:, :, :],
                                                         ALU.mult, ALU.add))
                    p.wait(v, "fvch")
                    p.inc("fvch", v.tensor_reduce(q2[:, :], r2[:, :, :], AX.X, ALU.max))
                    p.wait(v, "fvch")
                    p.inc("fvch", v.tensor_tensor(r2[:, :, :], r3[:, :, :],
                                                  q2[:, :].unsqueeze(2).broadcast_to([128, 16, 8]), ALU.is_ge))
                    p.wait(v, "fvch")
                    p.inc("fvch", v.tensor_mul(r3[:, :, :], r3[:, :, :], r2[:, :, :]))
                    p.wait(v, "fvch")
                    p.inc("fvch", v.tensor_reduce(q2[:, :], r3[:, :, :], AX.X, ALU.add))
                    p.wait(v, "fvch")
                    p.inc("fvch", v.tensor_scalar(q2[:, :], q2[:, :], 1e-8, None, ALU.max))
                    p.wait(v, "fvch")
                    p.inc("fvch", v.reciprocal(q2[:, :], q2[:, :]))
                    p.wait(v, "fvch")
                    p.inc("fvch", v.tensor_tensor(r3[:, :, :], r3[:, :, :],
                                                  q2[:, :].unsqueeze(2).broadcast_to([128, 16, 8]), ALU.mult))
                    p.wait(v, "fvch")
                    p.inc("fvch", v.tensor_sub(r3[:, :, :], r3[:, :, :], r1[:, :, :]))
                    p.wait(v, "fvch")
                    p.inc("fvch", v.tensor_tensor(r3[:, :, :], r3[:, :, :],
                                                  q1[:, :].unsqueeze(2).broadcast_to([128, 16, 8]), ALU.mult))
                    p.wait(v, "fvch")
                    p.inc("fvch", v.tensor_add(r1[:, :, :], r1[:, :, :], r3[:, :, :]))
                    p.wait(v, "fvch")
                    p.inc("fvch", v.tensor_reduce(q2[:, :], r1[:, :, :], AX.X, ALU.add))
                    p.wait(v, "fvch")
                    p.inc("fvch", v.tensor_scalar(q2[:, :], q2[:, :], 1e-8, None, ALU.max))
                    p.wait(v, "fvch")
                    p.inc("fvch", v.reciprocal(q2[:, :], q2[:, :]))
                    p.wait(v, "fvch")
                    p.inc("frw", v.tensor_tensor(w_all[:, :, :], r1[:, :, :],
                                                 q2[:, :].unsqueeze(2).broadcast_to([128, 16, 8]), ALU.mult), 1)

        with ExitStack() as es:
            def sb(nm, shp, dt):
                return es.enter_context(nc.sbuf_tensor(uniq(nm), shp, dt))
            Wg = sb("f_wg", [128, NG, D], BF16)
            Wu = sb("f_wu", [128, NG, D], BF16)
            Wd = sb("f_wd", [128, NG, D], BF16)
            h_all = sb("f_h", [128, NG, T], BF16)
            tsil = sb("f_tsil", [128, 2, 512], F32)
            h1t = sb("f_h1", [128, 2, 512], F32)
            W16 = sb("f_w16", [128, 2, 512], BF16)
            Dt = sb("f_dt", [128, 4, 128], BF16)
            pg = [es.enter_context(nc.psum_tensor(uniq(f"f_pg{i}"), [128, 512], F32)) for i in range(2)]
            pu = [es.enter_context(nc.psum_tensor(uniq(f"f_pu{i}"), [128, 512], F32)) for i in range(2)]
            po = [es.enter_context(nc.psum_tensor(uniq(f"f_po{i}"), [128, 512], F32)) for i in range(3)]
            psW = es.enter_context(nc.psum_tensor(uniq("f_psW"), [128, 512], F32))

            for e in range(8):
                @block.gpsimd
                def _(g, e=e):
                    if e == 0:
                        p.wait(g, "fxn")
                        p.wait(g, "frw")
                    if e > 0 or b > 0:
                        p.wait(g, "fgm", p.val("fgm"))
                        p.wait(g, "fum", p.val("fum"))
                    p.inc("gwg", g.dma_start(
                        out=Wg[:, :, :],
                        in_=p.io["L0_ffn_Wg"][e].rearrange("(kg p) i -> p kg i", p=128)), 16)
                    p.inc("gwu", g.dma_start(
                        out=Wu[:, :, :],
                        in_=p.io["L0_ffn_Wu"][e].rearrange("(kg p) i -> p kg i", p=128)), 16)
                    if e > 0 or b > 0:
                        p.wait(g, "fdm", p.val("fdm"))
                    p.inc("gwd", g.dma_start(
                        out=Wd[:, :, :],
                        in_=p.io["L0_ffn_Wd"][e].rearrange("(ig p) d -> p ig d", p=128)), 16)

                base_gwg = p.val("gwg")
                base_gwu = p.val("gwu")
                base_gwd = p.val("gwd")

                for tp in range(2):
                    for sub in range(2):
                        @block.vector
                        def _(v, e=e, tp=tp, sub=sub):
                            if e == 0 and tp == 0 and sub == 0:
                                p.wait(v, "frw")
                            p.wait(v, "fbcw", p.val("fbcw"))
                            for tt4 in range(4):
                                tt = (2 * tp + sub) * 4 + tt4
                                p.inc("fD", v.tensor_scalar(Dt[:, tt4, :], ident16[:, :],
                                                            w_all[:, tt, e:e + 1], None, ALU.mult), 1)

                        @block.tensor
                        def _(t, e=e, tp=tp, sub=sub):
                            p.wait(t, "fD")
                            p.wait(t, "fw16", p.val("fw16"))
                            for tt4 in range(4):
                                ins = t.matmul(psW[:, tt4 * 128:(tt4 + 1) * 128], onesmat[:, :],
                                               Dt[:, tt4, :], start=True, stop=True)
                            p.inc("fbcw", ins, 1)

                        @block.vector
                        def _(v, e=e, tp=tp, sub=sub):
                            p.wait(v, "fbcw")
                            p.inc("fw16", v.tensor_copy(W16[:, sub, :], psW[:, :]), 1)

                    base_sil = p.val("fsil")
                    base_h1 = p.val("fh1")
                    base_gm = p.val("fgm")
                    base_um = p.val("fum")

                    @block.tensor
                    def _(t, e=e, tp=tp, base_sil=base_sil, base_h1=base_h1):
                        if e == 0 and tp == 0:
                            p.wait(t, "fxn")
                        if tp == 0:
                            t.wait_ge(p.s("gwg")[0], base_gwg)
                            t.wait_ge(p.s("gwu")[0], base_gwu)
                        for ig in range(NG):
                            t.wait_ge(p.s("fsil")[0], base_sil + 2 * ig)
                            for sub in range(2):
                                tcx = 2 * tp + sub
                                for kg in range(NG):
                                    ins = t.matmul(pg[sub][:, :], Wg[:, kg, ig * 128:(ig + 1) * 128],
                                                   xn[:, kg, tcx * 512:(tcx + 1) * 512],
                                                   start=(kg == 0), stop=(kg == NG - 1))
                                p.inc("fgm", ins, 1)
                            t.wait_ge(p.s("fh1")[0], base_h1 + 2 * ig)
                            for sub in range(2):
                                tcx = 2 * tp + sub
                                for kg in range(NG):
                                    ins = t.matmul(pu[sub][:, :], Wu[:, kg, ig * 128:(ig + 1) * 128],
                                                   xn[:, kg, tcx * 512:(tcx + 1) * 512],
                                                   start=(kg == 0), stop=(kg == NG - 1))
                                p.inc("fum", ins, 1)

                    @block.scalar
                    def _(s, e=e, tp=tp, base_sil=base_sil, base_h1=base_h1,
                          base_gm=base_gm):
                        for ig in range(NG):
                            for sub in range(2):
                                s.wait_ge(p.s("fgm")[0], base_gm + 2 * ig + sub + 1)
                                if ig >= 1:
                                    s.wait_ge(p.s("fh1")[0], base_h1 + 2 * (ig - 1) + sub + 1)
                                p.inc("fsil", s.activation(tsil[:, sub, :], pg[sub][:, :],
                                                           AF.Silu), 1)

                    @block.vector
                    def _(v, e=e, tp=tp, base_sil=base_sil, base_um=base_um):
                        p.wait(v, "fw16")
                        for ig in range(NG):
                            for sub in range(2):
                                v.wait_ge(p.s("fsil")[0], base_sil + 2 * ig + sub + 1)
                                v.wait_ge(p.s("fum")[0], base_um + 2 * ig + sub + 1)
                                p.inc("fh1", v.tensor_mul(h1t[:, sub, :], tsil[:, sub, :],
                                                          pu[sub][:, :]), 1)
                                p.wait(v, "fh1")
                                p.inc("fh", v.tensor_mul(
                                    h_all[:, ig, (2 * tp + sub) * 512:(2 * tp + sub + 1) * 512],
                                    h1t[:, sub, :], W16[:, sub, :]), 1)

                base_fh = p.val("fh") - 32

                @block.tensor
                def _(t, e=e):
                    base_dm = p.val("fdm")
                    t.wait_ge(p.s("gwd")[0], base_gwd)
                    g = 0
                    for tp in range(2):
                        t.wait_ge(p.s("fh")[0], base_fh + 16 * (tp + 1))
                        for dg in range(NG):
                            for sub in range(2):
                                tcx = 2 * tp + sub
                                G = base_dm + g
                                bank = po[G % 3]
                                t.wait_ge(p.s("fac")[0], max(0, G - 2))
                                for ig in range(NG):
                                    ins = t.matmul(bank[:, :], Wd[:, ig, dg * 128:(dg + 1) * 128],
                                                   h_all[:, ig, tcx * 512:(tcx + 1) * 512],
                                                   start=(ig == 0), stop=(ig == NG - 1))
                                p.inc("fdm", ins, 1)
                                g += 1

                @block.vector
                def _(v, e=e):
                    base_dm = p.val("fdm") - 32
                    if e == 0:
                        p.wait(v, "fmz")
                    g = 0
                    for tp in range(2):
                        for dg in range(NG):
                            for sub in range(2):
                                tcx = 2 * tp + sub
                                G = base_dm + g
                                v.wait_ge(p.s("fdm")[0], G + 1)
                                p.inc("fac", v.tensor_add(acc[:, dg, tcx * 512:(tcx + 1) * 512],
                                                          acc[:, dg, tcx * 512:(tcx + 1) * 512],
                                                          po[G % 3][:, :]), 1)
                                g += 1

        with ExitStack() as es:
            xc = es.enter_context(nc.sbuf_tensor(uniq("f_xc"), [128, NG, 512], F32))
            for tc in range(4):
                @block.sync
                def _(s, tc=tc):
                    if tc > 0:
                        p.wait(s, "fov")
                    else:
                        p.wait(s, "fac")
                    p.inc("fxc", s.dma_start(
                        out=xc[:, :, :],
                        in_=x_in[b, :, tc * 512:(tc + 1) * 512].rearrange("(dg p) t -> p dg t", p=128)), 16)

                @block.vector
                def _(v, tc=tc):
                    p.wait(v, "fxc")
                    p.wait(v, "fac")
                    if tc > 0:
                        p.wait(v, "fst")
                    p.inc("fov", v.tensor_add(acc[:, :, tc * 512:(tc + 1) * 512],
                                              acc[:, :, tc * 512:(tc + 1) * 512], xc[:, :, :]), 1)

                @block.scalar
                def _(s, tc=tc):
                    p.wait(s, "fov")
                    p.inc("fst", s.dma_start(
                        out=x_out[b, :, tc * 512:(tc + 1) * 512].rearrange("(dg p) t -> p dg t", p=128),
                        in_=acc[:, :, tc * 512:(tc + 1) * 512]), 16)


GTOK = 204

def emit_l1attn(p, x_in, x_out, nb):
    nc = p.nc
    cs = p.io["consts"]
    from contextlib import ExitStack
    with ExitStack() as es:
        def sb(nm, shp, dt):
            return es.enter_context(nc.sbuf_tensor(uniq(nm), shp, dt))
        Wo = sb("L_wo", [128, NG, D], BF16)
        Qb = sb("L_q", [128, NG, T], BF16)
        Kb = sb("L_k", [128, NG, T], BF16)
        Vb = sb("L_v", [128, 16, D], BF16)
        ID16 = sb("L_id16", [128, 128], BF16)
        ones16 = sb("L_ones", [128, 1], BF16)
        onesrow16 = sb("L_onesr", [1, 128], BF16)
        onesmat = sb("L_onesm", [128, 128], BF16)
        block = es.enter_context(nc.Block())

        @block.sync
        def _(s):
            p.inc("Lcld", s.dma_start(out=ones16[:, :], in_=cs["ONESCOL16"]), 16)
            p.inc("Lcld", s.dma_start(out=onesrow16[:, :], in_=cs["ONESROW16"]), 16)
            p.inc("Lcld", s.dma_start(out=onesmat[:, :], in_=cs["ONESMAT16"]), 16)
            p.inc("Lcld", s.dma_start(out=ID16[:, :], in_=cs["IDENT16"]), 16)

        @block.gpsimd
        def _(g):
            p.inc("Lgw", g.dma_start(
                out=Wo[:, :, :],
                in_=p.io["L1_attn_Wout"].rearrange("(kg p) m -> p kg m", p=128)), 16)

        for b in range(nb):
            _l1attn_batch(p, block, b, x_in, x_out, Wo, Qb, Kb, Vb,
                          ID16, ones16, onesrow16, onesmat)

        @block.sync
        def _(s):
            p.wait(s, "Lost")


def _l1attn_batch(p, block, b, x_in, x_out, Wo, Qb, Kb, Vb, ID16,
                  ones16, onesrow16, onesmat):
    nc = p.nc
    from contextlib import ExitStack
    with ExitStack() as es:
        def sb(nm, shp, dt):
            return es.enter_context(nc.sbuf_tensor(uniq(nm), shp, dt))
        Wq = sb("L_wq", [128, NG, D], BF16)
        Wk = sb("L_wk", [128, NG, D], BF16)
        Wv = sb("L_wv", [128, NG, D], BF16)
        xb1 = sb("L_xb", [128, NG, 512], BF16)
        xn1 = sb("L_xn", [128, NG, 512], BF16)
        qraw = sb("L_qraw", [128, 2, 512], BF16)
        rtmp = sb("L_rtmp", [128, 512], F32)
        C16 = sb("L_c16", [128, T], BF16)
        S16 = sb("L_s16", [128, T], BF16)
        SWP = sb("L_swp", [128, 128], BF16)
        sc = alloc_ln_scratch(p, es, "L_ln")
        sc["ones16"] = ones16
        sc["onesrow16"] = onesrow16
        lnsum = es.enter_context(nc.psum_tensor(uniq("L_lnsum"), [1, 512], F32))
        lnsq = es.enter_context(nc.psum_tensor(uniq("L_lnsq"), [1, 512], F32))
        psA = es.enter_context(nc.psum_tensor(uniq("L_psA"), [128, 512], F32))
        psB = es.enter_context(nc.psum_tensor(uniq("L_psB"), [128, 512], F32))
        psw = es.enter_context(nc.psum_tensor(uniq("L_psw"), [128, 512], F32))
        pb = {"lnsum": lnsum, "lnsq": lnsq, "psA": psA, "psB": psB}
        pj = [es.enter_context(nc.psum_tensor(uniq(f"L_pj{i}"), [128, 512], F32))
              for i in range(3)]

        @block.gpsimd
        def _(g):
            if b > 0:
                p.wait(g, "Lost")
            p.inc("Lgw2", g.dma_start(
                out=Wq[:, :, :], in_=p.io["L1_attn_Wq"].rearrange("(kg p) m -> p kg m", p=128)), 16)
            p.inc("Lgw2", g.dma_start(
                out=Wk[:, :, :], in_=p.io["L1_attn_Wk"].rearrange("(kg p) m -> p kg m", p=128)), 16)
            p.inc("Lgw2", g.dma_start(
                out=Wv[:, :, :], in_=p.io["L1_attn_Wv"].rearrange("(kg p) m -> p kg m", p=128)), 16)
            p.inc("Lgw2", g.dma_start(out=C16[:, :], in_=p.io["consts"]["ROPE_C"]), 16)
            p.inc("Lgw2", g.dma_start(out=S16[:, :], in_=p.io["consts"]["ROPE_S"]), 16)

        @block.sync
        def _(s):
            if b > 0:
                p.wait(s, "Lost")
            p.inc("Lsld", s.dma_start(out=SWP[:, :], in_=p.io["consts"]["SWAPMAT"]), 16)

        base_gw2 = p.val("Lgw2")
        for tc in range(4):
            xnc = xn1

            @block.gpsimd
            def _(g, tc=tc):
                if b > 0 and tc == 0:
                    p.wait(g, "Lost")
                if b > 0 or tc >= 1:
                    p.wait(g, "Lxn")
                p.inc("Lgxb", g.dma_start(
                    out=xb1[:, :, :],
                    in_=x_in[b, :, tc * 512:(tc + 1) * 512].rearrange("(dg p) t -> p dg t", p=128)), 16)

            emit_ln_chunk(p, block, "L", xb1, xnc, sc, pb, xn_guard=("Lpmm", 0))

            @block.tensor
            def _(t, tc=tc, xnc=xnc):
                base_ev = p.val("Lpev")
                base_sm = p.val("Lsm")
                p.wait(t, "Lxn")
                t.wait_ge(p.s("Lgw2")[0], base_gw2)
                p.wait(t, "Lcld")
                p.wait(t, "Lsld")
                k = 0
                for (W, kind) in [(Wq, "q"), (Wk, "k"), (Wv, "v")]:
                    for gidx in range(NG):
                        bank = pj[k % 3]
                        t.wait_ge(p.s("Lpev")[0], base_ev if k < 3 else base_ev + k - 2)
                        if kind in ("q", "k"):
                            for kg in range(NG):
                                ins = t.matmul(bank[:, :], W[:, kg, gidx * 128:(gidx + 1) * 128],
                                               xnc[:, kg, :], start=(kg == 0), stop=(kg == NG - 1))
                        else:
                            half, tt4 = gidx % 2, gidx // 2
                            for kg in range(NG):
                                ins = t.matmul(bank[:, :], xnc[:, kg, tt4 * 128:(tt4 + 1) * 128],
                                               W[:, kg, half * 512:(half + 1) * 512],
                                               start=(kg == 0), stop=(kg == NG - 1))
                        p.inc("Lpmm", ins, 1)
                        k += 1
                        if kind in ("q", "k"):
                            t.wait_ge(p.s("Lpev")[0], base_ev + k)
                            t.wait_ge(p.s("Lsm")[0], base_sm + k - 1)
                            ins = t.matmul(psw[:, :], SWP[:, :], qraw[:, (k - 1) % 2, :],
                                           start=True, stop=True)
                            p.inc("Lsw", ins, 1)

            @block.scalar
            def _(s, tc=tc):
                base_mm = p.val("Lpmm") - 24
                base_rp = p.val("Lrp")
                if b > 0 and tc == 0:
                    p.wait(s, "Lapv")
                k = 0
                for (W, kind) in [(Wq, "q"), (Wk, "k"), (Wv, "v")]:
                    for gidx in range(NG):
                        bank = pj[k % 3]
                        s.wait_ge(p.s("Lpmm")[0], base_mm + k + 1)
                        if kind in ("q", "k"):
                            kqk = k if k < 16 else None
                            if k >= 2 and kqk is not None:
                                s.wait_ge(p.s("Lrp")[0], base_rp + (k - 2) + 1)
                            ins = s.copy(qraw[:, k % 2, :], bank[:, :])
                        else:
                            half, tt4 = gidx % 2, gidx // 2
                            vt = tc * 4 + tt4
                            ins = s.copy(Vb[:, vt, half * 512:(half + 1) * 512], bank[:, :])
                        p.inc("Lpev", ins, 1)
                        k += 1

            @block.vector
            def _(v, tc=tc):
                base_sw = p.val("Lsw") - 16
                base_ev = p.val("Lpev") - 24
                sl = slice(tc * 512, (tc + 1) * 512)
                k = 0
                for (dst, kind) in [(Qb, "q"), (Kb, "k")]:
                    for gidx in range(NG):
                        v.wait_ge(p.s("Lpev")[0], base_ev + k + 1)
                        v.wait_ge(p.s("Lsw")[0], base_sw + k + 1)
                        p.wait(v, "Lrp")
                        p.inc("Lsm", v.tensor_mul(rtmp[:, :], psw[:, :], S16[:, sl]), 1)
                        p.wait(v, "Lsm")
                        p.inc("Lvch", v.tensor_tensor(qraw[:, k % 2, :], qraw[:, k % 2, :],
                                                      C16[:, sl], ALU.mult))
                        p.wait(v, "Lvch")
                        p.inc("Lrp", v.tensor_tensor(dst[:, gidx, sl], qraw[:, k % 2, :],
                                                     rtmp[:, :], ALU.add), 1)
                        k += 1

    with ExitStack() as es:
        def sb(nm, shp, dt):
            return es.enter_context(nc.sbuf_tensor(uniq(nm), shp, dt))
        Cb = sb("L_ctx", [128, NG, T], BF16)
        PTg = sb("L_ptg", [128, 2, 256], BF16)
        KLS = sb("L_kls", [128, T], BF16)
        KLT = sb("L_klt", [128, 2, 15, 64], BF16)
        kv16 = sb("L_kv16", [128, 64], BF16)
        mx = sb("L_mx", [128, 1], F32)
        asum = sb("L_asum", [128, 1], F32)
        rr32 = sb("L_rr32", [128, 256], F32)
        from contextlib import ExitStack as _ES
        es2 = _ES()
        sA2 = [es2.enter_context(nc.psum_tensor(uniq(f"L_sA{i}"), [128, 256], F32)) for i in range(2)]
        rsP = es2.enter_context(nc.psum_tensor(uniq("L_rsP"), [128, 256], F32))
        ctxP = es2.enter_context(nc.psum_tensor(uniq("L_ctxP"), [128, 256], F32))
        ptr = [es2.enter_context(nc.psum_tensor(uniq(f"L_ptr{i}"), [128, 64], F32))
               for i in range(2)]
        kvP = es2.enter_context(nc.psum_tensor(uniq("L_kvP"), [128, 64], F32))
        loP = es2.enter_context(nc.psum_tensor(uniq("L_loP"), [128, 512], F32))

        jcnt = [128, GTOK - 128]
        lpending = [None]
        for h in range(NH):
            hp, hr = h // 2, (h % 2) * 64
            qs = Qb[hr:hr + 64, hp, :]
            ks = Kb[hr:hr + 64, hp, :]
            base_ex = p.val("Laex")
            base_pv = p.val("Lapv")
            base_mm2 = p.val("Lamm")

            @block.tensor
            def _(t, h=h, hr=hr, qs=qs, ks=ks):
                if h == 0:
                    p.wait(t, "Lpev")
                    p.wait(t, "Lrp")
                p.wait(t, "Lnrm")
                for jt in range(2):
                    t.wait_ge(p.s("Laex")[0], base_ex + max(0, jt - 1))
                    ins = t.matmul(sA2[jt][0:jcnt[jt], 0:GTOK], ks[:, jt * 128:jt * 128 + jcnt[jt]],
                                   qs[:, 0:GTOK], start=True, stop=True, tile_position=(hr, 0))
                    p.inc("Lamm", ins, 1)
                for jt in range(2):
                    t.wait_ge(p.s("Laex")[0], base_ex + jt + 1)
                    t.matmul(rsP[:, 0:GTOK], onesmat[0:jcnt[jt], :], PTg[0:jcnt[jt], jt, 0:GTOK],
                             start=(jt == 0), stop=(jt == 1))
                for jt in range(2):
                    ins = t.matmul(ctxP[hr:hr + 64, 0:GTOK],
                                   Vb[0:jcnt[jt], jt, h * 64:(h + 1) * 64],
                                   PTg[0:jcnt[jt], jt, 0:GTOK],
                                   start=(jt == 0), stop=(jt == 1), tile_position=(0, hr))
                    p.inc("Lapv", ins, 1)

            @block.scalar
            def _(s, h=h, base_mm2=base_mm2, base_pv=base_pv):
                for jt in range(2):
                    s.wait_ge(p.s("Lamm")[0], base_mm2 + jt + 1)
                    if h > 0:
                        s.wait_ge(p.s("Lapv")[0], base_pv)
                    ins = s.activation(PTg[0:jcnt[jt], jt, 0:GTOK], sA2[jt][0:jcnt[jt], 0:GTOK],
                                       AF.Exp, scale=0.125)
                    p.inc("Laex", ins, 1)

            @block.vector
            def _(v, h=h, hp=hp, hr=hr, base_pv=base_pv):
                v.wait_ge(p.s("Lapv")[0], base_pv + 2)
                p.inc("Lrc", v.reciprocal(rr32[:, 0:GTOK], rsP[:, 0:GTOK]), 1)
                if b > 0 and h == 0:
                    p.wait(v, "Lomm")
                p.wait(v, "Lrc")
                p.inc("Lnrm", v.tensor_mul(Cb[hr:hr + 64, hp, 0:GTOK],
                                           ctxP[hr:hr + 64, 0:GTOK], rr32[hr:hr + 64, 0:GTOK]), 1)

            lfn = lpending[0]
            if lfn is not None:
                lfn()
            lpending[0] = None

            if h % 2 == 0:
                @block.vector
                def _(v, h=h, hp=hp):
                    p.wait(v, "Lkt", p.val("Lkt"))
                    p.inc("Lvch", v.memset(KLS[:, 128:GTOK], 0.0))
                    p.wait(v, "Lvch")
                    p.inc("Lmx", v.tensor_reduce(mx[:, :], Kb[:, hp, GTOK:T],
                                                 AX.X, ALU.max, negate=True), 1)

                @block.scalar
                def _(s, h=h, hp=hp):
                    p.wait(s, "Lmx")
                    ins = s.activation(KLS[:, GTOK:T], Kb[:, hp, GTOK:T],
                                       AF.Exp, bias=mx[:, :], accum_out=asum[:, :])
                    p.inc("Lke", ins, 1)

                @block.vector
                def _(v, h=h):
                    p.wait(v, "Lke")
                    p.inc("Lvch", v.reciprocal(asum[:, :], asum[:, :]))
                    p.wait(v, "Lvch")
                    p.inc("Lkn", v.tensor_scalar(KLS[:, GTOK:T], KLS[:, GTOK:T],
                                                 asum[:, :], None, ALU.mult), 1)

            kn_target = p.val("Lkn")

            def local_mm(h=h, hr=hr, hp=hp, qs=qs, kn_target=kn_target):
                @block.tensor
                def _(t):
                    t.wait_ge(p.s("Lkn")[0], kn_target)
                    base_kt = p.val("Lkte")
                    for tt in range(1, 16):
                        t.wait_ge(p.s("Lkte")[0], base_kt + tt - 2 if tt >= 3 else base_kt)
                        ins = t.matmul(ptr[tt % 2][:, :],
                                       KLS[hr:hr + 64, tt * 128:(tt + 1) * 128],
                                       ID16[hr:hr + 64, 64:128] if hr == 64 else ID16[0:64, 0:64],
                                       start=True, stop=True, tile_position=(hr, 0))
                        p.inc("Lkt", ins, 1)

                @block.vector
                def _(v):
                    base_kt = p.val("Lkt") - 15
                    for tt in range(1, 16):
                        v.wait_ge(p.s("Lkt")[0], base_kt + tt)
                        p.inc("Lkte", v.tensor_copy(KLT[:, h % 2, tt - 1, :], ptr[tt % 2][:, 0:64]), 1)

                @block.tensor
                def _(t):
                    p.wait(t, "Lkte")
                    for tt in range(1, 16):
                        ins = t.matmul(kvP[hr:hr + 64, :], Vb[:, tt, h * 64:(h + 1) * 64],
                                       KLT[:, h % 2, tt - 1, :], start=(tt == 1), stop=(tt == 15),
                                       tile_position=(0, hr))
                    p.inc("Lkv", ins, 1)

                @block.vector
                def _(v):
                    p.wait(v, "Lkv")
                    p.inc("Lkve", v.tensor_copy(kv16[hr:hr + 64, :], kvP[hr:hr + 64, :]), 1)

                @block.tensor
                def _(t):
                    base_loe = p.val("Lloe")
                    p.wait(t, "Lkve")
                    lo_chunks = [(GTOK, 512), (512, 1024), (1024, 1536), (1536, 2048)]
                    for i_, (c0, c1) in enumerate(lo_chunks):
                        t.wait_ge(p.s("Lloe")[0], base_loe + i_)
                        ins = t.matmul(loP[hr:hr + 64, 0:c1 - c0], kv16[hr:hr + 64, :],
                                       qs[:, c0:c1], start=True, stop=True,
                                       tile_position=(hr, hr))
                        p.inc("Llo", ins, 1)

                @block.vector
                def _(v):
                    base_lo = p.val("Llo") - 4
                    lo_chunks = [(GTOK, 512), (512, 1024), (1024, 1536), (1536, 2048)]
                    for i_, (c0, c1) in enumerate(lo_chunks):
                        v.wait_ge(p.s("Llo")[0], base_lo + i_ + 1)
                        p.inc("Lloe", v.tensor_copy(Cb[hr:hr + 64, hp, c0:c1],
                                                    loP[hr:hr + 64, 0:c1 - c0]), 1)

            lpending[0] = local_mm

        lfn = lpending[0]
        if lfn is not None:
            lfn()
        lpending[0] = None

        es2.close()
        xc = sb("L_xc", [128, NG, 512], F32)
        ot = sb("L_ot", [128, NG, 512], F32)
        po0 = es.enter_context(nc.psum_tensor(uniq("L_po0"), [128, 512], F32))
        po1 = es.enter_context(nc.psum_tensor(uniq("L_po1"), [128, 512], F32))
        po = [po0, po1]
        for tc in range(4):
            @block.sync
            def _(s, tc=tc):
                if tc > 0:
                    p.wait(s, "Loev")
                else:
                    p.wait(s, "Lnrm")
                p.inc("Loxc", s.dma_start(
                    out=xc[:, :, :],
                    in_=x_in[b, :, tc * 512:(tc + 1) * 512].rearrange("(dg p) t -> p dg t", p=128)), 16)

            @block.tensor
            def _(t, tc=tc):
                base_oev = p.val("Loev")
                p.wait(t, "Lnrm")
                p.wait(t, "Lloe")
                p.wait(t, "Lgw")
                for mg in range(NG):
                    t.wait_ge(p.s("Loev")[0], base_oev if mg < 2 else base_oev + mg - 1)
                    for kg in range(NG):
                        ins = t.matmul(po[mg % 2][:, :], Wo[:, kg, mg * 128:(mg + 1) * 128],
                                       Cb[:, kg, tc * 512:(tc + 1) * 512],
                                       start=(kg == 0), stop=(kg == NG - 1))
                    p.inc("Lomm", ins, 1)

            @block.vector
            def _(v, tc=tc):
                base_omm = p.val("Lomm") - NG
                p.wait(v, "Loxc")
                if tc > 0:
                    p.wait(v, "Lost")
                for mg in range(NG):
                    v.wait_ge(p.s("Lomm")[0], base_omm + mg + 1)
                    p.inc("Loev", v.tensor_add(ot[:, mg, :], po[mg % 2][:, :], xc[:, mg, :]), 1)

            @block.scalar
            def _(s, tc=tc):
                p.wait(s, "Loev")
                p.inc("Lost", s.dma_start(
                    out=x_out[b, :, tc * 512:(tc + 1) * 512].rearrange("(dg p) t -> p dg t", p=128),
                    in_=ot[:, :, :]), 16)


def emit_l1ffn(p, x_in, x_out, nb, h1s, wqs1, wqs2, out_td=None):
    nc = p.nc
    cs = p.io["consts"]
    from contextlib import ExitStack
    with ExitStack() as es:
        def sb(nm, shp, dt):
            return es.enter_context(nc.sbuf_tensor(uniq(nm), shp, dt))
        ones16 = sb("M_ones", [128, 1], BF16)
        onesrow16 = sb("M_onesr", [1, 128], BF16)
        onescol32 = sb("M_onesc32", [128, 1], F32)
        onesrow32 = sb("M_onesr32", [1, 128], F32)
        onesmat = sb("M_onesm", [128, 128], BF16)
        ident16 = sb("M_id16", [128, 128], BF16)
        ident32 = sb("M_id32", [128, 128], F32)
        cw = [sb(f"M_cw{k}", [128, 32], F32) for k in range(3)]
        cb = sb("M_cb", [128, 32], F32)
        srep1 = sb("M_sr1", [128, 2], F32)
        srep2 = sb("M_sr2", [128, 2], F32)
        block = es.enter_context(nc.Block())

        @block.sync
        def _(s):
            for nm, dst in [("ONESCOL16", ones16), ("ONESROW16", onesrow16),
                            ("ONESCOL32", onescol32), ("ONESROW32", onesrow32),
                            ("ONESMAT16", onesmat), ("IDENT16", ident16), ("IDENT32", ident32),
                            ("ROPE_C", None)][:7]:
                p.inc("Mcld", s.dma_start(out=dst[:, :], in_=cs[nm]), 16)
            for k in range(3):
                p.inc("Mcld", s.dma_start(out=cw[k][:, :], in_=p.io[f"convw{k}_fm"]), 16)
            p.inc("Mcld", s.dma_start(out=cb[:, :], in_=p.io["convb_fm"]), 16)

        with ExitStack() as esw:
            stage = esw.enter_context(nc.sbuf_tensor(uniq("M_stage"), [128, 2, 4096], F32))
            qst = esw.enter_context(nc.sbuf_tensor(uniq("M_qst"), [128, 2, 4096], BF16))
            ystage = esw.enter_context(nc.sbuf_tensor(uniq("M_ystage"), [128, 4096], F32))
            abuf = esw.enter_context(nc.sbuf_tensor(uniq("M_abuf"), [128, 4096], F32))
            rsum = esw.enter_context(nc.sbuf_tensor(uniq("M_rsum"), [128, 3], F32))
            sc1 = esw.enter_context(nc.sbuf_tensor(uniq("M_sc1"), [1, 4], F32))
            pm = esw.enter_context(nc.psum_tensor(uniq("M_pm"), [128, 2], F32))

            def w_src(wi, kg):
                if wi == 0:
                    return p.io["L1_ffn_w1T"][kg * 128:(kg + 1) * 128, :]
                return p.io["L1_ffn_w2T"][kg * 512:(kg + 1) * 512, :].rearrange(
                    "(c p) w -> p c w", p=128)

            def w_dst(wi, kg, wq):
                if wi == 0:
                    return wq[kg * 128:(kg + 1) * 128, :]
                return wq[kg * 512:(kg + 1) * 512, :].rearrange("(c p) w -> p c w", p=128)

            for wi, (wq, srep, nelem) in enumerate([
                    (wqs1, srep1, 4096 * 1024), (wqs2, srep2, 4096 * 1024)]):
                for kg in range(NG):
                    @block.sync
                    def _(s, wi=wi, kg=kg):
                        if kg > 1 or wi > 0:
                            p.wait(s, "Mab", max(0, p.val("Mab") - 1))
                            p.wait(s, "Myw", p.val("Myw"))
                        p.inc("Mwl" + ("b" if kg % 2 else ""),
                              s.dma_start(out=stage[:, kg % 2, :], in_=w_src(wi, kg)), 16)

                    @block.vector
                    def _(v, wi=wi, kg=kg):
                        p.wait(v, "Mwl" + ("b" if kg % 2 else ""))
                        if kg == 0:
                            p.inc("Mvch", v.memset(rsum[:, wi:wi + 1], 0.0))
                            p.wait(v, "Mvch")
                        p.wait(v, "Mab", p.val("Mab"))
                        p.inc("Mvch", v.tensor_reduce(rsum[:, 2:3], stage[:, kg % 2, :],
                                                      AX.X, ALU.add, apply_absolute_value=True))
                        p.wait(v, "Mvch")
                        p.inc("Mab", v.tensor_add(rsum[:, wi:wi + 1], rsum[:, wi:wi + 1],
                                                  rsum[:, 2:3]), 1)

                @block.tensor
                def _(t, wi=wi):
                    p.wait(t, "Mab")
                    p.wait(t, "Mcld")
                    p.inc("Mtm", t.matmul(pm[0:1, wi:wi + 1], onescol32[:, :], rsum[:, wi:wi + 1],
                                          start=True, stop=True), 1)

                @block.vector
                def _(v, wi=wi, nelem=nelem):
                    p.wait(v, "Mtm")
                    p.inc("Mvch", v.tensor_scalar(sc1[0:1, 0:1], pm[0:1, wi:wi + 1],
                                                  1.0 / nelem, None, ALU.mult))
                    p.wait(v, "Mvch")
                    p.inc("Mvch", v.tensor_scalar(sc1[0:1, 0:1], sc1[0:1, 0:1], 1e-5, None, ALU.max))
                    p.wait(v, "Mvch")
                    p.inc("Mvch", v.reciprocal(sc1[0:1, 1:2], sc1[0:1, 0:1]))
                    p.wait(v, "Mvch")
                    p.inc("Msc", v.tensor_copy(sc1[0:1, 2:4], sc1[0:1, 0:2]), 1)

                @block.tensor
                def _(t, wi=wi):
                    p.wait(t, "Msc")
                    p.inc("Mbc1", t.matmul(pm[:, 0:2], onesrow32[:, :], sc1[0:1, 0:2],
                                           start=True, stop=True), 1)

                @block.vector
                def _(v, wi=wi, srep=srep):
                    p.wait(v, "Mbc1")
                    p.inc("Mvch", v.tensor_copy(srep[:, 0:1], pm[:, 1:2]))
                    p.wait(v, "Mvch")
                    p.inc("Msr", v.tensor_copy(srep[:, 1:2], pm[:, 0:1]), 1)

                for kg in range(NG):
                    @block.sync
                    def _(s, wi=wi, kg=kg):
                        p.wait(s, "Myw", max(0, p.val("Myw") - 1))
                        if kg < 2:
                            p.wait(s, "Mab")
                        p.inc("Mwl2" + ("b" if kg % 2 else ""),
                              s.dma_start(out=stage[:, kg % 2, :], in_=w_src(wi, kg)), 16)

                    @block.vector
                    def _(v, wi=wi, kg=kg, srep=srep):
                        p.wait(v, "Mwl2" + ("b" if kg % 2 else ""))
                        if kg == 0:
                            p.wait(v, "Msr")
                        p.wait(v, "Mqz", p.val("Mqz"))
                        p.inc("Myw", v.tensor_scalar(ystage[:, :], stage[:, kg % 2, :],
                                                     srep[:, 0:1], None, ALU.mult), 1)
                        p.wait(v, "Myw")
                        p.inc("Mvch", v.tensor_scalar(abuf[:, :], ystage[:, :], 0.5, None, ALU.is_ge))
                        p.wait(v, "Mvch")
                        p.inc("Mvch", v.tensor_scalar(ystage[:, :], ystage[:, :], -0.5, None,
                                                      ALU.is_le))
                        p.wait(v, "Mvch")
                        p.wait(v, "Mqs" + ("b" if kg % 2 else ""))
                        p.inc("Mqz", v.tensor_sub(qst[:, kg % 2, :], abuf[:, :], ystage[:, :]), 1)

                    @block.scalar
                    def _(s, wi=wi, kg=kg, wq=wq):
                        p.wait(s, "Mqz")
                        p.inc("Mqs" + ("b" if kg % 2 else ""),
                              s.dma_start(out=w_dst(wi, kg, wq), in_=qst[:, kg % 2, :]), 16)

        for b in range(nb):
            _mark(p, f"l1f_b{b}")
            _l1ffn_batch(p, block, b, x_in, x_out, h1s, wqs1, wqs2, ones16, onesrow16,
                         onesmat, ident16, ident32, cw, cb, srep1, srep2, out_td)

        @block.sync
        def _(s):
            p.wait(s, "Mst")


def _l1ffn_batch(p, block, b, x_in, x_out, h1s, wqs1, wqs2, ones16, onesrow16,
                 onesmat, ident16, ident32, cw, cb, srep1, srep2, out_td=None):
    nc = p.nc
    from contextlib import ExitStack
    _mark(p, f"l1f_b{b}_p1")
    with ExitStack() as es:
        def sb(nm, shp, dt):
            return es.enter_context(nc.sbuf_tensor(uniq(nm), shp, dt))
        w1q = sb("M_w1q", [128, NG, 4096], BF16)
        xb1 = sb("M_xb", [128, NG, 512], BF16)
        xn1 = sb("M_xn", [128, NG, 512], BF16)
        yb = sb("M_y", [128, 2, NG, 512], BF16)
        ab = sb("M_a", [128, NG, 512], BF16)
        accb = sb("M_acc", [128, NG, 512], BF16)
        sgb = sb("M_sg", [128, NG, 512], BF16)
        mb = sb("M_mb", [128, NG, 512], BF16)
        Sbc = sb("M_sbc", [128, 512], BF16)
        Rbc = sb("M_rbc", [128, 512], BF16)
        mt = sb("M_mt", [128, 4], F32)
        st2 = sb("M_st2", [128, 4], F32)
        Dt = sb("M_dt", [128, 128], BF16)
        h1sb = sb("M_h1sb", [128, NG, 512], BF16)
        sc = alloc_ln_scratch(p, es, "M_ln")
        sc["ones16"] = ones16
        sc["onesrow16"] = onesrow16
        lnsum = es.enter_context(nc.psum_tensor(uniq("M_lnsum"), [1, 512], F32))
        lnsq = es.enter_context(nc.psum_tensor(uniq("M_lnsq"), [1, 512], F32))
        psA = es.enter_context(nc.psum_tensor(uniq("M_psA"), [128, 512], F32))
        psB = es.enter_context(nc.psum_tensor(uniq("M_psB"), [128, 512], F32))
        pq1 = es.enter_context(nc.psum_tensor(uniq("M_pq1"), [128, 512], F32))
        pw = [es.enter_context(nc.psum_tensor(uniq(f"M_pw{i}"), [128, 512], F32)) for i in range(3)]
        pb = {"lnsum": lnsum, "lnsq": lnsq, "psA": psA, "psB": psB}

        @block.gpsimd
        def _(g):
            if b == 0:
                p.wait(g, "Mqs")
                p.wait(g, "Mqsb")
            else:
                p.wait(g, "Mst")
            p.inc("Mwq1", g.dma_start(
                out=w1q[:, :, :], in_=wqs1.rearrange("(kg p) h -> p kg h", p=128)), 16)

        for tc in range(4):
            sl = slice(tc * 512, (tc + 1) * 512)

            @block.gpsimd
            def _(g, tc=tc):
                if b > 0 and tc == 0:
                    p.wait(g, "Mst")
                if b > 0 or tc >= 1:
                    p.wait(g, "cxn")
                p.inc("Mgxb", g.dma_start(
                    out=xb1[:, :, :],
                    in_=x_in[b, :, tc * 512:(tc + 1) * 512].rearrange("(dg p) t -> p dg t", p=128)), 16)

            emit_ln_chunk(p, block, "M", xb1, xn1, sc, pb, xn_guard=("Mw1", 0))

            @block.scalar
            def _(s, tc=tc):
                p.wait(s, "Mxn")
                p.wait(s, "Mxq", p.val("Mxq"))
                p.inc("Msg2", s.sign(sgb[:, :, :], xn1[:, :, :]), 1)

            @block.vector
            def _(v, tc=tc):
                p.wait(v, "Mxn")
                p.wait(v, "Mtr", p.val("Mtr"))
                p.inc("Mgm", v.tensor_reduce(ab[:, 0, :], xn1[:, :, :].transpose([0, 2, 1]),
                                             AX.X, ALU.max, apply_absolute_value=True), 1)

            @block.tensor
            def _(t, tc=tc):
                base_mt = p.val("Mmt")
                p.wait(t, "Mgm")
                p.wait(t, "Mbce2", p.val("Mbce2"))
                for i in range(4):
                    if i > 0:
                        t.wait_ge(p.s("Mmt")[0], base_mt + i)
                    ins = t.matmul(pq1[:, i * 128:(i + 1) * 128], ab[:, 0, i * 128:(i + 1) * 128],
                                   ident16[:, :], start=True, stop=True)
                    p.inc("Mtr", ins, 1)

            @block.vector
            def _(v, tc=tc):
                base_tr = p.val("Mtr") - 4
                for i in range(4):
                    v.wait_ge(p.s("Mtr")[0], base_tr + i + 1)
                    p.inc("Mmt", v.tensor_reduce(mt[:, i:i + 1], pq1[:, i * 128:(i + 1) * 128],
                                                  AX.X, ALU.max), 1)
                p.wait(v, "Mmt")
                p.inc("Mvch", v.tensor_scalar(mt[:, :], mt[:, :], 1e-5, None, ALU.max))
                p.wait(v, "Mvch")
                p.inc("Mvch", v.reciprocal(st2[:, :], mt[:, :]))
                p.wait(v, "Mvch")
                p.inc("Mvch", v.tensor_scalar(st2[:, :], st2[:, :], 127.0, None, ALU.mult))
                p.wait(v, "Mvch")
                p.inc("Msq2", v.tensor_scalar(mt[:, :], mt[:, :], 1.0 / 127.0, None, ALU.mult), 1)

            @block.vector
            def _(v, tc=tc):
                p.wait(v, "Msq2")
                p.wait(v, "Mbcm2", p.val("Mbcm2"))
                for i in range(4):
                    p.inc("MD", v.tensor_scalar(mb[:, 0, i * 128:(i + 1) * 128], ident16[:, :],
                                                st2[:, i:i + 1], None, ALU.mult), 1)

            @block.tensor
            def _(t, tc=tc):
                p.wait(t, "MD")
                p.wait(t, "Mmt")
                for i in range(4):
                    ins = t.matmul(pq1[:, i * 128:(i + 1) * 128], onesmat[:, :],
                                   mb[:, 0, i * 128:(i + 1) * 128], start=True, stop=True)
                p.inc("Mbcm2", ins, 1)

            @block.vector
            def _(v, tc=tc):
                p.wait(v, "Mbcm2")
                p.inc("Mbce2", v.tensor_copy(Sbc[:, :], pq1[:, :]), 1)
                for i in range(4):
                    p.inc("MD", v.tensor_scalar(mb[:, 1, i * 128:(i + 1) * 128], ident16[:, :],
                                                mt[:, i:i + 1], None, ALU.mult), 1)

            @block.tensor
            def _(t, tc=tc):
                p.wait(t, "MD")
                p.wait(t, "Mbce2", p.val("Mbce2"))
                for i in range(4):
                    ins = t.matmul(pq1[:, i * 128:(i + 1) * 128], onesmat[:, :],
                                   mb[:, 1, i * 128:(i + 1) * 128], start=True, stop=True)
                p.inc("Mbcm2", ins, 1)

            @block.vector
            def _(v, tc=tc):
                p.wait(v, "Mbcm2")
                p.inc("Mbce2", v.tensor_copy(Rbc[:, :], pq1[:, :]), 1)

            @block.vector
            def _(v, tc=tc):
                p.wait(v, "Mbce2")
                p.wait(v, "Mw1", max(0, p.val("Mw1") - 32))
                p.inc("My", v.tensor_tensor(yb[:, tc % 2, :, :], xn1[:, :, :],
                                            Sbc[:, :].unsqueeze(1).broadcast_to([128, NG, 512]),
                                            ALU.mult), 1)

            @block.scalar
            def _(s, tc=tc):
                p.wait(s, "My")
                p.inc("Mabs2", s.activation(ab[:, :, :], yb[:, tc % 2, :, :], AF.Abs), 1)

            @block.vector
            def _(v, tc=tc):
                p.wait(v, "Mabs2")
                p.wait(v, "Mbcm2", p.val("Mbcm2"))
                p.inc("Mvch", v.tensor_scalar(accb[:, :, :], ab[:, :, :], 63.5, 64.0,
                                              ALU.is_ge, ALU.mult))
                p.wait(v, "Mvch")
                for B in (32.0, 16.0, 8.0, 4.0, 2.0, 1.0):
                    p.inc("Mvch", v.tensor_scalar(mb[:, :, :], accb[:, :, :], B - 0.5,
                                                  None, ALU.add))
                    p.wait(v, "Mvch")
                    p.inc("Mvch", v.tensor_tensor(mb[:, :, :], mb[:, :, :], ab[:, :, :],
                                                  ALU.is_le))
                    p.wait(v, "Mvch")
                    p.inc("Mvch", v.tensor_scalar(mb[:, :, :], mb[:, :, :], B, None, ALU.mult))
                    p.wait(v, "Mvch")
                    p.inc("Mvch", v.tensor_add(accb[:, :, :], accb[:, :, :], mb[:, :, :]))
                    p.wait(v, "Mvch")
                p.inc("Mvch", v.tensor_mul(sgb[:, :, :], accb[:, :, :], sgb[:, :, :]))
                p.wait(v, "Mvch")
                p.inc("Mxq", v.tensor_tensor(yb[:, tc % 2, :, :], sgb[:, :, :],
                                             Rbc[:, :].unsqueeze(1).broadcast_to([128, NG, 512]),
                                             ALU.mult), 1)

            for hgb in range(4):
                @block.tensor
                def _(t, tc=tc, hgb=hgb):
                    base_e = p.val("Mw1e")
                    if tc == 0 and hgb == 0:
                        p.wait(t, "Mwq1")
                    p.wait(t, "Mxq")
                    for hgi in range(NG):
                        hg = hgb * NG + hgi
                        t.wait_ge(p.s("Mw1e")[0], base_e if hgi < 3 else base_e + hgi - 2)
                        for kg in range(NG):
                            ins = t.matmul(pw[hgi % 3][:, :], w1q[:, kg, hg * 128:(hg + 1) * 128],
                                           yb[:, tc % 2, kg, :], start=(kg == 0), stop=(kg == NG - 1))
                        p.inc("Mw1", ins, 1)

                @block.scalar
                def _(s, tc=tc, hgb=hgb):
                    base_mm = p.val("Mw1") - NG
                    p.wait(s, "Mh1s", p.val("Mh1s"))
                    for hgi in range(NG):
                        s.wait_ge(p.s("Mw1")[0], base_mm + hgi + 1)
                        p.inc("Mw1e", s.copy(h1sb[:, hgi, :], pw[hgi % 3][:, :]), 1)

                @block.sync
                def _(s, tc=tc, hgb=hgb):
                    p.wait(s, "Mw1e")
                    p.inc("Mh1s", s.dma_start(
                        out=h1s[b, hgb * 1024:(hgb + 1) * 1024, tc * 512:(tc + 1) * 512]
                            .rearrange("(hgi p) t -> p hgi t", p=128),
                        in_=h1sb[:, :, :]), 16)

    _mark(p, f"l1f_b{b}_p2")
    with ExitStack() as es:
        def sb(nm, shp, dt):
            return es.enter_context(nc.sbuf_tensor(uniq(nm), shp, dt))
        w2q = sb("M_w2q", [128, 32, D], BF16)
        hc = sb("M_hc", [128, 32, 258], BF16)
        h2 = sb("M_h2", [128, 32, 256], BF16)
        y2 = sb("M_y2", [128, 32, 256], BF16)
        sg2 = sb("M_sg2", [128, 2, 32, 256], BF16)
        macc = sb("M_macc", [128, 32, 256], BF16)
        mbf = sb("M_mbf", [128, 32, 256], BF16)
        dgr = sb("M_dgr", [128, 4, 6, 128], BF16)
        gmb = sb("M_gmb", [128, 256], BF16)
        S2 = sb("M_s2", [128, 256], BF16)
        R2 = sb("M_r2", [128, 2, 256], BF16)
        mt2 = sb("M_mt2", [128, 2], F32)
        st22 = sb("M_st22", [128, 2], F32)
        xc2 = sb("M_xc2", [128, 4, 256], F32)
        oo = sb("M_oo", [128, 4, 256], F32)
        pq2 = es.enter_context(nc.psum_tensor(uniq("M_pq2"), [128, 512], F32))
        cpb = es.enter_context(nc.psum_tensor(uniq("M_cpb"), [128, 512], F32))
        po = [es.enter_context(nc.psum_tensor(uniq(f"M_po{i}"), [128, 512], F32)) for i in range(4)]
        ptb = [es.enter_context(nc.psum_tensor(uniq(f"M_ptb{i}"), [128, 128], F32))
               for i in range(2)] if out_td is not None else None

        @block.gpsimd
        def _(g):
            p.wait(g, "Mh1s")
            p.wait(g, "pev")
            p.wait(g, "Mxq")
            p.inc("Mwq2", g.dma_start(
                out=w2q[:, :, :], in_=wqs2.rearrange("(hg p) d -> p hg d", p=128)), 16)



        pending = [None]

        def emit_deferred():
            fn = pending[0]
            if fn is not None:
                fn()
            pending[0] = None

        def conv_stage(tcn):
            if tcn >= 8:
                return
            c0n, c1n = tcn * 256, (tcn + 1) * 256

            @block.sync
            def _(s, tcn=tcn, c0n=c0n, c1n=c1n):
                if tcn == 0:
                    p.wait(s, "Mw1e")
                    p.wait(s, "Mh1s")
                    p.wait(s, "Mxq")
                if tcn > 0:
                    p.wait(s, "Mcg", p.val("Mcg"))
                lo = max(0, c0n - 1)
                hi = min(T, c1n + 1)
                p.inc("Mh1l", s.dma_start(
                    out=hc[:, :, (1 if tcn == 0 else 0):(257 if tcn == 7 else 258)],
                    in_=h1s[b, :, lo:hi].rearrange("(hg p) t -> p hg t", p=128)), 16)

            if tcn in (0, 7):
                @block.gpsimd
                def _(g, tcn=tcn):
                    p.wait(g, "Mh1l")
                    p.inc("Mgch", g.memset(hc[:, :, 0:1] if tcn == 0 else hc[:, :, 257:258],
                                           0.0), 1)

            @block.gpsimd
            def _(g, tcn=tcn):
                base_cg = p.val("Mcg")
                for r in range(16):
                    g.wait_ge(p.s("Mcg")[0], max(0, base_cg + r - 3))
                    p.inc(f"Mdg{r % 4}", g.dma_start(
                        out=dgr[:, r % 4, :, :],
                        in_=p.io["convdiag"][6 * r:6 * r + 6].rearrange("n p m -> p n m")), 16)

            @block.tensor
            def _(t, tcn=tcn):
                base_sl2 = p.val("Msl2")
                base_dg = {s_: p.val(f"Mdg{s_}") - 64 for s_ in range(4)}
                p.wait(t, "Mh1l")
                if tcn in (0, 7):
                    p.wait(t, "Mgch")
                p.wait(t, "Mbce3", p.val("Mbce3"))
                for r in range(16):
                    bank = cpb if r % 2 == 0 else pq2
                    t.wait_ge(p.s("Msl2")[0], max(0, base_sl2 + 2 * r - 2))
                    t.wait_ge(p.s(f"Mdg{r % 4}")[0], base_dg[r % 4] + 16 * (r // 4 + 1))
                    for h2i in range(2):
                        hg = 2 * r + h2i
                        for tap in range(3):
                            ins = t.matmul(bank[:, h2i * 256:(h2i + 1) * 256],
                                           dgr[:, r % 4, h2i * 3 + tap, :],
                                           hc[:, hg, tap:tap + 256],
                                           start=(tap == 0), stop=(tap == 2))
                    p.inc("Mcg", ins, 1)

        conv_stage(0)
        for tc in range(8):
            c0, c1 = tc * 256, (tc + 1) * 256

            @block.scalar
            def _(s, tc=tc):
                base_cg = p.val("Mcg") - 16
                p.wait(s, "qy", p.val("qy"))
                p.wait(s, "Msg3", p.val("Msg3"))
                for r in range(16):
                    bank = cpb if r % 2 == 0 else pq2
                    s.wait_ge(p.s("Mcg")[0], base_cg + r + 1)
                    for h2i in range(2):
                        hg = 2 * r + h2i
                        p.inc("Msl2", s.activation(h2[:, hg, :],
                                                   bank[:, h2i * 256:(h2i + 1) * 256],
                                                   AF.Silu, bias=cb[:, hg:hg + 1],
                                                   scale=srep1[:, 1:2]), 1)
                p.wait(s, "Msl2")
                p.wait(s, "Mq2", p.val("Mq2"))
                p.inc("Mabh", s.activation(macc[:, :, :], h2[:, :, :], AF.Abs), 1)
                p.wait(s, "Mw2", max(0, p.val("Mw2") - 8))
                p.inc("Msg3", s.sign(sg2[:, tc % 2, :, :], h2[:, :, :]), 1)

            @block.vector
            def _(v, tc=tc):
                p.wait(v, "Mabh")
                p.wait(v, "Mtr2", p.val("Mtr2"))
                p.wait(v, "Mbcm3", p.val("Mbcm3"))
                p.inc("Mvch", v.tensor_tensor(mbf[:, 0:16, :], macc[:, 0:16, :],
                                              macc[:, 16:32, :], ALU.max))
                p.wait(v, "Mvch")
                for w in (8, 4, 2):
                    p.inc("Mvch", v.tensor_tensor(mbf[:, 0:w, :], mbf[:, 0:w, :],
                                                  mbf[:, w:2 * w, :], ALU.max))
                    p.wait(v, "Mvch")
                p.inc("Mgm2", v.tensor_tensor(gmb[:, :], mbf[:, 0, :], mbf[:, 1, :],
                                              ALU.max), 1)

            @block.tensor
            def _(t, tc=tc):
                base_mt = p.val("Mmt2")
                p.wait(t, "Mgm2")
                p.wait(t, "Mbce3", p.val("Mbce3"))
                for i in range(2):
                    if i > 0:
                        t.wait_ge(p.s("Mmt2")[0], base_mt + i)
                    ins = t.matmul(pq2[:, i * 128:(i + 1) * 128], gmb[:, i * 128:(i + 1) * 128],
                                   ident16[:, :], start=True, stop=True)
                    p.inc("Mtr2", ins, 1)

            @block.vector
            def _(v, tc=tc):
                base_tr = p.val("Mtr2") - 2
                for i in range(2):
                    v.wait_ge(p.s("Mtr2")[0], base_tr + i + 1)
                    p.inc("Mmt2", v.tensor_reduce(mt2[:, i:i + 1], pq2[:, i * 128:(i + 1) * 128],
                                                  AX.X, ALU.max), 1)
                p.wait(v, "Mmt2")
                p.inc("Mvch", v.tensor_scalar(mt2[:, :], mt2[:, :], 1e-5, None, ALU.max))
                p.wait(v, "Mvch")
                p.inc("Mvch", v.reciprocal(st22[:, :], mt2[:, :]))
                p.wait(v, "Mvch")
                p.inc("Mvch", v.tensor_scalar(st22[:, :], st22[:, :], 127.0, None, ALU.mult))
                p.wait(v, "Mvch")
                p.inc("Mvch", v.tensor_scalar(mt2[:, :], mt2[:, :], srep2[:, 1:2], 1.0 / 127.0,
                                              ALU.mult, ALU.mult))
                p.wait(v, "Mvch")
                p.inc("Msq3", v.tensor_copy(st22[:, :], st22[:, :]), 1)

            @block.vector
            def _(v, tc=tc):
                p.wait(v, "Msq3")
                p.wait(v, "Mbcm3", p.val("Mbcm3"))
                for j, (vecsrc, i) in enumerate([(st22, 0), (st22, 1), (mt2, 0), (mt2, 1)]):
                    p.inc("MD2", v.tensor_scalar(
                        mbf[:, j // 2, (j % 2) * 128:(j % 2 + 1) * 128], ident16[:, :],
                        vecsrc[:, i:i + 1], None, ALU.mult), 1)

            @block.tensor
            def _(t, tc=tc):
                p.wait(t, "MD2")
                p.wait(t, "Mmt2")
                for j in range(4):
                    ins = t.matmul(pq2[:, j * 128:(j + 1) * 128], onesmat[:, :],
                                   mbf[:, j // 2, (j % 2) * 128:(j % 2 + 1) * 128],
                                   start=True, stop=True)
                p.inc("Mbcm3", ins, 1)

            @block.vector
            def _(v, tc=tc):
                p.wait(v, "Mbcm3")
                p.inc("Mbce3", v.tensor_copy(S2[:, :], pq2[:, 0:256]))
                p.wait(v, "Mbce3")
                p.inc("Mbce3", v.tensor_copy(R2[:, tc % 2, :], pq2[:, 256:512]), 1)

            emit_deferred()
            conv_stage(tc + 1)

            @block.vector
            def _(v, tc=tc):
                p.wait(v, "Mbce3")
                p.inc("My2", v.tensor_tensor(y2[:, :, :], macc[:, :, :],
                                             S2[:, :].unsqueeze(1).broadcast_to([128, 32, 256]),
                                             ALU.mult), 1)

            def ladder(eng, chain, lo, hi, tc=tc):
                p.wait(eng, "My2")
                p.wait(eng, "Msg3", p.val("Msg3"))
                p.inc(chain, eng.tensor_scalar(macc[:, lo:hi, :], y2[:, lo:hi, :], 63.5, 64.0,
                                               ALU.is_ge, ALU.mult))
                p.wait(eng, chain)
                for B in (32.0, 16.0, 8.0, 4.0, 2.0, 1.0):
                    p.inc(chain, eng.tensor_scalar(mbf[:, lo:hi, :], macc[:, lo:hi, :], B - 0.5,
                                                   None, ALU.add))
                    p.wait(eng, chain)
                    p.inc(chain, eng.tensor_tensor(mbf[:, lo:hi, :], mbf[:, lo:hi, :],
                                                   y2[:, lo:hi, :], ALU.is_le))
                    p.wait(eng, chain)
                    p.inc(chain, eng.tensor_scalar(mbf[:, lo:hi, :], mbf[:, lo:hi, :], B,
                                                   None, ALU.mult))
                    p.wait(eng, chain)
                    p.inc(chain, eng.tensor_add(macc[:, lo:hi, :], macc[:, lo:hi, :],
                                                mbf[:, lo:hi, :]))
                    p.wait(eng, chain)
                p.inc("Mq2", eng.tensor_mul(sg2[:, tc % 2, lo:hi, :], macc[:, lo:hi, :],
                                            sg2[:, tc % 2, lo:hi, :]), 1)

            @block.vector
            def _(v, tc=tc):
                p.wait(v, "Mbcm3", p.val("Mbcm3"))
                ladder(v, "Mlv", 0, 11, tc)
                ladder(v, "Mlv", 11, 22, tc)
                ladder(v, "Mlv", 22, 32, tc)

            @block.tensor
            def _(t, tc=tc):
                base_ev = p.val("Mov")
                if tc == 0:
                    p.wait(t, "Mwq2")
                p.wait(t, "Mq2")
                t.wait_ge(p.s("Mov")[0], base_ev)
                for dg in range(NG):
                    dst = po[dg % 4][:, (dg // 4) * 256:(dg // 4) * 256 + 256]
                    for hg in range(32):
                        ins = t.matmul(dst, w2q[:, hg, dg * 128:(dg + 1) * 128],
                                       sg2[:, tc % 2, hg, :], start=(hg == 0), stop=(hg == 31))
                    p.inc("Mw2", ins, 1)

            def deferred(tc=tc, c0=c0, c1=c1):
                base_mm = p.val("Mw2") - NG
                for half in range(2):
                    @block.sync
                    def _(s, half=half):
                        p.wait(s, "Mov")
                        if out_td is not None:
                            p.wait(s, "Mst")
                        p.inc("Mxc2", s.dma_start(
                            out=xc2[:, :, :],
                            in_=x_in[b, half * 512:(half + 1) * 512, c0:c1]
                                .rearrange("(dg p) t -> p dg t", p=128)), 16)

                    @block.vector
                    def _(v, half=half):
                        p.wait(v, "Mst")
                        for dgi in range(4):
                            dg = half * 4 + dgi
                            v.wait_ge(p.s("Mw2")[0], base_mm + dg + 1)
                            src_ = po[dg % 4][:, (dg // 4) * 256:(dg // 4) * 256 + 256]
                            p.inc("Mev2", v.tensor_tensor(oo[:, dgi, :], src_,
                                                          R2[:, tc % 2, :], ALU.mult), 1)

                    @block.gpsimd
                    def _(g, half=half):
                        base_ev2 = p.val("Mev2") - 4
                        p.wait(g, "Mxc2")
                        for dgi in range(4):
                            g.wait_ge(p.s("Mev2")[0], base_ev2 + dgi + 1)
                            p.inc("Mov", g.tensor_add(oo[:, dgi, :], oo[:, dgi, :],
                                                      xc2[:, dgi, :]), 1)

                    if out_td is None:
                        @block.scalar
                        def _(s, half=half):
                            p.wait(s, "Mov")
                            p.inc("Mst", s.dma_start(
                                out=x_out[b, half * 512:(half + 1) * 512, c0:c1]
                                    .rearrange("(dg p) t -> p dg t", p=128),
                                in_=oo[:, :, :]), 16)
                    else:
                        @block.tensor
                        def _(t, half=half):
                            base_te = p.val("Mte")
                            p.wait(t, "Mov")
                            k = 0
                            for dgi in range(4):
                                for tt in range(2):
                                    if k >= 2:
                                        t.wait_ge(p.s("Mte")[0], base_te + k - 1)
                                    ins = t.transpose(ptb[k % 2][:, :],
                                                      oo[:, dgi, tt * 128:(tt + 1) * 128],
                                                      ident32[:, :])
                                    p.inc("Mtt", ins, 1)
                                    k += 1

                        @block.vector
                        def _(v, half=half):
                            base_tt = p.val("Mtt") - 8
                            k = 0
                            for dgi in range(4):
                                for tt in range(2):
                                    v.wait_ge(p.s("Mtt")[0], base_tt + k + 1)
                                    p.inc("Mte", v.tensor_copy(
                                        xc2[:, tt * 2 + dgi // 2,
                                            (dgi % 2) * 128:(dgi % 2) * 128 + 128],
                                        ptb[k % 2][:, :]), 1)
                                    k += 1

                        @block.scalar
                        def _(s, half=half):
                            p.wait(s, "Mte")
                            for tt in range(2):
                                p.inc("Mst", s.dma_start(
                                    out=out_td[b, c0 + tt * 128:c0 + (tt + 1) * 128,
                                               half * 512:(half + 1) * 512]
                                        .rearrange("t (a c) -> t a c", a=2),
                                    in_=xc2[:, tt * 2:tt * 2 + 2, :]), 16)

            pending[0] = deferred

        emit_deferred()


def build_program(nb, stages=("trin", "l0a", "l0f", "l1a", "l1f", "trout")):
    from contextlib import ExitStack
    nc = bass.Bass("TRN2", target_bir_lowering=False)
    stack = ExitStack()
    p = P(nc, stack)
    p.io = declare_io(nc, nb)
    x_in = nc.dram_tensor("x", [nb, T, D], F32, kind="ExternalInput").ap()
    out = nc.dram_tensor("out", [nb, T, D], F32, kind="ExternalOutput").ap()
    xr0 = nc.dram_tensor("xr0", [nb, D, T], F32, kind="Internal").ap()
    xr1 = nc.dram_tensor("xr1", [nb, D, T], F32, kind="Internal").ap()
    xr2 = nc.dram_tensor("xr2", [nb, D, T], F32, kind="Internal").ap()
    xr3 = nc.dram_tensor("xr3", [nb, D, T], F32, kind="Internal").ap()
    xr4 = nc.dram_tensor("xr4", [nb, D, T], F32, kind="Internal").ap()
    h1s = nc.dram_tensor("h1s", [nb, 4096, T], BF16, kind="Internal").ap()
    wqs1 = nc.dram_tensor("wqs1", [D, 4096], BF16, kind="Internal").ap()
    wqs2 = nc.dram_tensor("wqs2", [4096, D], BF16, kind="Internal").ap()
    scratch = [xr0, xr1, xr2, xr3, xr4]
    with stack:
        cur = None
        if "trin" in stages:
            emit_tr_td2fm(p, x_in, xr0, nb, "ti")
            cur = 0
        if "l0a" in stages:
            emit_l0attn(p, scratch[cur], scratch[cur + 1], nb)
            cur += 1
        if "l0f" in stages:
            emit_l0ffn(p, scratch[cur], scratch[cur + 1], nb)
            cur += 1
        if "l1a" in stages:
            emit_l1attn(p, scratch[cur], scratch[cur + 1], nb)
            cur += 1
        if "l1f" in stages:
            emit_l1ffn(p, scratch[cur], scratch[cur + 1], nb, h1s, wqs1, wqs2,
                       out_td=(out if "trout" in stages else None))
            cur += 1
        elif "trout" in stages:
            emit_tr_fm2td(p, scratch[cur], out, nb, "to")
            cur = None
        if "l1f" in stages and "trout" not in stages:
            pass
    return nc


def kernel(**inputs):
    nb = 2
    maps = shard_inputs(inputs, nb)
    nc = build_program(nb)
    res = run_bass_kernel_spmd(nc, maps, core_ids=list(range(NCORES)))
    return np.concatenate([r["out"] for r in res.results], axis=0).astype(np.float32)
